# version 01
# baseline (speedup 1.0000x reference)
import os
import numpy as np
import ml_dtypes
import concourse.bass as bass
import concourse.mybir as mybir
from concourse.bass_utils import run_bass_kernel_spmd

F32 = mybir.dt.float32
BF16 = mybir.dt.bfloat16
AF = mybir.ActivationFunctionType
ALU = mybir.AluOpType
AX = mybir.AxisListType
BFNP = ml_dtypes.bfloat16

B, T, D = 4, 4096, 1024
H, DH = 16, 64
TPC = 2048
FC = 8
CH = 128
NCHUNK = TPC // CH
QT = 256
NQ = TPC // QT
NKT = TPC // 128
NF = 1.0 / 32.0
CSHIFT = -64.0
LNEPS = 1e-5

SBASE = 16512
A1 = SBASE
A2 = SBASE + 65568
A3 = SBASE + 131104
A4 = SBASE + 163872
A5 = SBASE + 196640
SOLV = A4

ENGS = ("tensor", "vector", "scalar", "gpsimd", "sync")


class Prog:
    def __init__(self):
        self.nc = bass.Bass("TRN2", target_bir_lowering=False, debug=False)
        self.sems = {}
        self.cnt = {}

    def sem(self, name):
        if name not in self.sems:
            self.sems[name] = self.nc.alloc_semaphore(name)
            self.cnt[name] = 0
        return self.sems[name]

    def inc(self, instr, name, k=1):
        instr.then_inc(self.sem(name), k)
        self.cnt[name] += k
        return self.cnt[name]

    def dma(self, eng, out, in_, name):
        i = eng.dma_start(out=out, in_=in_)
        self.inc(i, name, 16)
        return i

    def n(self, name):
        self.sem(name)
        return self.cnt[name]

    def fence(self, engs=ENGS):
        return [(f"s_{e}", self.n(f"s_{e}")) for e in engs]


class Sched:

    def __init__(self, p):
        self.p = p
        self.steps = []

    def add(self, eng, fn, deps=(), sem_waits=()):
        self.steps.append(dict(eng=eng, fn=fn, deps=list(deps),
                               sem_waits=list(sem_waits), seq=None))
        return len(self.steps) - 1

    def emit(self, blk):
        p = self.p
        counts = {e: 0 for e in ENGS}
        for s in self.steps:
            counts[s["eng"]] += 1
            s["seq"] = counts[s["eng"]]
        base = {e: p.n(f"s_{e}") for e in ENGS}
        for e in ENGS:
            mysteps = [s for s in self.steps if s["eng"] == e]
            if not mysteps:
                continue

            def body(eng, mysteps=mysteps, e=e):
                for s in mysteps:
                    for d in s["deps"]:
                        ds = self.steps[d]
                        eng.wait_ge(p.sem(f"s_{ds['eng']}"),
                                    base[ds["eng"]] + ds["seq"])
                    for (sn, v) in s["sem_waits"]:
                        eng.wait_ge(p.sem(sn), v)
                    instr = s["fn"](eng)
                    nm = type(instr.ins).__name__ if instr is not None else ""
                    if instr is None or "DMA" in nm or "Dma" in nm or "Collective" in nm:
                        instr = eng.nop()
                    p.inc(instr, f"s_{e}", 1)
            getattr(blk, e)(body)


def build(stop_after="all", want_taps=()):
    p = Prog()
    nc = p.nc
    STAGES = ["x1", "x2", "l1prod", "l1solve", "l1pass2", "l1o", "x3", "x4", "all"]
    lim = STAGES.index(stop_after) if stop_after in STAGES else len(STAGES) - 1

    def reach(stg):
        return STAGES.index(stg) <= lim

    dt_i = lambda nm, sh, dt=F32: nc.dram_tensor(nm, sh, dt, kind="ExternalInput").ap()
    d_x = dt_i("x_fm", [D, T])
    d_wqT = dt_i("wqT", [D, D])
    d_wkT = dt_i("wkT", [D, D])
    d_wvT = dt_i("wvT", [D, D])
    d_wre = dt_i("wre", [D, D])
    d_wim = dt_i("wim", [D, D])
    d_bias = dt_i("bias_fm", [128, FC])
    d_wv1T = dt_i("wv1T", [D, D])
    d_woSW = dt_i("woSW", [64, H * D])
    d_wgbT = dt_i("wgbT", [D, 32])
    d_rwT = dt_i("rwT", [D, 8])
    d_w1 = dt_i("w1_all", [8, D, D])
    d_w2 = dt_i("w2_all", [8, D, D])
    d_sgate = dt_i("s_gate", [128, 1])
    d_freT = dt_i("c_freT", [128, FC * 128], BF16)
    d_fimT = dt_i("c_fimT", [128, FC * 128], BF16)
    d_fimnT = dt_i("c_fimnT", [128, FC * 128], BF16)
    d_gre = dt_i("c_gre", [D, D], BF16)
    d_gimn = dt_i("c_gimn", [D, D], BF16)
    d_cpack = dt_i("c_pack", [128, 512], BF16)
    d_cind = dt_i("c_ind", [16, FC * 128], BF16)
    d_cinde = dt_i("c_inde", [8, 8 * 128], BF16)
    d_cselh = dt_i("c_selh", [16, H * 64], BF16)
    d_conesb = dt_i("c_onesb", [128, 132], BF16)
    d_onesrow = dt_i("c_onesrow", [1, 128], BF16)
    d_cfl = dt_i("c_fl", [128, 4])

    d_out = nc.dram_tensor("out_fm", [D, TPC], F32, kind="ExternalOutput").ap()
    taps = {}
    for tn in want_taps:
        if tn == "srow":
            taps[tn] = nc.dram_tensor("tap_srow", [1, TPC], F32,
                                      kind="ExternalOutput").ap()
        elif tn in ("acc", "kfm", "vtk0", "xbff_a", "xbff_b", "qt"):
            taps[tn] = nc.dram_tensor("tap_" + tn, [128, 16384], BF16,
                                      kind="ExternalOutput").ap()
        elif tn == "rbcd":
            taps[tn] = nc.dram_tensor("tap_rbcd", [128, TPC], F32,
                                      kind="ExternalOutput").ap()
        elif tn in ("rbc2", "mbc2"):
            taps[tn] = nc.dram_tensor("tap_" + tn, [128, TPC], F32,
                                      kind="ExternalOutput").ap()
        elif tn == "lnstat":
            taps[tn] = nc.dram_tensor("tap_lnstat", [128, 128], F32,
                                      kind="ExternalOutput").ap()
        elif tn == "lnrow":
            taps[tn] = nc.dram_tensor("tap_lnrow", [1, 2 * TPC], BF16,
                                      kind="ExternalOutput").ap()
        else:
            taps[tn] = nc.dram_tensor("tap_" + tn, [D, TPC],
                                      BF16 if tn in ("att", "l1o") else F32,
                                      kind="ExternalOutput").ap()

    cc_m_in = nc.dram_tensor("cc_m_in", [128, D], BF16)
    cc_m_out = nc.dram_tensor("cc_m_out", [8 * 128, D], BF16)
    cc_rb_in = nc.dram_tensor("cc_rb_in", [128, FC], BF16)
    cc_rb_out = nc.dram_tensor("cc_rb_out", [256, FC], BF16)
    cc_s_in = nc.dram_tensor("cc_s_in", [128, 8 * 64], F32)
    cc_s_out = nc.dram_tensor("cc_s_out", [256, 8 * 64], F32)
    _serk = {"kind": "ExternalOutput"} if os.environ.get("DBG_SER") else {}
    if os.environ.get("DBG_SER"):
        d_dbg_rk = nc.dram_tensor("dbg_rk", [128, FC * (TPC + 1)], BF16, kind="ExternalOutput")
        d_dbg_btk = nc.dram_tensor("dbg_btk", [128, NCHUNK * H], F32, kind="ExternalOutput")
        d_dbg_vtk = nc.dram_tensor("dbg_vtk", [128, NCHUNK * D], BF16, kind="ExternalOutput")
        d_dbg_nrm = nc.dram_tensor("dbg_nrm", [16, TPC], F32, kind="ExternalOutput")
    d_serP = nc.dram_tensor("ser_pt", [NCHUNK, 128, 8 * 64], BF16, **_serk)
    d_serQ = nc.dram_tensor("ser_q", [NCHUNK, 128, 8 * 64], F32, **_serk)
    d_serW = nc.dram_tensor("ser_yw", [NCHUNK, 128, 8 * 128], BF16, **_serk)
    d_serO = nc.dram_tensor("ser_yo", [NCHUNK, 128, 8 * 128], BF16, **_serk)

    at = nc.alloc_sbuf_tensor_at
    s_xbff = at("xbf_full", [128, FC * T], BF16, offset=A1)
    s_Mall = at("M_all", [128, FC * D], BF16, offset=A1)
    s_rk = at("rk", [128, FC * (TPC + 1)], BF16, offset=A1)
    s_wv1t = at("wv1t", [128, FC * D], BF16, offset=A1 + 32800)
    s_wgbt = at("wgbt", [128, FC * 32], BF16, offset=A1 + 49216)
    s_x2sq = at("x2sq", [128, 512], BF16, offset=A1 + 49792)
    s_nrm = at("nrm", [16, TPC], F32, offset=A1 + 50816)
    s_nrmb = at("nrm_bf", [16, TPC], BF16, offset=A1 + 59008)
    s_brow = at("brow", [16, TPC], BF16, offset=A1 + 50816)
    s_woSW = at("woSW", [64, H * D], BF16, offset=A1 + 32800)
    s_moew = [at(f"moew{i}", [128, FC * D], BF16, offset=A1 + i * 16384)
              for i in range(4)]
    s_kfm = at("kfm", [128, FC * TPC], BF16, offset=A2)
    s_vtk0 = at("vtk0", [128, NKT * D], BF16, offset=A2 + 32768)
    s_xres = at("xres", [128, FC * TPC], F32, offset=A2)
    s_fre = at("fre", [128, FC * 128], BF16, offset=A3)
    s_fim = at("fim", [128, FC * 128], BF16, offset=A3 + 2048)
    s_fimn = at("fimn", [128, FC * 128], BF16, offset=A3 + 4096)
    s_areT = at("areT", [128, FC * 128], BF16, offset=A3 + 6144)
    s_aimT = at("aimT", [128, FC * 128], BF16, offset=A3 + 8192)
    s_mmine = at("m_mine", [128, D], BF16, offset=A3 + 10240)
    s_acc = at("acc_att", [128, FC * TPC], BF16, offset=A3)
    s_rbc = at("rbc", [128, TPC], F32, offset=A3)
    s_mbc = at("mbc", [128, TPC], F32, offset=A3 + 8192)
    s_zbf = at("zbf", [128, TPC], BF16, offset=A3 + 16384)
    s_zsq = at("zsq", [128, TPC], BF16, offset=A3 + 20480)
    s_rrow = at("rrow", [1, 2 * TPC], BF16, offset=A3 + 24576)
    s_vtk1 = at("vtk1", [128, NCHUNK * D], BF16, offset=A3)
    s_htile = at("htile", [128, FC * 512], BF16, offset=A3)
    s_wexp = at("wexp", [128, 512], BF16, offset=A3 + 8192)
    s_wgtb = at("wgt_rows", [8, TPC], BF16, offset=A3 + 9216)
    s_rtk = at("rtk", [128, 3 * 8], F32, offset=A3 + 13312)
    s_mp = at("mp", [128, 24], F32, offset=A3 + 13408)
    s_wgtk = at("wgtk", [128, 16], BF16, offset=A3 + 13504)
    s_wslabA = at("wslabA", [128, FC * D], BF16, offset=A4)
    s_wslabB = at("wslabB", [128, FC * D], BF16, offset=A4 + 16384)
    s_qtile = at("qtile", [128, FC * QT], BF16, offset=A4)
    s_rsb = at("recipS_bf", [1, TPC], BF16, offset=A4 + 12288)
    s_recipSbc = at("recipSbc", [128, TPC], F32, offset=A4 + 16384)
    s_ptile = at("ptile", [128, 16 * QT], BF16, offset=A4 + 4096)
    s_xbf = at("xbf", [128, FC * TPC], BF16, offset=A4)
    s_fg = at("fg", [128, 2 * 14 * 128], BF16, offset=SOLV)
    s_X = at("Xbuf", [128, 2 * 2 * 128], BF16, offset=SOLV + 7168)
    s_ktk = at("ktk", [128, 2 * 64], BF16, offset=SOLV + 8192)
    s_kx = at("kx", [64, 2 * 128], F32, offset=SOLV + 8448)
    s_yxt = at("yxt", [128, 2 * 128], BF16, offset=SOLV + 9472)
    s_yt = at("yt", [128, 2 * 128], BF16, offset=SOLV + 9984)
    s_stgP = at("stgP", [128, 8 * 64], BF16, offset=SOLV + 10496)
    s_stgQ = at("stgQ", [128, 8 * 64], F32, offset=SOLV + 11520)
    s_stgW = at("stgW", [128, 8 * 128], BF16, offset=SOLV + 13568)
    s_stgO = at("stgO", [128, 8 * 128], BF16, offset=SOLV + 15616)
    s_och = at("ochunk", [64, H * 128], BF16, offset=SOLV + 17664)
    s_gexp = at("gexp", [64, H * 128], BF16, offset=SOLV + 21760)
    s_p2P = [at(f"p2P{i}", [128, 8 * 64], BF16, offset=SOLV + i * 1024)
             for i in range(2)]
    s_p2Q = [at(f"p2Q{i}", [128, 8 * 64], F32, offset=SOLV + 2048 + i * 2048)
             for i in range(2)]
    s_p2W = [at(f"p2W{i}", [128, 8 * 128], BF16, offset=SOLV + 6144 + i * 2048)
             for i in range(2)]
    s_p2O = [at(f"p2O{i}", [128, 8 * 128], BF16, offset=SOLV + 10240 + i * 2048)
             for i in range(2)]
    s_ident = at("ident", [128, 128], BF16, offset=A5)
    s_masksl = at("mask_sl", [128, 128], BF16, offset=A5 + 256)
    s_masksui = at("mask_sui", [128, 128], BF16, offset=A5 + 512)
    s_swap = at("swapm", [128, 128], BF16, offset=A5 + 768)
    s_onesb = at("onesb", [128, 132], BF16, offset=A5 + 1024)
    s_onesrow = at("onesrow", [1, 128], BF16, offset=A5 + 1312)
    s_cdyn = at("cdyn", [16, 1024], BF16, offset=A5 + 1568)
    DYN = A5 + 3616
    s_srow = at("srow", [1, TPC], F32, offset=DYN)
    s_Spair = at("Spair", [128, 8 * 64], F32, offset=DYN)
    s_Sb = at("Sb", [128, 8 * 64], BF16, offset=DYN + 2048)
    s_Sin = at("Sin", [128, 8 * 64], F32, offset=DYN + 3072)
    s_btk = at("btk", [128, NCHUNK * H], F32, offset=DYN + 5120)
    s_stat = at("stat_tk", [128, 16 * 8], F32, offset=DYN + 6144)
    s_statb = at("statb_tk", [128, 32], BF16, offset=DYN + 6656)
    s_grow = at("g_rows", [16, TPC], BF16, offset=A5 + 11808)
    s_sgt = at("sgt", [128, 1], F32, offset=A5 + 15904)
    s_biasf = at("biasf", [128, FC], F32, offset=A5 + 15936)
    s_rbnd = at("rbnd", [128, FC], BF16, offset=A5 + 15968)
    s_cfl = at("cfl", [128, 4], F32, offset=A5 + 16000)

    pb = [nc.alloc_psum_tensor(f"pb{i}", [128, 512], F32) for i in range(7)]
    pT = nc.alloc_psum_tensor("pTt", [128, 1024], BF16)

    ONEC = s_onesb[:, 128:129]

    def blk1(c):
        return s_onesb[:, c * 16:(c + 1) * 16]

    def xb(c, t0, n):
        return s_xbff[:, c * T + t0: c * T + t0 + n]

    def st(buf, c, t0, n):
        return buf[:, c * TPC + t0: c * TPC + t0 + n]

    def rksl(c, col0, n):
        return s_rk[:, c * (TPC + 1) + col0: c * (TPC + 1) + col0 + n]

    with nc.Block() as blk:
        def p0_sync(sync):
            for (dst, src) in [(s_ident[:], d_cpack[:, 0:128]),
                               (s_masksl[:], d_cpack[:, 128:256]),
                               (s_masksui[:], d_cpack[:, 256:384]),
                               (s_swap[:], d_cpack[:, 384:512]),
                               (s_onesb[:], d_conesb), (s_onesrow[:], d_onesrow),
                               (s_fre[:], d_freT), (s_fim[:], d_fimT),
                               (s_fimn[:], d_fimnT), (s_sgt[:], d_sgate),
                               (s_biasf[:], d_bias), (s_cfl[:], d_cfl)]:
                p.dma(sync, dst, src, "dma_c")
        blk.sync(p0_sync)

        def p0_gp(gp):
            for c in range(FC):
                p.dma(gp, xb(c, 0, T), d_x[c * 128:(c + 1) * 128, :], "dma_x")
            for c in range(FC):
                p.dma(gp, s_kfm[:, c * D:(c + 1) * D],
                      d_wre[c * 128:(c + 1) * 128, :], "dma_w1")
            for c in range(FC):
                p.dma(gp, s_kfm[:, 8192 + c * D: 8192 + (c + 1) * D],
                      d_wim[c * 128:(c + 1) * 128, :], "dma_w1")
            for c in range(FC):
                p.dma(gp, s_vtk0[:, c * D:(c + 1) * D],
                      d_gre[c * 128:(c + 1) * 128, :], "dma_w1")
            for c in range(FC):
                p.dma(gp, s_vtk0[:, 8192 + c * D: 8192 + (c + 1) * D],
                      d_gimn[c * 128:(c + 1) * 128, :], "dma_w1")
            for c in range(FC):
                p.dma(gp, s_wslabA[:, c * D:(c + 1) * D],
                      d_wkT[c * 128:(c + 1) * 128, :], "dma_wk0")
            for c in range(FC):
                p.dma(gp, s_wslabB[:, c * D:(c + 1) * D],
                      d_wqT[c * 128:(c + 1) * 128, :], "dma_wq")
        blk.gpsimd(p0_gp)

        def wre_t(dc, kc):
            return s_kfm[:, dc * D + kc * 128: dc * D + (kc + 1) * 128]

        def wim_t(dc, kc):
            return s_kfm[:, 8192 + dc * D + kc * 128: 8192 + dc * D + (kc + 1) * 128]

        sc = Sched(p)
        W_RDY = [("dma_w1", p.n("dma_w1")), ("dma_c", p.n("dma_c"))]
        evs = []
        for kc in range(FC):
            def mk_a(eng, kc=kc, im=False):
                bank = pb[1] if im else pb[0]
                for dc in range(FC):
                    eng.matmul(bank[:, 0:128],
                               (wim_t if im else wre_t)(dc, kc),
                               s_fre[:, dc * 128:(dc + 1) * 128],
                               start=(dc == 0), stop=False)
                for dc in range(FC):
                    i = eng.matmul(bank[:, 0:128],
                                   (wre_t if im else wim_t)(dc, kc),
                                   (s_fim if im else s_fimn)[:, dc * 128:(dc + 1) * 128],
                                   start=False, stop=(dc == FC - 1))
                return i
            i_re = sc.add("tensor", lambda e, kc=kc: mk_a(e, kc, False),
                          deps=([evs[-1][0]] if evs else []),
                          sem_waits=(W_RDY if kc == 0 else ()))
            e_re = sc.add("vector", lambda e, kc=kc: e.tensor_copy(
                s_areT[:, kc * 128:(kc + 1) * 128], pb[0][:, 0:128]), deps=[i_re])
            i_im = sc.add("tensor", lambda e, kc=kc: mk_a(e, kc, True),
                          deps=([evs[-1][1]] if evs else []))
            e_im = sc.add("vector", lambda e, kc=kc: e.tensor_copy(
                s_aimT[:, kc * 128:(kc + 1) * 128], pb[1][:, 0:128]), deps=[i_im])
            evs.append((e_re, e_im))
        mevs = []
        for half in range(2):
            def mk_m(eng, half=half):
                for kc in range(FC):
                    eng.matmul(pb[2][:, 0:512], s_areT[:, kc * 128:(kc + 1) * 128],
                               s_vtk0[:, kc * D + half * 512: kc * D + half * 512 + 512],
                               start=(kc == 0), stop=False)
                for kc in range(FC):
                    i = eng.matmul(pb[2][:, 0:512], s_aimT[:, kc * 128:(kc + 1) * 128],
                                   s_vtk0[:, 8192 + kc * D + half * 512: 8192 + kc * D + half * 512 + 512],
                                   start=False, stop=(kc == FC - 1))
                return i
            im = sc.add("tensor", mk_m,
                        deps=([evs[-1][0], evs[-1][1]] if half == 0 else [mevs[0]]))
            ev = sc.add("vector", lambda e, half=half: e.tensor_copy(
                s_mmine[:, half * 512: half * 512 + 512], pb[2][:, 0:512]), deps=[im])
            mevs.append(ev)

        def ship_m(eng):
            i = eng.dma_start(out=cc_m_in[:, :], in_=s_mmine[:])
            p.inc(i, "dma_cc", 16)
            return i
        dship = sc.add("gpsimd", ship_m, deps=mevs)

        def do_ccm(eng):
            eng.wait_ge(p.sem("dma_cc"), p.n("dma_cc"))
            i = eng.collective_compute(
                "AllGather", ALU.bypass,
                replica_groups=[[0, 1, 2, 3, 4, 5, 6, 7]],
                ins=[cc_m_in.ap().opt()], outs=[cc_m_out.ap().opt()])
            p.inc(i, "ccm_flag", 1)
            return i
        sc.add("gpsimd", do_ccm, deps=[dship])
        sc.emit(blk)
        CCM_DONE = [("ccm_flag", 1)]

        prev_tail = {}
        for half in range(2):
            sc = Sched(p)
            kv_evs = []
            if half == 1:
                def ld_wk1(eng):
                    for w in prev_tail["attend"] + prev_tail["pb45"]:
                        eng.wait_ge(p.sem(w[0]), w[1])
                    for c in range(FC):
                        i = eng.dma_start(out=s_wslabA[:, c * D:(c + 1) * D],
                                          in_=d_wkT[c * 128:(c + 1) * 128, :])
                        p.inc(i, "dma_wk1", 16)
                    return i
                sc.add("gpsimd", ld_wk1)
            for dc in range(FC):
                for t4 in range(4):
                    j = dc * 4 + t4

                    def mk_k(eng, dc=dc, t4=t4, j=j, half=half):
                        bank = pb[4 + j % 2]
                        for c in range(FC):
                            i = eng.matmul(
                                bank[:, 0:512],
                                s_wslabA[:, c * D + dc * 128: c * D + (dc + 1) * 128],
                                xb(c, half * TPC + t4 * 512, 512),
                                start=(c == 0), stop=(c == FC - 1))
                        return i

                    def ev_k(eng, dc=dc, t4=t4, j=j):
                        return eng.tensor_copy(
                            s_kfm[:, dc * TPC + t4 * 512: dc * TPC + t4 * 512 + 512],
                            pb[4 + j % 2][:, 0:512])
                    w = list(prev_tail.get("pb45", []))
                    if j == 0:
                        w += [("dma_x", 16 * FC), (f"dma_wk{half}", 16 * FC)]
                        if half == 1:
                            w += prev_tail["attend"]
                    im = sc.add("tensor", mk_k,
                                deps=([kv_evs[-2]] if len(kv_evs) >= 2 else []),
                                sem_waits=(w if j <= 1 else ()))
                    kv_evs.append(sc.add("vector", ev_k, deps=[im]))
            k_done = list(kv_evs[-2:])
            def ld_wv(eng, half=half):
                for c in range(FC):
                    i = eng.dma_start(out=s_wslabA[:, c * D:(c + 1) * D],
                                      in_=d_wvT[c * 128:(c + 1) * 128, :])
                    p.inc(i, f"dma_wv{half}", 16)
                return i
            sc.add("gpsimd", ld_wv, deps=k_done)
            for tt in range(NKT):
                for vh in range(2):
                    j = tt * 2 + vh

                    def mk_v(eng, tt=tt, vh=vh, j=j, half=half):
                        bank = pb[4 + j % 2]
                        for c in range(FC):
                            i = eng.matmul(
                                bank[:, 0:512],
                                xb(c, half * TPC + tt * 128, 128),
                                s_wslabA[:, c * D + vh * 512: c * D + vh * 512 + 512],
                                start=(c == 0), stop=(c == FC - 1))
                        return i

                    def ev_v(eng, tt=tt, vh=vh, j=j):
                        return eng.tensor_copy(
                            s_vtk0[:, tt * D + vh * 512: tt * D + vh * 512 + 512],
                            pb[4 + j % 2][:, 0:512])
                    w = [(f"dma_wv{half}", 16 * FC)] if j == 0 else ()
                    im = sc.add("tensor", mk_v, deps=[kv_evs[-2]], sem_waits=w)
                    kv_evs.append(sc.add("vector", ev_v, deps=[im]))
            vk_done = list(kv_evs[-2:])

            prev = None
            for qi in range(NQ):
                qevs = []
                for g in range(4):
                    def mk_q(eng, qi=qi, g=g):
                        bank = pb[4 + g % 2]
                        for k in range(2):
                            dc = g * 2 + k
                            for c in range(FC):
                                i = eng.matmul(
                                    bank[:, k * 256: k * 256 + 256],
                                    s_wslabB[:, c * D + dc * 128: c * D + (dc + 1) * 128],
                                    xb(c, qi * QT, QT),
                                    start=(c == 0), stop=(c == FC - 1))
                        return i

                    def ev_q(eng, qi=qi, g=g):
                        return eng.tensor_copy(
                            s_qtile[:, g * 512:(g + 1) * 512], pb[4 + g % 2][:, 0:512])
                    w = [("dma_wq", 16 * FC)] if (half == 0 and qi == 0 and g == 0) else ()
                    if qi == 0:
                        deps = vk_done if g < 2 else [qevs[g - 2]]
                    else:
                        deps = [prev["exp"][14 + g % 2]] if g < 2 else [qevs[g - 2]]
                    im = sc.add("tensor", mk_q, deps=deps, sem_waits=w)
                    qevs.append(sc.add("vector", ev_q, deps=[im]))
                exp_steps, log_steps = [], []
                for kt in range(NKT):
                    def mk_log(eng, qi=qi, kt=kt):
                        bank = pb[4 + kt % 2]
                        for c in range(FC):
                            i = eng.matmul(
                                bank[:, 0:256],
                                s_kfm[:, c * TPC + kt * 128: c * TPC + (kt + 1) * 128],
                                s_qtile[:, c * QT:(c + 1) * QT],
                                start=(c == 0), stop=(c == FC - 1))
                        return i
                    deps = list(qevs[2:]) if kt < 2 else [exp_steps[kt - 2]]
                    ilog = sc.add("tensor", mk_log, deps=deps)

                    def mk_exp(eng, qi=qi, kt=kt):
                        return eng.activation(
                            s_ptile[:, kt * QT: kt * QT + QT],
                            pb[4 + kt % 2][:, 0:256], AF.Exp,
                            bias=s_cfl[:, 0:1], scale=1.0)
                    deps = [ilog]
                    if kt == 0 and prev:
                        deps += [prev["pv"][-1]]
                    iexp = sc.add("scalar", mk_exp, deps=deps,
                                  sem_waits=([("dma_c", p.n("dma_c"))]
                                             if (half == 0 and qi == 0 and kt == 0) else ()))
                    exp_steps.append(iexp)
                pv_steps = []

                def mk_ssum(eng, qi=qi):
                    for kt in range(NKT):
                        i = eng.matmul(pb[6][0:1, 0:256], ONEC,
                                       s_ptile[:, kt * QT: kt * QT + QT],
                                       start=(kt == 0), stop=(kt == NKT - 1))
                    return i
                pv_steps.append(sc.add("tensor", mk_ssum,
                                       deps=[exp_steps[-1]] +
                                       ([prev["sevac"][4]] if prev else [])))
                for vc in range(FC):
                    def mk_pv(eng, qi=qi, vc=vc):
                        for kt in range(NKT):
                            i = eng.matmul(
                                pb[vc // 2][:, (vc % 2) * 256: (vc % 2) * 256 + 256],
                                s_vtk0[:, kt * D + vc * 128: kt * D + (vc + 1) * 128],
                                s_ptile[:, kt * QT: kt * QT + QT],
                                start=(kt == 0), stop=(kt == NKT - 1))
                        return i
                    deps = [exp_steps[-1]] + ([prev["sevac"][vc // 2]] if prev else [])
                    pv_steps.append(sc.add("tensor", mk_pv, deps=deps))
                sevac = []
                for g in range(4):
                    def ev_acc(eng, qi=qi, g=g, half=half):
                        for k in range(2):
                            vc = g * 2 + k
                            dst = s_acc[:, vc * TPC + qi * QT: vc * TPC + qi * QT + QT]
                            src = pb[g][:, k * 256: k * 256 + 256]
                            if half == 0:
                                i = eng.tensor_copy(dst, src)
                            else:
                                i = eng.tensor_tensor(dst, dst, src, ALU.add)
                        return i
                    sevac.append(sc.add("vector", ev_acc, deps=[pv_steps[-1]]))

                def ev_s(eng, qi=qi, half=half):
                    dst = s_srow[0:1, qi * QT: qi * QT + QT]
                    if half == 0:
                        return eng.tensor_copy(dst, pb[6][0:1, 0:256])
                    return eng.tensor_tensor(dst, dst, pb[6][0:1, 0:256], ALU.add)
                sevac.append(sc.add("vector", ev_s, deps=[pv_steps[-1]]))
                prev = dict(exp=exp_steps, pv=pv_steps, sevac=sevac)
            sc.emit(blk)
            prev_tail = dict(pb45=p.fence(("scalar",)), attend=p.fence(("vector",)))

        ATT_DONE = p.fence(("vector", "tensor", "scalar"))

        def dbg_tap(sync):
            for w in ATT_DONE:
                sync.wait_ge(p.sem(w[0]), w[1])
            if "acc" in taps:
                p.dma(sync, taps["acc"], s_acc[:], "dma_tap")
            if "srow" in taps:
                p.dma(sync, taps["srow"], s_srow[:], "dma_tap")
            if "kfm" in taps:
                p.dma(sync, taps["kfm"], s_kfm[:], "dma_tap")
            if "vtk0" in taps:
                p.dma(sync, taps["vtk0"], s_vtk0[:], "dma_tap")
            if "xbff_a" in taps:
                p.dma(sync, taps["xbff_a"], s_xbff[:, 0:16384], "dma_tap")
            if "xbff_b" in taps:
                p.dma(sync, taps["xbff_b"], s_xbff[:, 16384:32768], "dma_tap")
            if "qt" in taps:
                p.dma(sync, taps["qt"][:, 0:FC * QT], s_qtile[:], "dma_tap")
        if any(t in taps for t in ("acc", "srow", "kfm", "vtk0", "xbff_a", "xbff_b", "qt")):
            blk.sync(dbg_tap)
        TAPW = [("dma_tap", p.n("dma_tap"))] if p.n("dma_tap") else []

        def ld_x0(sync):
            for w in ATT_DONE:
                sync.wait_ge(p.sem(w[0]), w[1])
            for c in range(FC):
                p.dma(sync, st(s_xres, c, 0, TPC), d_x[c * 128:(c + 1) * 128, 0:TPC],
                      "dma_x0")
        blk.sync(ld_x0)

        sc = Sched(p)
        r0 = sc.add("vector", lambda e: e.reciprocal(s_srow[:], s_srow[:]),
                    sem_waits=ATT_DONE + TAPW)
        r1 = sc.add("vector", lambda e: e.tensor_scalar_mul(s_srow[:], s_srow[:], NF),
                    deps=[r0])
        r2 = sc.add("scalar", lambda e: e.copy(s_rsb[:], s_srow[:]), deps=[r1])
        bexp = []
        for t4 in range(4):
            def mk_b(eng, t4=t4):
                return eng.matmul(pb[4 + t4 % 2][:, 0:512],
                                  s_onesrow[0:1, 0:128],
                                  s_rsb[0:1, t4 * 512:(t4 + 1) * 512],
                                  start=True, stop=True)
            im = sc.add("tensor", mk_b, deps=[r2] + ([bexp[t4 - 2]] if t4 >= 2 else []))
            bexp.append(sc.add("vector", lambda e, t4=t4: e.tensor_copy(
                s_recipSbc[:, t4 * 512:(t4 + 1) * 512], pb[4 + t4 % 2][:, 0:512]),
                deps=[im]))
        zsteps = []
        for c in range(FC):
            a1 = sc.add("vector", lambda e, c=c: e.tensor_tensor(
                st(s_acc, c, 0, TPC), st(s_acc, c, 0, TPC), s_recipSbc[:], ALU.mult),
                deps=[bexp[-1]] if c == 0 else [],
                sem_waits=(TAPW if c == 0 else ()))
            a2 = sc.add("vector", lambda e, c=c: e.tensor_tensor(
                st(s_xres, c, 0, TPC), st(s_xres, c, 0, TPC), st(s_acc, c, 0, TPC),
                ALU.add), deps=[a1],
                sem_waits=[("dma_x0", 16 * FC)] if c == 0 else ())
            zsteps.append(a2)
        sc.emit(blk)

        Z_TAPW = []
        if "z" in taps or "rbcd" in taps:
            zf = p.fence(("vector",))

            def dbg2(sync):
                for w in zf:
                    sync.wait_ge(p.sem(w[0]), w[1])
                if "z" in taps:
                    for c in range(FC):
                        p.dma(sync, taps["z"][c * 128:(c + 1) * 128, :],
                              st(s_xres, c, 0, TPC), "dma_tap")
                if "rbcd" in taps:
                    p.dma(sync, taps["rbcd"][0:128, :], s_recipSbc[:], "dma_tap")
            blk.sync(dbg2)
            Z_TAPW = [("dma_tap", p.n("dma_tap"))]

        lndbg = {k: taps[k] for k in ("lnstat", "lnrow", "rbc2", "mbc2") if k in taps}
        emit_ln(p, blk, locals(), dep_fence=p.fence(("vector",)) + Z_TAPW,
                dbg=lndbg or None)
        if want_taps and "x1" in taps:
            emit_tap(p, blk, locals(), taps["x1"], s_xres, F32)
        if stop_after == "x1":
            emit_final(p, blk, locals(), s_xres, stop_early=True)
            return p, dict(locals())

        emit_ffn(p, blk, locals())
        if want_taps and "x2" in taps:
            emit_tap(p, blk, locals(), taps["x2"], s_xres, F32)
        if stop_after == "x2":
            emit_final(p, blk, locals(), s_xres, stop_early=True)
            return p, dict(locals())

        sub = {"l1prod": "prod", "l1solve": "solve", "l1pass2": "pass2"}.get(stop_after, "all")
        emit_l1(p, blk, locals(), taps if "l1o" in taps else None, substage=sub)
        if stop_after in ("l1prod", "l1solve", "l1pass2"):
            emit_final(p, blk, locals(), s_xres, stop_early=True)
            return p, dict(locals())
        if stop_after == "l1o":
            emit_final(p, blk, locals(), s_xres, stop_early=True)
            return p, dict(locals())
        if want_taps and "x3" in taps:
            emit_tap(p, blk, locals(), taps["x3"], s_xres, F32)
        if stop_after == "x3":
            emit_final(p, blk, locals(), s_xres, stop_early=True)
            return p, dict(locals())

        emit_moe(p, blk, locals())
        if want_taps and "x4" in taps:
            emit_tap(p, blk, locals(), taps["x4"], s_xres, F32)

        emit_final(p, blk, locals(), s_xres, stop_early=False)
    return p, dict(locals())


class NS:
    def __init__(self, d):
        self.__dict__.update(d)


def emit_ln(p, blk, env, dep_fence, dbg=None):
    g = NS(env)
    sc = Sched(p)
    ST = g.s_stat
    z0 = sc.add("vector", lambda e: e.memset(ST[:, 0:32], 0.0), sem_waits=dep_fence)
    tprev = None
    eprev = z0
    for c in range(FC):
        s1 = sc.add("scalar", lambda e, c=c: e.copy(
            g.s_zbf[:], g.st(g.s_xres, c, 0, TPC)),
            deps=([] if tprev is None else [tprev]),
            sem_waits=(dep_fence if c == 0 else ()))
        s2 = sc.add("scalar", lambda e, c=c: e.square(
            g.s_zsq[:], g.st(g.s_xres, c, 0, TPC)))

        def mk_sum(eng, c=c):
            for tt in range(16):
                eng.matmul(g.pb[4][:, tt:tt + 1],
                           g.s_zbf[:, tt * 128:(tt + 1) * 128], g.ONEC,
                           start=True, stop=True)
            for tt in range(16):
                i = eng.matmul(g.pb[4][:, 16 + tt:17 + tt],
                               g.s_zsq[:, tt * 128:(tt + 1) * 128], g.ONEC,
                               start=True, stop=True)
            return i
        tprev = sc.add("tensor", mk_sum, deps=[s1, s2, eprev],
                       sem_waits=(dep_fence if c == 0 else ()))
        eprev = sc.add("vector", lambda e, c=c: e.tensor_tensor(
            ST[:, 0:32], ST[:, 0:32], g.pb[4][:, 0:32], ALU.add), deps=[tprev])
    v = eprev
    v = sc.add("vector", lambda e: e.tensor_scalar_mul(ST[:, 32:48], ST[:, 0:16], 1.0 / D), deps=[v])
    v = sc.add("vector", lambda e: e.tensor_scalar_mul(ST[:, 48:64], ST[:, 16:32], 1.0 / D), deps=[v])
    v = sc.add("vector", lambda e: e.tensor_mul(ST[:, 64:80], ST[:, 32:48], ST[:, 32:48]), deps=[v])
    v = sc.add("vector", lambda e: e.tensor_sub(ST[:, 48:64], ST[:, 48:64], ST[:, 64:80]), deps=[v])
    v = sc.add("vector", lambda e: e.tensor_scalar_add(ST[:, 48:64], ST[:, 48:64], LNEPS), deps=[v])
    sq = sc.add("scalar", lambda e: e.sqrt(ST[:, 64:80], ST[:, 48:64]), deps=[v])
    v = sc.add("vector", lambda e: e.reciprocal(ST[:, 80:96], ST[:, 64:80]), deps=[sq])
    vb = sc.add("vector", lambda e: e.tensor_copy(g.s_statb[:, 0:16], ST[:, 80:96]), deps=[v])
    vb = sc.add("vector", lambda e: e.tensor_copy(g.s_statb[:, 16:32], ST[:, 32:48]), deps=[vb])
    vr = vb
    for grp in range(4):
        def mk_tr(eng, grp=grp):
            for k in range(8):
                i = eng.transpose(g.pT[0:1, k * 128:(k + 1) * 128],
                                  g.s_statb[:, grp * 8 + k: grp * 8 + k + 1],
                                  g.s_ident[:])
            return i
        ttg = sc.add("tensor", mk_tr, deps=[vr])

        def mk_rows(eng, grp=grp):
            for k in range(8):
                col = grp * 8 + k
                dst = (g.s_rrow[0:1, col * 128:(col + 1) * 128] if col < 16 else
                       g.s_rrow[0:1, TPC + (col - 16) * 128: TPC + (col - 15) * 128])
                i = eng.tensor_copy(dst, g.pT[0:1, k * 128:(k + 1) * 128])
            return i
        vr = sc.add("vector", mk_rows, deps=[ttg])
    bsteps = []
    for row, dst in ((0, g.s_rbc), (1, g.s_mbc)):
        for t4 in range(4):
            j = row * 4 + t4

            def mk_b(eng, row=row, t4=t4, j=j):
                return eng.matmul(g.pb[4 + j % 2][:, 0:512],
                                  g.s_onesrow[0:1, 0:128],
                                  g.s_rrow[0:1, row * TPC + t4 * 512: row * TPC + (t4 + 1) * 512],
                                  start=True, stop=True)
            deps = [vr] if j < 2 else [bsteps[j - 2]]
            im = sc.add("tensor", mk_b, deps=deps)
            bsteps.append(sc.add("vector", lambda e, dst=dst, t4=t4, j=j: e.tensor_copy(
                dst[:, t4 * 512:(t4 + 1) * 512], g.pb[4 + j % 2][:, 0:512]), deps=[im]))
    dbg_w = []
    if dbg:
        def dbg_ln(sync):
            for st_ in [bsteps[-1], bsteps[3]]:
                pass
            sync.wait_ge(p.sem("s_vector"), p.n("s_vector"))
            if "lnstat" in dbg:
                p.dma(sync, dbg["lnstat"], g.s_stat[:], "dma_tap")
            if "lnrow" in dbg:
                p.dma(sync, dbg["lnrow"], g.s_rrow[:], "dma_tap")
            if "rbc2" in dbg:
                p.dma(sync, dbg["rbc2"], g.s_rbc[:], "dma_tap")
            if "mbc2" in dbg:
                p.dma(sync, dbg["mbc2"], g.s_mbc[:], "dma_tap")
        dbg_w = [True]
    fin = []
    aprev = None
    for c in range(FC):
        a1 = sc.add("vector", lambda e, c=c: e.tensor_sub(
            g.st(g.s_xres, c, 0, TPC), g.st(g.s_xres, c, 0, TPC), g.s_mbc[:]),
            deps=([bsteps[7]] if c == 0 else [aprev]))
        a2 = sc.add("vector", lambda e, c=c: e.tensor_mul(
            g.st(g.s_xres, c, 0, TPC), g.st(g.s_xres, c, 0, TPC), g.s_rbc[:]),
            deps=[a1] + ([bsteps[3]] if c == 0 else []))
        aprev = a2
        fin.append(sc.add("scalar", lambda e, c=c: e.copy(
            g.st(g.s_xbf, c, 0, TPC), g.st(g.s_xres, c, 0, TPC)), deps=[a2]))
    sc.emit(blk)
    if dbg:
        blk.sync(dbg_ln)


def emit_tap(p, blk, env, tap_ap, src, dt):
    g = NS(env)
    f = p.fence()

    def tap(sync):
        for w in f:
            sync.wait_ge(p.sem(w[0]), w[1])
        for c in range(FC):
            p.dma(sync, tap_ap[c * 128:(c + 1) * 128, :], g.st(src, c, 0, TPC),
                  "dma_tap")
    blk.sync(tap)


def emit_final(p, blk, env, src, stop_early):
    g = NS(env)
    f = p.fence()

    def fin(sync):
        for w in f:
            sync.wait_ge(p.sem(w[0]), w[1])
        for c in range(FC):
            p.dma(sync, g.d_out[c * 128:(c + 1) * 128, :], g.st(src, c, 0, TPC),
                  "dma_out")
        sync.wait_ge(p.sem("dma_out"), p.n("dma_out"))
        if p.n("dma_tap"):
            sync.wait_ge(p.sem("dma_tap"), p.n("dma_tap"))
        if p.n("dma_dbg"):
            sync.wait_ge(p.sem("dma_dbg"), p.n("dma_dbg"))
    blk.sync(fin)


def emit_ffn(p, blk, env):
    g = NS(env)

    def ld_m(sync):
        for w in env["CCM_DONE"]:
            sync.wait_ge(p.sem(w[0]), w[1])
        for ic in range(FC):
            p.dma(sync, g.s_Mall[:, ic * D:(ic + 1) * D],
                  g.cc_m_out[ic * 128:(ic + 1) * 128, :], "dma_m")
    blk.sync(ld_m)

    sc = Sched(p)
    F_X1 = p.fence(("scalar",))
    evs = []
    for oc in range(FC):
        for t4 in range(4):
            j = oc * 4 + t4

            def mk(eng, oc=oc, t4=t4, j=j):
                bank = g.pb[j % 4]
                for ic in range(FC):
                    i = eng.matmul(
                        bank[:, 0:512],
                        g.s_Mall[:, ic * D + oc * 128: ic * D + (oc + 1) * 128],
                        g.st(g.s_xbf, ic, t4 * 512, 512),
                        start=(ic == 0), stop=(ic == FC - 1))
                return i

            def ev(eng, oc=oc, t4=t4, j=j):
                return eng.scalar_tensor_tensor(
                    g.st(g.s_xres, oc, t4 * 512, 512),
                    g.pb[j % 4][:, 0:512], g.s_biasf[:, oc:oc + 1],
                    g.st(g.s_xres, oc, t4 * 512, 512), ALU.add, ALU.add)
            w = [("dma_m", p.n("dma_m"))] + F_X1 if j == 0 else ()
            im = sc.add("tensor", mk, deps=([evs[j - 4]] if j >= 4 else []),
                        sem_waits=w)
            evs.append(sc.add("vector", ev, deps=[im]))
    sc.emit(blk)
    emit_ln(p, blk, env, dep_fence=p.fence(("vector",)))


def emit_l1(p, blk, env, tapd, substage="all"):
    g = NS(env)
    nc = p.nc

    def ld1(gp):
        F = p.fence(("scalar",))
        for w in F:
            gp.wait_ge(p.sem(w[0]), w[1])
        for c in range(FC):
            p.dma(gp, g.s_wv1t[:, c * D:(c + 1) * D],
                  g.d_wv1T[c * 128:(c + 1) * 128, :], "dma_wv1")
        for c in range(FC):
            p.dma(gp, g.s_wgbt[:, c * 32:(c + 1) * 32],
                  g.d_wgbT[c * 128:(c + 1) * 128, :], "dma_wgb")
    blk.gpsimd(ld1)

    def ld2(sync):
        p.dma(sync, g.s_cdyn[:, 0:FC * 128], g.d_cind, "dma_cd")
    blk.sync(ld2)

    sc = Sched(p)
    FLN2 = p.fence(("scalar", "vector"))
    sig = []
    for role in range(2):
        for t4 in range(4):
            j = role * 4 + t4

            def mk(eng, role=role, t4=t4, j=j):
                bank = g.pb[4 + j % 2]
                for c in range(FC):
                    i = eng.matmul(
                        bank[0:16, 0:512],
                        g.s_wgbt[:, c * 32 + role * 16: c * 32 + role * 16 + 16],
                        g.st(g.s_xbf, c, t4 * 512, 512),
                        start=(c == 0), stop=(c == FC - 1))
                return i

            def ev(eng, role=role, t4=t4, j=j):
                dst = g.s_grow if role == 0 else g.s_brow
                return eng.activation(dst[0:16, t4 * 512:(t4 + 1) * 512],
                                      g.pb[4 + j % 2][0:16, 0:512], AF.Sigmoid)
            w = ([("dma_wgb", p.n("dma_wgb"))] + FLN2) if j == 0 else ()
            im = sc.add("tensor", mk, deps=([sig[j - 2]] if j >= 2 else []),
                        sem_waits=w)
            sig.append(sc.add("scalar", ev, deps=[im]))
    btk = []
    for tt in range(16):
        def mk_t(eng, tt=tt):
            return eng.transpose(g.pT[0:128, (tt % 4) * 16:(tt % 4) * 16 + 16],
                                 g.s_brow[0:16, tt * 128:(tt + 1) * 128],
                                 g.s_ident[0:16, 0:16])
        deps = [sig[4 + min(tt // 4, 3)]] + ([btk[tt - 1]] if tt >= 1 else [])
        im = sc.add("tensor", mk_t, deps=deps)
        btk.append(sc.add("vector", lambda e, tt=tt: e.tensor_copy(
            g.s_btk[:, tt * 16:(tt + 1) * 16],
            g.pT[0:128, (tt % 4) * 16:(tt % 4) * 16 + 16]), deps=[im]))
    vevs = []
    for tt in range(NCHUNK):
        for vh in range(2):
            j = tt * 2 + vh

            def mk_v(eng, tt=tt, vh=vh, j=j):
                bank = g.pb[4 + j % 2]
                for c in range(FC):
                    i = eng.matmul(
                        bank[:, 0:512],
                        g.st(g.s_xbf, c, tt * 128, 128),
                        g.s_wv1t[:, c * D + vh * 512: c * D + vh * 512 + 512],
                        start=(c == 0), stop=(c == FC - 1))
                return i

            def ev_v(eng, tt=tt, vh=vh, j=j):
                return eng.tensor_copy(
                    g.s_vtk1[:, tt * D + vh * 512: tt * D + vh * 512 + 512],
                    g.pb[4 + j % 2][:, 0:512])
            w = [("dma_wv1", p.n("dma_wv1"))] if j == 0 else ()
            deps = [btk[-1], btk[-2]] if j < 2 else [vevs[j - 2]]
            im = sc.add("tensor", mk_v, deps=deps, sem_waits=w)
            vevs.append(sc.add("vector", ev_v, deps=[im]))
    nsteps = []
    for t4 in range(4):
        sq_steps = []
        for c in range(FC):
            s1 = sc.add("scalar", lambda e, c=c, t4=t4: e.square(
                g.s_x2sq[:, 0:512], g.st(g.s_xbf, c, t4 * 512, 512)),
                deps=([nsteps[-1]] if (c == 0 and nsteps) else
                      ([vevs[-1]] if (c == 0 and t4 == 0) else [])) + ([sq_steps[-1]] if sq_steps else []))

            def mk_n(eng, c=c, t4=t4):
                return eng.matmul(g.pb[6][0:16, 0:512], g.blk1(c),
                                  g.s_x2sq[:, 0:512],
                                  start=(c == 0), stop=(c == FC - 1))
            t1 = sc.add("tensor", mk_n, deps=[s1])
            sq_steps.append(t1)
        nsteps.append(sc.add("scalar", lambda e, t4=t4: e.sqrt(
            g.s_nrm[0:16, t4 * 512:(t4 + 1) * 512], g.pb[6][0:16, 0:512]),
            deps=[sq_steps[-1]]))
    v1 = sc.add("vector", lambda e: e.tensor_scalar_max(
        g.s_nrm[0:16, :], g.s_nrm[0:16, :], 1e-12), deps=[nsteps[-1]])
    v2 = sc.add("vector", lambda e: e.reciprocal(g.s_nrm[0:16, :], g.s_nrm[0:16, :]),
                deps=[v1])
    v3 = sc.add("vector", lambda e: e.tensor_copy(g.s_nrmb[0:16, :], g.s_nrm[0:16, :]),
                deps=[v2])
    rkevs = []
    bnd = sc.add("vector", lambda e: e.memset(
        g.s_rk[:].rearrange("p (c w) -> p c w", w=TPC + 1)[:, :, 0], 0.0))
    for c in range(FC):
        for t4 in range(4):
            j = c * 4 + t4

            def mk_e(eng, c=c, t4=t4, j=j):
                return eng.matmul(
                    g.pb[4 + j % 2][:, 0:512],
                    g.s_cdyn[0:16, c * 128:(c + 1) * 128],
                    g.s_nrmb[0:16, t4 * 512:(t4 + 1) * 512],
                    start=True, stop=True)

            def ev_e(eng, c=c, t4=t4, j=j):
                return eng.tensor_tensor(
                    g.rksl(c, 1 + t4 * 512, 512),
                    g.st(g.s_xbf, c, t4 * 512, 512),
                    g.pb[4 + j % 2][:, 0:512], ALU.mult)
            w = [("dma_cd", p.n("dma_cd"))] if j == 0 else ()
            deps = [v3] if j < 2 else [rkevs[j - 2]]
            im = sc.add("tensor", mk_e, deps=deps, sem_waits=w)
            rkevs.append(sc.add("vector", ev_e, deps=[im, bnd]))
    def shipb(eng):
        src = g.s_rk[:].rearrange("p (c w) -> p c w", w=TPC + 1)[:, :, TPC]
        with p.nc.allow_non_contiguous_dma(reason="8 strided boundary cols"):
            i = eng.dma_start(out=g.cc_rb_in[:, :], in_=src)
        p.inc(i, "dma_cc", 16)
        return i
    sb = sc.add("gpsimd", shipb, deps=rkevs[-2:])

    def ccb(eng):
        eng.wait_ge(p.sem("dma_cc"), p.n("dma_cc"))
        i = eng.collective_compute(
            "AllGather", ALU.bypass,
            replica_groups=[[0, 1], [2, 3], [4, 5], [6, 7]],
            ins=[g.cc_rb_in.ap().opt()], outs=[g.cc_rb_out.ap().opt()])
        p.inc(i, "ccb_flag", 1)
        return i
    cb = sc.add("gpsimd", ccb, deps=[sb])

    def ldb(eng):
        eng.wait_ge(p.sem("ccb_flag"), 1)
        i = eng.dma_start(out=g.s_rbnd[:], in_=g.cc_rb_out[0:128, :])
        p.inc(i, "rbnd_flag", 16)
        return i
    lb = sc.add("gpsimd", ldb, deps=[cb])
    gb = sc.add("vector", lambda e: e.tensor_scalar_mul(
        g.s_rbnd[:], g.s_rbnd[:], g.s_sgt[:, 0:1]),
        deps=[lb], sem_waits=[("rbnd_flag", 16)])
    BNDRY = sc.add("vector", lambda e: e.tensor_copy(
        g.s_rk[:].rearrange("p (c w) -> p c w", w=TPC + 1)[:, :, 0], g.s_rbnd[:]),
        deps=[gb])
    dumpb = None
    if os.environ.get("DBG_SER"):
        def dumps(gp):
            p.dma(gp, g.d_dbg_rk.ap(), g.s_rk[:], "dma_dbg")
            p.dma(gp, g.d_dbg_btk.ap(), g.s_btk[:], "dma_dbg")
            p.dma(gp, g.d_dbg_vtk.ap(), g.s_vtk1[:], "dma_dbg")
            return p.dma(gp, g.d_dbg_nrm.ap(), g.s_nrm[0:16, :], "dma_dbg")
        dumpb = sc.add("gpsimd", dumps,
                       deps=[BNDRY, rkevs[-1], btk[-1], vevs[-1], v3])
    def ld3(gp):
        for k in range(4):
            p.dma(gp, g.s_woSW[0:64, k * 4096:(k + 1) * 4096],
                  g.d_woSW[0:64, k * 4096:(k + 1) * 4096], "dma_wo")
    w3 = sc.add("gpsimd", ld3,
                deps=[rkevs[-1], v3] + ([dumpb] if dumpb is not None else []),
                sem_waits=([("dma_dbg", 64)] if dumpb is not None else []))
    sc.emit(blk)

    if substage == "prod":
        return
    emit_solves(p, blk, env)
    if substage == "solve":
        return
    emit_pass2(p, blk, env, tapd)
    if substage == "pass2":
        return
    emit_ln(p, blk, env, dep_fence=p.fence(("vector", "tensor")))


def emit_solves(p, blk, env):
    g = NS(env)
    sc = Sched(p)
    PROD = p.fence(("vector", "scalar"))
    z0 = sc.add("vector", lambda e: e.memset(g.s_Spair[:], 0.0), sem_waits=PROD)
    zb = sc.add("vector", lambda e: e.memset(g.s_Sb[:], 0.0), deps=[z0])
    LADDER = [None] * 7

    prev_pair = {}
    dma_chunks = []
    p1_steps = {}
    for ci in range(NCHUNK):
        head_final = []
        for h in range(H):
            P = (ci * H + h) % 2
            ro = 64 * (h % 2)
            hc = h // 2
            rkK = g.s_rk[ro:ro + 64,
                         hc * (TPC + 1) + ci * 128: hc * (TPC + 1) + ci * 128 + 128]
            rkR = g.s_rk[ro:ro + 64,
                         hc * (TPC + 1) + 1 + ci * 128: hc * (TPC + 1) + 1 + ci * 128 + 128]
            bcol = g.s_btk[:, ci * 16 + h: ci * 16 + h + 1]
            idn = g.s_ident if ro == 0 else g.s_swap
            FGO = P * 12 * 128
            fgs = lambda k: g.s_fg[:, FGO + k * 128: FGO + (k + 1) * 128]
            Xs = lambda k: g.s_X[:, (P * 2 + k) * 128: (P * 2 + k + 1) * 128]
            ktks = g.s_ktk[:, P * 64:(P + 1) * 64]
            yts = g.s_yt[:, P * 128:(P + 1) * 128]
            yxs = g.s_yxt[:, P * 128:(P + 1) * 128]
            pTs = lambda k: g.pT[:, (P * 4 + k) * 128: (P * 4 + k) * 128 + 128]
            dep_reuse = [prev_pair[k] for k in prev_pair] if prev_pair else []

            t_gen = sc.add("tensor", lambda e, rkK=rkK, ro=ro: e.matmul(
                g.pb[0][:, 0:128], rkK, rkK, start=True, stop=True,
                tile_position=(ro, 0)),
                deps=(dep_reuse if True else []),
                sem_waits=([("s_vector", p.n("s_vector"))] if (ci == 0 and h == 0) else ()))
            v_f0 = sc.add("vector", lambda e, bcol=bcol, fgs=fgs: e.scalar_tensor_tensor(
                fgs(0), g.pb[0][:, 0:128], bcol, g.s_masksl[:], ALU.mult, ALU.mult),
                deps=[t_gen])
            t_g0 = sc.add("tensor", lambda e, fgs=fgs, pTs=pTs: e.transpose(
                pTs(0), fgs(0), g.s_ident[:]), deps=[v_f0])
            v_g0 = sc.add("vector", lambda e, fgs=fgs, pTs=pTs: e.tensor_copy(
                fgs(1), pTs(0)), deps=[t_g0])
            ev_prev = v_g0
            rung_evs = []
            for r in range(1, 7):
                def mk_r(eng, r=r, fgs=fgs):
                    i = eng.matmul(g.pb[1][:, ((r - 1) % 2) * 256: ((r - 1) % 2) * 256 + 128],
                                   fgs(2 * r - 2), fgs(2 * r - 1), start=True, stop=True)
                    if r <= 5:
                        i = eng.matmul(g.pb[1][:, ((r - 1) % 2) * 256 + 128: ((r - 1) % 2) * 256 + 256],
                                       fgs(2 * r - 1), fgs(2 * r - 2), start=True, stop=True)
                    return i

                def ev_r(eng, r=r, fgs=fgs):
                    eng.tensor_copy(fgs(2 * r + 1),
                                    g.pb[1][:, ((r - 1) % 2) * 256: ((r - 1) % 2) * 256 + 128])
                    if r <= 5:
                        return eng.tensor_copy(
                            fgs(2 * r),
                            g.pb[1][:, ((r - 1) % 2) * 256 + 128: ((r - 1) % 2) * 256 + 256])
                    return None
                tm = sc.add("tensor", mk_r, deps=[ev_prev] + ([rung_evs[r - 3]] if r >= 3 else []))
                ev_prev = sc.add("vector", ev_r, deps=[tm])
                rung_evs.append(ev_prev)
            t_kt = sc.add("tensor", lambda e, rkK=rkK, pTs=pTs, idn=idn, ro=ro: e.transpose(
                pTs(1)[:, 0:64], rkK, idn[ro:ro + 64, 0:64]), deps=[v_g0])
            v_kt = sc.add("vector", lambda e, ktks=ktks, pTs=pTs: e.tensor_copy(
                ktks, pTs(1)[:, 0:64]), deps=[t_kt])
            v_x0a = sc.add("vector", lambda e, Xs=Xs, ktks=ktks, bcol=bcol: e.tensor_scalar_mul(
                Xs(0)[:, 0:64], ktks, bcol), deps=[v_kt])
            v_x0b = sc.add("vector", lambda e, Xs=Xs, ci=ci, h=h, bcol=bcol: e.tensor_scalar_mul(
                Xs(0)[:, 64:128], g.s_vtk1[:, ci * D + h * 64: ci * D + (h + 1) * 64], bcol),
                deps=[v_x0a])
            xcur = 0
            ev_app = v_x0b
            app_evs = []
            for j in range(7):
                def mk_a(eng, j=j, xcur=xcur, fgs=fgs, Xs=Xs):
                    return eng.matmul(g.pb[2][:, (j % 2) * 128: (j % 2) * 128 + 128],
                                      fgs(2 * j + 1), Xs(xcur), start=True, stop=True)
                gdep = v_g0 if j == 0 else rung_evs[j - 1]
                tm = sc.add("tensor", mk_a, deps=[ev_app, gdep])
                odd = (h % 2 == 1)

                def ev_a(eng, j=j, xcur=xcur, Xs=Xs):
                    src = g.pb[2][:, (j % 2) * 128: (j % 2) * 128 + 128]
                    op = ALU.subtract if j == 0 else ALU.add
                    return eng.tensor_tensor(Xs(1 - xcur), Xs(xcur), src, op)
                ev_app = sc.add("vector", ev_a, deps=[tm])
                app_evs.append(ev_app)
                xcur = 1 - xcur
            XF = xcur
            t_yt = sc.add("tensor", lambda e, rkK=rkK, rkR=rkR, ro=ro: e.matmul(
                g.pb[3][:, 0:128], rkK, rkR, start=True, stop=True,
                tile_position=(ro, 0)), deps=[v_f0])
            v_yt = sc.add("vector", lambda e, yts=yts: e.tensor_tensor(
                yts, g.pb[3][:, 0:128], g.s_masksui[:], ALU.mult), deps=[t_yt])
            t_yx = sc.add("tensor", lambda e, yts=yts, Xs=Xs, XF=XF: e.matmul(
                g.pb[3][:, 128:256], yts, Xs(XF), start=True, stop=True),
                deps=[v_yt, ev_app])
            v_yx = sc.add("vector", lambda e, yxs=yxs: e.tensor_copy(
                yxs, g.pb[3][:, 128:256]), deps=[t_yx])
            t_yxt = sc.add("tensor", lambda e, yxs=yxs, pTs=pTs, ro=ro: e.transpose(
                pTs(2)[ro:ro + 64, 0:128], yxs[:, 0:64], g.s_ident[:],
                tile_position=(0, ro)), deps=[v_yx])
            t_yxt2 = sc.add("tensor", lambda e, yxs=yxs, pTs=pTs, ro=ro: e.transpose(
                pTs(3)[ro:ro + 64, 0:128], yxs[:, 64:128], g.s_ident[:],
                tile_position=(0, ro)), deps=[v_yx])
            odd = (h % 2 == 1)

            def ev_yxt(eng, pTs=pTs, h=h, ro=ro, hc=hc):
                eng.tensor_scalar_mul(
                    g.s_stgW[ro:ro + 64, hc * 128:(hc + 1) * 128],
                    pTs(2)[ro:ro + 64, 0:128], -1.0)
                return eng.tensor_copy(
                    g.s_stgO[ro:ro + 64, hc * 128:(hc + 1) * 128],
                    pTs(3)[ro:ro + 64, 0:128])
            v_yxt = sc.add("vector", ev_yxt, deps=[t_yxt, t_yxt2],
                           sem_waits=([("dma_ser", 64 * ci)] if (h == 0 and ci > 0) else []))
            t_kx = sc.add("tensor", lambda e, ktks=ktks, Xs=Xs, XF=XF, ro=ro: e.matmul(
                g.pb[4][ro:ro + 64, 0:128], ktks, Xs(XF), start=True, stop=True,
                tile_position=(0, ro)), deps=[ev_app, v_kt])
            wlo, qlo = 0, 64

            def ev_kx2(eng, ro=ro, wlo=wlo, qlo=qlo, hc=hc, ktks=ktks):
                eng.tensor_copy(ktks[ro:ro + 64, 0:64],
                                g.pb[4][ro:ro + 64, wlo:wlo + 64])
                return eng.tensor_copy(
                    g.s_stgQ[ro:ro + 64, hc * 64:(hc + 1) * 64],
                    g.pb[4][ro:ro + 64, qlo:qlo + 64])
            stg_w = [("dma_ser", 64 * ci)] if (h == 0 and ci > 0) else []
            v_kx = sc.add("vector", ev_kx2, deps=[t_kx], sem_waits=stg_w)
            t_pt = sc.add("tensor", lambda e, ktks=ktks, pTs=pTs, ro=ro: e.transpose(
                pTs(3)[ro:ro + 64, 0:64],
                ktks[ro:ro + 64, 0:64],
                g.s_ident[ro:ro + 64, ro:ro + 64] if ro else g.s_ident[0:64, 0:64],
                tile_position=(ro, ro)), deps=[v_kx])
            v_pt = sc.add("vector", lambda e, pTs=pTs, ro=ro, hc=hc: e.tensor_scalar_mul(
                g.s_stgP[ro:ro + 64, hc * 64:(hc + 1) * 64], pTs(3)[ro:ro + 64, 0:64], -1.0),
                deps=[t_pt])
            head_final.append((v_yxt, v_pt, v_kx))
            prev_pair = dict(a=t_gen, b=ev_app, c=v_yxt, d=v_pt)
            prev_pair["e"] = t_pt
        chain = []
        for h in range(H):
            ro = 64 * (h % 2)
            hc = h // 2

            def mk_s(eng, ro=ro, hc=hc):
                return eng.matmul(g.pb[5][ro:ro + 64, hc * 64:(hc + 1) * 64],
                                  g.s_stgP[ro:ro + 64, hc * 64:(hc + 1) * 64],
                                  g.s_Sb[ro:ro + 64, hc * 64:(hc + 1) * 64],
                                  start=True, stop=True, tile_position=(ro, ro))
            deps = [head_final[h][1]] + ([z0, zb] if ci == 0 else []) + ([chain[-1][1]] if chain else [])
            tm = sc.add("tensor", mk_s, deps=deps)

            ss_ = g.s_Spair[ro:ro + 64, hc * 64:(hc + 1) * 64]
            e1 = sc.add("vector", lambda e, ss=ss_, ro=ro, hc=hc: e.tensor_tensor(
                ss, ss, g.pb[5][ro:ro + 64, hc * 64:(hc + 1) * 64], ALU.add), deps=[tm])
            e2 = sc.add("vector", lambda e, ss=ss_, ro=ro, hc=hc: e.tensor_tensor(
                ss, ss, g.s_stgQ[ro:ro + 64, hc * 64:(hc + 1) * 64], ALU.add), deps=[e1])
            ev = sc.add("vector", lambda e, ss=ss_, ro=ro, hc=hc: e.tensor_copy(
                g.s_Sb[ro:ro + 64, hc * 64:(hc + 1) * 64], ss), deps=[e2])
            chain.append((tm, ev))
        def ship(eng, ci=ci):
            p.dma(eng, g.d_serP[ci], g.s_stgP[:], "dma_ser")
            p.dma(eng, g.d_serQ[ci], g.s_stgQ[:], "dma_ser")
            p.dma(eng, g.d_serW[ci], g.s_stgW[:], "dma_ser")
            i = p.dma(eng, g.d_serO[ci], g.s_stgO[:], "dma_ser")
            return i
        dd = sc.add("gpsimd", ship,
                    deps=[head_final[-1][0], head_final[-1][1], chain[-1][1]])
        dma_chunks.append(dd)
    sc.emit(blk)


def emit_pass2(p, blk, env, tapd):
    import os
    P2CUT = int(os.environ.get('P2CUT', '16'))
    P2SKIP = os.environ.get('P2SKIP', '')
    g = NS(env)
    sc = Sched(p)
    SOLVED = p.fence(("vector", "tensor"))

    def ship_s(eng):
        for w in SOLVED:
            eng.wait_ge(p.sem(w[0]), w[1])
        i = eng.dma_start(out=g.cc_s_in[:, :], in_=g.s_Spair[:])
        p.inc(i, "dma_cc", 16)
        return i
    s1 = sc.add("gpsimd", ship_s, sem_waits=[("dma_ser", 64 * NCHUNK)])

    def cc_s(eng):
        eng.wait_ge(p.sem("dma_cc"), p.n("dma_cc"))
        i = eng.collective_compute(
            "AllGather", ALU.bypass,
            replica_groups=[[0, 1], [2, 3], [4, 5], [6, 7]],
            ins=[g.cc_s_in.ap().opt()], outs=[g.cc_s_out.ap().opt()])
        p.inc(i, "ccs_flag", 1)
        return i
    s2 = sc.add("gpsimd", cc_s, deps=[s1])

    def ld_s(eng):
        eng.wait_ge(p.sem("ccs_flag"), 1)
        i = eng.dma_start(out=g.s_Sin[:], in_=g.cc_s_out[0:128, :])
        p.inc(i, "sin_flag", 16)
        return i
    s3 = sc.add("gpsimd", ld_s, deps=[s2])
    v0 = sc.add("vector", lambda e: e.tensor_scalar_mul(
        g.s_Spair[:], g.s_Sin[:], g.s_sgt[:, 0:1]),
        deps=[s3], sem_waits=[("sin_flag", 16)])
    vb = sc.add("vector", lambda e: e.tensor_copy(g.s_Sb[:], g.s_Spair[:]), deps=[v0])

    TF = p.fence(("tensor",))

    def ld_selh(eng):
        for w in TF:
            eng.wait_ge(p.sem(w[0]), w[1])
        i = eng.dma_start(out=g.s_cdyn[:, 0:H * 64], in_=g.d_cselh)
        p.inc(i, "dma_cd2", 16)
        return i
    sc.add("sync", ld_selh)

    dma_in = []
    cons_done = {0: [vb], 1: [vb]}
    last_och_cons = None
    for ci in range(min(NCHUNK, P2CUT)):
        par = ci % 2

        def ld(eng, ci=ci, par=par):
            p.dma(eng, g.s_p2P[par][:], g.d_serP[ci], f"dma_p2{par}")
            p.dma(eng, g.s_p2Q[par][:], g.d_serQ[ci], f"dma_p2{par}")
            p.dma(eng, g.s_p2W[par][:], g.d_serW[ci], f"dma_p2{par}")
            i = p.dma(eng, g.s_p2O[par][:], g.d_serO[ci], f"dma_p2{par}")
            return i
        dd = sc.add("gpsimd", ld, deps=list(cons_done[par]),
                    sem_waits=([("dma_ser", 64 * NCHUNK)] if ci < 2 else ()))
        dma_in.append(dd)
        IN_RDY = [(f"dma_p2{par}", 64 * (ci // 2 + 1))]

        gexp_evs = []
        for hg in range(4):
            def mk_g(eng, ci=ci, hg=hg):
                for k in range(4):
                    h = hg * 4 + k
                    i = eng.matmul(g.pb[0][0:64, k * 128:(k + 1) * 128],
                                   g.s_cdyn[0:16, h * 64:(h + 1) * 64],
                                   g.s_grow[0:16, ci * 128:(ci + 1) * 128],
                                   start=True, stop=True)
                return i

            def ev_g(eng, hg=hg):
                return eng.tensor_copy(g.s_gexp[0:64, hg * 512:(hg + 1) * 512],
                                       g.pb[0][0:64, 0:512])
            w = [("dma_cd2", 16)] if (ci == 0 and hg == 0) else []
            deps = [gexp_evs[-1]] if gexp_evs else ([last_och_cons] if last_och_cons else [])
            im = sc.add("tensor", mk_g, deps=deps + ([vb] if ci == 0 else []), sem_waits=w)
            gexp_evs.append(sc.add("vector", ev_g, deps=[im]))

        oevs = []
        supd = []
        for h in range(H):
            ro = 64 * (h % 2)
            hc = h // 2
            rkR = g.s_rk[ro:ro + 64,
                         hc * (TPC + 1) + 1 + ci * 128: hc * (TPC + 1) + 1 + ci * 128 + 128]
            Ssl = g.s_Spair[ro:ro + 64, hc * 64:(hc + 1) * 64]
            Sbl = g.s_Sb[ro:ro + 64, hc * 64:(hc + 1) * 64]
            par2 = par
            rOloc = 64 - ro

            def mk_o(eng, h=h, ro=ro, hc=hc, rkR=rkR, Sbl=Sbl, par2=par2):
                slot = g.pb[1 + 2 * (h % 2)][0:64, ((h // 2) % 4) * 128:((h // 2) % 4) * 128 + 128]
                eng.matmul(slot, Sbl, rkR, start=True, stop=False,
                           tile_position=(ro, 0))
                eng.matmul(slot, Sbl,
                           g.s_p2W[par2][ro:ro + 64, hc * 128:(hc + 1) * 128],
                           start=False, stop=False, tile_position=(ro, 0))
                lhs_i = g.s_ident[ro:ro + 64, ro:ro + 64] if ro else g.s_ident[0:64, 0:64]
                return eng.matmul(slot, lhs_i,
                                  g.s_p2O[par2][ro:ro + 64, hc * 128:(hc + 1) * 128],
                                  start=False, stop=True, tile_position=(ro, 0))
            deps = [gexp_evs[h // 4]] + ([oevs[h - 2]] if h >= 2 else ([last_och_cons] if last_och_cons else []))
            deps += [supd[-1][0]] if supd else []
            if "o" in P2SKIP:
                tm = sc.add("tensor", lambda e: None, deps=deps,
                            sem_waits=(IN_RDY if h == 0 else []))
            else:
                tm = sc.add("tensor", mk_o, deps=deps,
                            sem_waits=(IN_RDY if h == 0 else []))

            def ev_o(eng, h=h):
                return eng.tensor_tensor(
                    g.s_och[0:64, h * 128:(h + 1) * 128],
                    g.pb[1 + 2 * (h % 2)][0:64, ((h // 2) % 4) * 128:((h // 2) % 4) * 128 + 128],
                    g.s_gexp[0:64, h * 128:(h + 1) * 128], ALU.mult)
            oevs.append(sc.add("vector", (lambda e: None) if "o" in P2SKIP else ev_o, deps=[tm]))

            def mk_su(eng, ro=ro, hc=hc, Sbl=Sbl, par2=par2):
                return eng.matmul(g.pb[5][ro:ro + 64, hc * 64:(hc + 1) * 64],
                                  g.s_p2P[par2][ro:ro + 64, hc * 64:(hc + 1) * 64],
                                  Sbl, start=True, stop=True, tile_position=(ro, ro))
            tsu = sc.add("tensor", (lambda e: None) if "s" in P2SKIP else mk_su,
                         deps=[tm] + ([supd[-1][1]] if supd else []))

            u1 = sc.add("vector", lambda e, Ssl=Ssl, ro=ro, hc=hc: e.tensor_tensor(
                Ssl, Ssl, g.pb[5][ro:ro + 64, hc * 64:(hc + 1) * 64], ALU.add), deps=[tsu])
            u2 = sc.add("vector", lambda e, Ssl=Ssl, ro=ro, hc=hc, par2=par2: e.tensor_tensor(
                Ssl, Ssl, g.s_p2Q[par2][ro:ro + 64, hc * 64:(hc + 1) * 64], ALU.add), deps=[u1])
            vsu = sc.add("vector", lambda e, Ssl=Ssl, Sbl=Sbl: e.tensor_copy(
                Sbl, Ssl), deps=[u2])
            supd.append((tsu, vsu))
        cons_done[par] = [supd[-1][1], oevs[-1]]

        op_evs = []
        for jc in range(FC):
            def mk_op(eng, ci=ci, jc=jc):
                for h in range(H):
                    i = eng.matmul(
                        g.pb[2][:, (jc % 4) * 128:(jc % 4) * 128 + 128],
                        g.s_woSW[0:64, h * D + jc * 128: h * D + (jc + 1) * 128],
                        g.s_och[0:64, h * 128:(h + 1) * 128],
                        start=(h == 0), stop=(h == H - 1))
                return i
            w = [("dma_wo", p.n("dma_wo"))] if (ci == 0 and jc == 0) else []
            deps = [oevs[-1]] + ([op_evs[jc - 1]] if jc >= 1 else [])
            im = sc.add("tensor", (lambda e: None) if "p" in P2SKIP else mk_op,
                        deps=deps, sem_waits=w)

            def ev_op(eng, ci=ci, jc=jc):
                dst = g.st(g.s_xres, jc, ci * 128, 128)
                return eng.tensor_tensor(
                    dst, dst, g.pb[2][:, (jc % 4) * 128:(jc % 4) * 128 + 128], ALU.add)
            op_evs.append(sc.add("vector", (lambda e: None) if "p" in P2SKIP else ev_op, deps=[im]))
        last_och_cons = op_evs[-1]
    sc.emit(blk)
    if tapd:
        pass


def emit_moe(p, blk, env):
    g = NS(env)
    nc = p.nc
    sc = Sched(p)
    FLN3 = p.fence(("scalar", "vector"))
    FTEN = p.fence(("tensor",))

    def ld_r(eng):
        for w in FLN3 + FTEN:
            eng.wait_ge(p.sem(w[0]), w[1])
        for c in range(FC):
            p.dma(eng, g.s_wgbt[:, c * 32: c * 32 + 8],
                  g.d_rwT[c * 128:(c + 1) * 128, :], "dma_rw")
        p.dma(eng, g.s_moew[0][:].rearrange("p (c d) -> p c d", d=D),
              g.d_w1[0].rearrange("(c p) d -> p c d", p=128), "dma_me0")
        i = p.dma(eng, g.s_moew[2][:].rearrange("p (c d) -> p c d", d=D),
                  g.d_w2[0].rearrange("(c p) d -> p c d", p=128), "dma_me0")
        return i
    sc.add("gpsimd", ld_r)

    def ld_inde(eng):
        i = eng.dma_start(out=g.s_cdyn[0:8, 0:8 * 128], in_=g.d_cinde)
        p.inc(i, "dma_cd3", 16)
        return i
    sc.add("sync", ld_inde,
           sem_waits=FTEN + [("dma_cd2", p.n("dma_cd2"))])

    revs = []
    for t4 in range(4):
        def mk_r(eng, t4=t4):
            bank = g.pb[4 + t4 % 2]
            for c in range(FC):
                i = eng.matmul(bank[0:8, 0:512],
                               g.s_wgbt[:, c * 32: c * 32 + 8],
                               g.st(g.s_xbf, c, t4 * 512, 512),
                               start=(c == 0), stop=(c == FC - 1))
            return i
        w = [("dma_rw", 128)] + FLN3 if t4 == 0 else []
        im = sc.add("tensor", mk_r, deps=([revs[t4 - 2]] if t4 >= 2 else []),
                    sem_waits=w)
        revs.append(sc.add("vector", lambda e, t4=t4: e.tensor_copy(
            g.s_wgtb[0:8, t4 * 512:(t4 + 1) * 512], g.pb[4 + t4 % 2][0:8, 0:512]),
            deps=[im]))
    mz = sc.add("vector", lambda e: e.memset(g.s_mp[:, 8:24], 1e30))
    wtr = [mz]
    for tt in range(16):
        def mk_t(eng, tt=tt):
            return eng.transpose(g.pT[0:128, (tt % 4) * 8:(tt % 4) * 8 + 8],
                                 g.s_wgtb[0:8, tt * 128:(tt + 1) * 128],
                                 g.s_ident[0:8, 0:8])
        deps = [revs[-1]] if tt == 0 else [wtr[-1]]
        tr = sc.add("tensor", mk_t, deps=deps)
        v = sc.add("vector", lambda e, tt=tt: e.tensor_copy(
            g.s_rtk[:, 0:8], g.pT[0:128, (tt % 4) * 8:(tt % 4) * 8 + 8]), deps=[tr])
        v = sc.add("vector", lambda e: e.reduce_max(
            g.s_mp[:, 8:9], g.s_rtk[:, 0:8], AX.X), deps=[v] + ([mz] if tt == 0 else []))
        v = sc.add("vector", lambda e: e.tensor_copy(
            g.s_mp[:, 0:1], g.s_mp[:, 8:9]), deps=[v])
        v = sc.add("vector", lambda e: e.match_replace(
            out=g.s_rtk[:, 8:16], in_to_replace=g.s_mp[:, 8:16],
            in_values=g.s_rtk[:, 0:8], imm_value=-1e30), deps=[v])
        v = sc.add("vector", lambda e: e.reduce_max(
            g.s_mp[:, 16:17], g.s_rtk[:, 8:16], AX.X), deps=[v])
        v = sc.add("vector", lambda e: e.tensor_copy(
            g.s_mp[:, 2:3], g.s_mp[:, 16:17]), deps=[v])
        v = sc.add("vector", lambda e: e.match_replace(
            out=g.s_rtk[:, 16:24], in_to_replace=g.s_mp[:, 16:24],
            in_values=g.s_rtk[:, 8:16], imm_value=-1e30), deps=[v])
        v = sc.add("vector", lambda e: e.tensor_sub(
            g.s_mp[:, 4:5], g.s_mp[:, 0:1], g.s_mp[:, 2:3]), deps=[v])
        v = sc.add("scalar", lambda e: e.activation(
            g.s_mp[:, 5:6], g.s_mp[:, 4:5], AF.Sigmoid), deps=[v])
        v = sc.add("vector", lambda e: e.tensor_scalar(
            g.s_mp[:, 6:7], g.s_mp[:, 5:6], -1.0, 1.0, ALU.mult, ALU.add), deps=[v])
        v = sc.add("vector", lambda e: e.tensor_tensor(
            g.s_rtk[:, 0:8], g.s_rtk[:, 0:8], g.s_rtk[:, 8:16], ALU.not_equal), deps=[v])
        v = sc.add("vector", lambda e: e.tensor_tensor(
            g.s_rtk[:, 8:16], g.s_rtk[:, 8:16], g.s_rtk[:, 16:24], ALU.not_equal), deps=[v])
        v = sc.add("vector", lambda e: e.tensor_scalar_mul(
            g.s_rtk[:, 0:8], g.s_rtk[:, 0:8], g.s_mp[:, 5:6]), deps=[v])
        v = sc.add("vector", lambda e: e.tensor_scalar_mul(
            g.s_rtk[:, 8:16], g.s_rtk[:, 8:16], g.s_mp[:, 6:7]), deps=[v])
        v = sc.add("vector", lambda e: e.tensor_add(
            g.s_rtk[:, 16:24], g.s_rtk[:, 0:8], g.s_rtk[:, 8:16]), deps=[v])
        v = sc.add("vector", lambda e: e.tensor_copy(
            g.s_wgtk[:, 0:8], g.s_rtk[:, 16:24]), deps=[v])
        tb = sc.add("tensor", lambda e, tt=tt: e.transpose(
            g.pT[0:8, 512 + (tt % 4) * 128: 512 + (tt % 4) * 128 + 128],
            g.s_wgtk[:, 0:8], g.s_ident[:]), deps=[v])
        v = sc.add("vector", lambda e, tt=tt: e.tensor_copy(
            g.s_wgtb[0:8, tt * 128:(tt + 1) * 128],
            g.pT[0:8, 512 + (tt % 4) * 128: 512 + (tt % 4) * 128 + 128]), deps=[tb])
        wtr.append(v)

    prev = dict(h=[], wexp=[], evac=[])
    for e_i in range(8):
        me = e_i % 2
        if e_i < 7:
            def pre(eng, e_i=e_i):
                nx = (e_i + 1) % 2
                p.dma(eng, g.s_moew[nx][:].rearrange("p (c d) -> p c d", d=D),
                      g.d_w1[e_i + 1].rearrange("(c p) d -> p c d", p=128),
                      f"dma_me{nx}")
                i = p.dma(eng, g.s_moew[2 + nx][:].rearrange("p (c d) -> p c d", d=D),
                          g.d_w2[e_i + 1].rearrange("(c p) d -> p c d", p=128),
                          f"dma_me{nx}")
                return i
            sc.add("gpsimd", pre,
                   deps=list(prev["evac"]) + list(prev["h"]) +
                        ([revs[-1]] if e_i == 0 else []),
                   sem_waits=(FTEN if e_i == 0 else []))
        W_RDY = [(f"dma_me{me}", 32 * (e_i // 2 + 1))]
        for t4 in range(4):
            def mk_w(eng, e_i=e_i, t4=t4):
                return eng.matmul(g.pb[4][:, 0:512],
                                  g.s_cdyn[0:8, e_i * 128:(e_i + 1) * 128],
                                  g.s_wgtb[0:8, t4 * 512:(t4 + 1) * 512],
                                  start=True, stop=True)
            wdeps = ([wtr[-1]] if (e_i == 0 and t4 == 0) else []) + prev["wexp"]
            wsw = [("dma_cd3", 16)] if (e_i == 0 and t4 == 0) else []
            imw = sc.add("tensor", mk_w, deps=wdeps, sem_waits=wsw)
            evw = sc.add("vector", lambda e: e.tensor_copy(
                g.s_wexp[:, 0:512], g.pb[4][:, 0:512]), deps=[imw])
            hsteps = []
            wave_last = []
            for wv in range(4):
                for k in range(2):
                    dc = wv * 2 + k

                    def mk_h(eng, dc=dc, k=k, t4=t4, me=me):
                        bank = g.pb[5 + k]
                        for c in range(FC):
                            i = eng.matmul(
                                bank[:, 0:512],
                                g.s_moew[me][:, c * D + dc * 128: c * D + (dc + 1) * 128],
                                g.st(g.s_xbf, c, t4 * 512, 512),
                                start=(c == 0), stop=(c == FC - 1))
                        return i
                    hdeps = []
                    if wv == 0 and k == 0:
                        hdeps = [evw] + prev["evac"]
                    elif len(wave_last) >= 2:
                        hdeps = [wave_last[-2]]
                    imh = sc.add("tensor", mk_h, deps=hdeps,
                                 sem_waits=(W_RDY if (t4 == 0 and dc == 0) else []))

                    def rl(eng, dc=dc, k=k):
                        return eng.activation(
                            g.s_htile[:, dc * 512:(dc + 1) * 512],
                            g.pb[5 + k][:, 0:512], AF.Relu)
                    sr = sc.add("scalar", rl, deps=[imh])

                    q1 = sc.add("vector", lambda e, dc=dc: e.tensor_tensor(
                        g.s_htile[:, dc * 512:(dc + 1) * 512],
                        g.s_htile[:, dc * 512:(dc + 1) * 512],
                        g.s_htile[:, dc * 512:(dc + 1) * 512], ALU.mult), deps=[sr])
                    vs = sc.add("vector", lambda e, dc=dc: e.tensor_tensor(
                        g.s_htile[:, dc * 512:(dc + 1) * 512],
                        g.s_htile[:, dc * 512:(dc + 1) * 512],
                        g.s_wexp[:, 0:512], ALU.mult), deps=[q1])
                    hsteps.append(vs)
                    wave_last.append(sr)
            w2evs = []
            for c in range(FC):
                def mk_o(eng, c=c, me=me):
                    for dc in range(FC):
                        i = eng.matmul(
                            g.pb[c % 4][:, 0:512],
                            g.s_moew[2 + me][:, dc * D + c * 128: dc * D + (c + 1) * 128],
                            g.s_htile[:, dc * 512:(dc + 1) * 512],
                            start=(dc == 0), stop=(dc == FC - 1))
                    return i
                deps = [hsteps[-1]] + ([w2evs[c - 4]] if c >= 4 else [])
                imo = sc.add("tensor", mk_o, deps=deps)

                def ev_o(eng, c=c, t4=t4):
                    dst = g.st(g.s_xres, c, t4 * 512, 512)
                    return eng.tensor_tensor(dst, dst, g.pb[c % 4][:, 0:512], ALU.add)
                w2evs.append(sc.add("vector", ev_o, deps=[imo]))
            prev = dict(h=[hsteps[-1]], wexp=[hsteps[-1]], evac=[w2evs[-1]])
    sc.emit(blk)
    emit_ln(p, blk, env, dep_fence=p.fence(("vector", "tensor")))


def _dft_consts():
    idx = np.arange(D, dtype=np.float64)
    ang = 2.0 * np.pi * np.outer(idx, idx) / D
    fre = np.cos(ang)
    fim = -np.sin(ang)
    gre = np.cos(ang) / D
    gim = np.sin(ang) / D
    return fre, fim, gre, gim


def _host_constants():
    fre, fim, gre, gim = _dft_consts()
    cs = {}
    ident = np.eye(128, dtype=np.float32)
    mask_sl = np.tril(np.ones((128, 128), np.float32), -1)
    mask_sui = np.triu(np.ones((128, 128), np.float32), 0)
    swap = np.zeros((128, 128), np.float32)
    swap[0:64, 64:128] = np.eye(64)
    swap[64:128, 0:64] = np.eye(64)
    cs["c_pack"] = np.concatenate([ident, mask_sl, mask_sui, swap], axis=1).astype(BFNP)
    onesb = np.zeros((128, 132), np.float32)
    for c in range(FC):
        onesb[0:64, c * 16 + 2 * c] = 1.0
        onesb[64:128, c * 16 + 2 * c + 1] = 1.0
    onesb[:, 128] = 1.0
    cs["c_onesb"] = onesb.astype(BFNP)
    cs["c_onesrow"] = np.ones((1, 128), np.float32).astype(BFNP)
    cfl = np.zeros((128, 4), np.float32); cfl[:, 0] = CSHIFT; cfl[:, 1] = 1e-24
    cs["c_fl"] = cfl
    cind = np.zeros((16, FC * 128), np.float32)
    for c in range(FC):
        cind[2 * c, c * 128: c * 128 + 64] = 1.0
        cind[2 * c + 1, c * 128 + 64: c * 128 + 128] = 1.0
    cs["c_ind"] = cind.astype(BFNP)
    cinde = np.zeros((8, 8 * 128), np.float32)
    for e in range(8):
        cinde[e, e * 128:(e + 1) * 128] = 1.0
    cs["c_inde"] = cinde.astype(BFNP)
    cselh = np.zeros((16, H * 64), np.float32)
    for h in range(H):
        cselh[h, h * 64:(h + 1) * 64] = 1.0
    cs["c_selh"] = cselh.astype(BFNP)
    cs["c_gre"] = gre.astype(BFNP)
    cs["c_gimn"] = (-gim).astype(BFNP)
    return cs, fre, fim


_BUILD_CACHE = {}


def kernel(L0_attn_x, L0_attn_Wq, L0_attn_Wk, L0_attn_Wv, L0_ffn_w_freq_real,
           L0_ffn_w_freq_imag, L0_ffn_bias, L1_attn_Wv, L1_attn_Wo, L1_attn_Wg,
           L1_attn_Wb, L1_ffn_router_w, L1_ffn_w1_all, L1_ffn_w2_all,
           _stop_after="all", _taps=()):
    key = (_stop_after, tuple(_taps))
    if key not in _BUILD_CACHE:
        _BUILD_CACHE.clear()
        _BUILD_CACHE[key] = build(stop_after=_stop_after, want_taps=_taps)
    p, env = _BUILD_CACHE[key]

    cs, fre, fim = _host_constants()
    ac = np.ascontiguousarray
    shared = dict(cs)
    shared["wqT"] = ac(L0_attn_Wq.T)
    shared["wkT"] = ac(L0_attn_Wk.T)
    shared["wvT"] = ac(L0_attn_Wv.T)
    shared["wre"] = ac(L0_ffn_w_freq_real)
    shared["wim"] = ac(L0_ffn_w_freq_imag)
    shared["bias_fm"] = ac(L0_ffn_bias.reshape(FC, 128).T)
    shared["wv1T"] = ac(L1_attn_Wv.T)
    woSW = ac(L1_attn_Wo.T.reshape(H, 64, D).transpose(1, 0, 2).reshape(64, H * D))
    shared["woSW"] = woSW
    shared["wgbT"] = ac(np.concatenate([L1_attn_Wg, L1_attn_Wb], axis=0).T)
    shared["rwT"] = ac(L1_ffn_router_w.T)
    shared["w1_all"] = ac(L1_ffn_w1_all)
    shared["w2_all"] = ac(L1_ffn_w2_all)

    in_maps = []
    for core in range(8):
        b, half = core // 2, core % 2
        m = dict(shared)
        xo = ac(L0_attn_x[b, half * TPC:(half + 1) * TPC, :].T)
        xr = ac(L0_attn_x[b, (1 - half) * TPC:(2 - half) * TPC, :].T)
        m["x_fm"] = np.concatenate([xo, xr], axis=1)
        m["s_gate"] = np.full((128, 1), float(half), np.float32)
        r0 = core * 128
        fr = fre[r0:r0 + 128, :]
        fi = fim[r0:r0 + 128, :]
        frT = ac(fr.T).reshape(FC, 128, 128).transpose(1, 0, 2).reshape(128, FC * 128)
        fiT = ac(fi.T).reshape(FC, 128, 128).transpose(1, 0, 2).reshape(128, FC * 128)
        m["c_freT"] = frT.astype(BFNP)
        m["c_fimT"] = fiT.astype(BFNP)
        m["c_fimnT"] = (-fiT).astype(BFNP)
        in_maps.append(m)

    res = run_bass_kernel_spmd(p.nc, in_maps, core_ids=list(range(8)))
    out = np.empty((B, T, D), np.float32)
    for core in range(8):
        b, half = core // 2, core % 2
        out[b, half * TPC:(half + 1) * TPC, :] = res.results[core]["out_fm"].T
    if _taps:
        tap_res = [{k: v for k, v in r.items() if k.startswith("tap_")}
                   for r in res.results]
        return out, tap_res
    return out


# version 02
# speedup vs baseline: 1.0104x; 1.0104x over previous
import os
import numpy as np
import ml_dtypes
import concourse.bass as bass
import concourse.mybir as mybir
from concourse.bass_utils import run_bass_kernel_spmd

F32 = mybir.dt.float32
BF16 = mybir.dt.bfloat16
AF = mybir.ActivationFunctionType
ALU = mybir.AluOpType
AX = mybir.AxisListType
BFNP = ml_dtypes.bfloat16

B, T, D = 4, 4096, 1024
H, DH = 16, 64
TPC = 2048
FC = 8
CH = 128
NCHUNK = TPC // CH
QT = 256
NQ = TPC // QT
NKT = TPC // 128
NF = 1.0 / 32.0
CSHIFT = -64.0
LNEPS = 1e-5

SBASE = 16512
A1 = SBASE
A2 = SBASE + 65568
A3 = SBASE + 131104
A4 = SBASE + 163872
A5 = SBASE + 196640
SOLV = A4

ENGS = ("tensor", "vector", "scalar", "gpsimd", "sync")


class Prog:
    def __init__(self):
        self.nc = bass.Bass("TRN2", target_bir_lowering=False, debug=False)
        self.sems = {}
        self.cnt = {}

    def sem(self, name):
        if name not in self.sems:
            self.sems[name] = self.nc.alloc_semaphore(name)
            self.cnt[name] = 0
        return self.sems[name]

    def inc(self, instr, name, k=1):
        instr.then_inc(self.sem(name), k)
        self.cnt[name] += k
        return self.cnt[name]

    def dma(self, eng, out, in_, name):
        i = eng.dma_start(out=out, in_=in_)
        self.inc(i, name, 16)
        return i

    def n(self, name):
        self.sem(name)
        return self.cnt[name]

    def fence(self, engs=ENGS):
        return [(f"s_{e}", self.n(f"s_{e}")) for e in engs]


class Sched:

    def __init__(self, p):
        self.p = p
        self.steps = []

    def add(self, eng, fn, deps=(), sem_waits=()):
        self.steps.append(dict(eng=eng, fn=fn, deps=list(deps),
                               sem_waits=list(sem_waits), seq=None))
        return len(self.steps) - 1

    def emit(self, blk):
        p = self.p
        counts = {e: 0 for e in ENGS}
        for s in self.steps:
            counts[s["eng"]] += 1
            s["seq"] = counts[s["eng"]]
        base = {e: p.n(f"s_{e}") for e in ENGS}
        for e in ENGS:
            mysteps = [s for s in self.steps if s["eng"] == e]
            if not mysteps:
                continue

            def body(eng, mysteps=mysteps, e=e):
                for s in mysteps:
                    for d in s["deps"]:
                        ds = self.steps[d]
                        eng.wait_ge(p.sem(f"s_{ds['eng']}"),
                                    base[ds["eng"]] + ds["seq"])
                    for (sn, v) in s["sem_waits"]:
                        eng.wait_ge(p.sem(sn), v)
                    instr = s["fn"](eng)
                    nm = type(instr.ins).__name__ if instr is not None else ""
                    if instr is None or "DMA" in nm or "Dma" in nm or "Collective" in nm:
                        instr = eng.nop()
                    p.inc(instr, f"s_{e}", 1)
            getattr(blk, e)(body)


def build(stop_after="all", want_taps=()):
    p = Prog()
    nc = p.nc
    STAGES = ["x1", "x2", "l1prod", "l1solve", "l1pass2", "l1o", "x3", "x4", "all"]
    lim = STAGES.index(stop_after) if stop_after in STAGES else len(STAGES) - 1

    def reach(stg):
        return STAGES.index(stg) <= lim

    dt_i = lambda nm, sh, dt=F32: nc.dram_tensor(nm, sh, dt, kind="ExternalInput").ap()
    d_x = dt_i("x_fm", [D, T])
    d_wqT = dt_i("wqT", [D, D])
    d_wkT = dt_i("wkT", [D, D])
    d_wvT = dt_i("wvT", [D, D])
    d_wre = dt_i("wre", [D, D])
    d_wim = dt_i("wim", [D, D])
    d_bias = dt_i("bias_fm", [128, FC])
    d_wv1T = dt_i("wv1T", [D, D])
    d_woSW = dt_i("woSW", [64, H * D])
    d_wgbT = dt_i("wgbT", [D, 32])
    d_rwT = dt_i("rwT", [D, 8])
    d_w1 = dt_i("w1_all", [8, D, D])
    d_w2 = dt_i("w2_all", [8, D, D])
    d_sgate = dt_i("s_gate", [128, 1])
    d_freT = dt_i("c_freT", [128, FC * 128], BF16)
    d_fimT = dt_i("c_fimT", [128, FC * 128], BF16)
    d_fimnT = dt_i("c_fimnT", [128, FC * 128], BF16)
    d_gre = dt_i("c_gre", [D, D], BF16)
    d_gimn = dt_i("c_gimn", [D, D], BF16)
    d_cpack = dt_i("c_pack", [128, 512], BF16)
    d_cind = dt_i("c_ind", [16, FC * 128], BF16)
    d_cinde = dt_i("c_inde", [8, 8 * 128], BF16)
    d_cselh = dt_i("c_selh", [16, H * 64], BF16)
    d_conesb = dt_i("c_onesb", [128, 132], BF16)
    d_onesrow = dt_i("c_onesrow", [1, 128], BF16)
    d_cfl = dt_i("c_fl", [128, 4])

    d_out = nc.dram_tensor("out_fm", [D, TPC], F32, kind="ExternalOutput").ap()
    taps = {}
    for tn in want_taps:
        if tn == "srow":
            taps[tn] = nc.dram_tensor("tap_srow", [1, TPC], F32,
                                      kind="ExternalOutput").ap()
        elif tn in ("acc", "kfm", "vtk0", "xbff_a", "xbff_b", "qt"):
            taps[tn] = nc.dram_tensor("tap_" + tn, [128, 16384], BF16,
                                      kind="ExternalOutput").ap()
        elif tn == "rbcd":
            taps[tn] = nc.dram_tensor("tap_rbcd", [128, TPC], F32,
                                      kind="ExternalOutput").ap()
        elif tn in ("rbc2", "mbc2"):
            taps[tn] = nc.dram_tensor("tap_" + tn, [128, TPC], F32,
                                      kind="ExternalOutput").ap()
        elif tn == "lnstat":
            taps[tn] = nc.dram_tensor("tap_lnstat", [128, 128], F32,
                                      kind="ExternalOutput").ap()
        elif tn == "lnrow":
            taps[tn] = nc.dram_tensor("tap_lnrow", [1, 2 * TPC], BF16,
                                      kind="ExternalOutput").ap()
        else:
            taps[tn] = nc.dram_tensor("tap_" + tn, [D, TPC],
                                      BF16 if tn in ("att", "l1o") else F32,
                                      kind="ExternalOutput").ap()

    cc_m_in = nc.dram_tensor("cc_m_in", [128, D], BF16)
    cc_m_out = nc.dram_tensor("cc_m_out", [8 * 128, D], BF16)
    cc_rb_in = nc.dram_tensor("cc_rb_in", [128, FC], BF16)
    cc_rb_out = nc.dram_tensor("cc_rb_out", [256, FC], BF16)
    cc_s_in = nc.dram_tensor("cc_s_in", [128, 8 * 64], F32)
    cc_s_out = nc.dram_tensor("cc_s_out", [256, 8 * 64], F32)
    _serk = {"kind": "ExternalOutput"} if os.environ.get("DBG_SER") else {}
    if os.environ.get("DBG_SER"):
        d_dbg_rk = nc.dram_tensor("dbg_rk", [128, FC * (TPC + 1)], BF16, kind="ExternalOutput")
        d_dbg_btk = nc.dram_tensor("dbg_btk", [128, NCHUNK * H], F32, kind="ExternalOutput")
        d_dbg_vtk = nc.dram_tensor("dbg_vtk", [128, NCHUNK * D], BF16, kind="ExternalOutput")
        d_dbg_nrm = nc.dram_tensor("dbg_nrm", [16, TPC], F32, kind="ExternalOutput")
    d_serP = nc.dram_tensor("ser_pt", [NCHUNK, 128, 8 * 64], BF16, **_serk)
    d_serQ = nc.dram_tensor("ser_q", [NCHUNK, 128, 8 * 64], F32, **_serk)
    d_serW = nc.dram_tensor("ser_yw", [NCHUNK, 128, 8 * 128], BF16, **_serk)
    d_serO = nc.dram_tensor("ser_yo", [NCHUNK, 128, 8 * 128], BF16, **_serk)

    at = nc.alloc_sbuf_tensor_at
    s_xbff = at("xbf_full", [128, FC * T], BF16, offset=A1)
    s_Mall = at("M_all", [128, FC * D], BF16, offset=A1)
    s_rk = at("rk", [128, FC * (TPC + 1)], BF16, offset=A1)
    s_wv1t = at("wv1t", [128, FC * D], BF16, offset=A1 + 32800)
    s_wgbt = at("wgbt", [128, FC * 32], BF16, offset=A1 + 49216)
    s_x2sq = at("x2sq", [128, 512], BF16, offset=A1 + 49792)
    s_nrm = at("nrm", [16, TPC], F32, offset=A1 + 50816)
    s_nrmb = at("nrm_bf", [16, TPC], BF16, offset=A1 + 59008)
    s_brow = at("brow", [16, TPC], BF16, offset=A1 + 50816)
    s_woSW = at("woSW", [64, H * D], BF16, offset=A1 + 32800)
    s_moew = [at(f"moew{i}", [128, FC * D], BF16, offset=A1 + i * 16384)
              for i in range(4)]
    s_kfm = at("kfm", [128, FC * TPC], BF16, offset=A2)
    s_vtk0 = at("vtk0", [128, NKT * D], BF16, offset=A2 + 32768)
    s_xres = at("xres", [128, FC * TPC], F32, offset=A2)
    s_fre = at("fre", [128, FC * 128], BF16, offset=A3)
    s_fim = at("fim", [128, FC * 128], BF16, offset=A3 + 2048)
    s_fimn = at("fimn", [128, FC * 128], BF16, offset=A3 + 4096)
    s_areT = at("areT", [128, FC * 128], BF16, offset=A3 + 6144)
    s_aimT = at("aimT", [128, FC * 128], BF16, offset=A3 + 8192)
    s_mmine = at("m_mine", [128, D], BF16, offset=A3 + 10240)
    s_acc = at("acc_att", [128, FC * TPC], BF16, offset=A3)
    s_rbc = at("rbc", [128, TPC], F32, offset=A3)
    s_mbc = at("mbc", [128, TPC], F32, offset=A3 + 8192)
    s_zbf = at("zbf", [128, TPC], BF16, offset=A3 + 16384)
    s_zsq = at("zsq", [128, TPC], BF16, offset=A3 + 20480)
    s_rrow = at("rrow", [1, 2 * TPC], BF16, offset=A3 + 24576)
    s_vtk1 = at("vtk1", [128, NCHUNK * D], BF16, offset=A3)
    s_htile = at("htile", [128, FC * 512], BF16, offset=A3)
    s_wexp = at("wexp", [128, 512], BF16, offset=A3 + 8192)
    s_wgtb = at("wgt_rows", [8, TPC], BF16, offset=A3 + 9216)
    s_rtk = at("rtk", [128, 3 * 8], F32, offset=A3 + 13312)
    s_mp = at("mp", [128, 24], F32, offset=A3 + 13408)
    s_wgtk = at("wgtk", [128, 16], BF16, offset=A3 + 13504)
    s_wslabA = at("wslabA", [128, FC * D], BF16, offset=A4)
    s_wslabB = at("wslabB", [128, FC * D], BF16, offset=A4 + 16384)
    s_qtile = at("qtile", [128, FC * QT], BF16, offset=A4)
    s_rsb = at("recipS_bf", [1, TPC], BF16, offset=A4 + 12288)
    s_recipSbc = at("recipSbc", [128, TPC], F32, offset=A4 + 16384)
    s_ptile = at("ptile", [128, 16 * QT], BF16, offset=A4 + 4096)
    s_xbf = at("xbf", [128, FC * TPC], BF16, offset=A4)
    s_fg = at("fg", [128, 2 * 14 * 128], BF16, offset=SOLV)
    s_X = at("Xbuf", [128, 2 * 2 * 128], BF16, offset=SOLV + 7168)
    s_ktk = at("ktk", [128, 2 * 64], BF16, offset=SOLV + 8192)
    s_kx = at("kx", [64, 2 * 128], F32, offset=SOLV + 8448)
    s_yxt = at("yxt", [128, 2 * 128], BF16, offset=SOLV + 9472)
    s_yt = at("yt", [128, 2 * 128], BF16, offset=SOLV + 9984)
    s_stgP = at("stgP", [128, 8 * 64], BF16, offset=SOLV + 10496)
    s_stgQ = at("stgQ", [128, 8 * 64], F32, offset=SOLV + 11520)
    s_stgW = at("stgW", [128, 8 * 128], BF16, offset=SOLV + 13568)
    s_stgO = at("stgO", [128, 8 * 128], BF16, offset=SOLV + 15616)
    s_och = at("ochunk", [64, H * 128], BF16, offset=SOLV + 17664)
    s_gexp = at("gexp", [64, H * 128], BF16, offset=SOLV + 21760)
    s_p2P = [at(f"p2P{i}", [128, 8 * 64], BF16, offset=SOLV + i * 1024)
             for i in range(2)]
    s_p2Q = [at(f"p2Q{i}", [128, 8 * 64], F32, offset=SOLV + 2048 + i * 2048)
             for i in range(2)]
    s_p2W = [at(f"p2W{i}", [128, 8 * 128], BF16, offset=SOLV + 6144 + i * 2048)
             for i in range(2)]
    s_p2O = [at(f"p2O{i}", [128, 8 * 128], BF16, offset=SOLV + 10240 + i * 2048)
             for i in range(2)]
    s_ident = at("ident", [128, 128], BF16, offset=A5)
    s_masksl = at("mask_sl", [128, 128], BF16, offset=A5 + 256)
    s_masksui = at("mask_sui", [128, 128], BF16, offset=A5 + 512)
    s_swap = at("swapm", [128, 128], BF16, offset=A5 + 768)
    s_onesb = at("onesb", [128, 132], BF16, offset=A5 + 1024)
    s_onesrow = at("onesrow", [1, 128], BF16, offset=A5 + 1312)
    s_cdyn = at("cdyn", [16, 1024], BF16, offset=A5 + 1568)
    DYN = A5 + 3616
    s_srow = at("srow", [1, TPC], F32, offset=DYN)
    s_Spair = at("Spair", [128, 8 * 64], F32, offset=DYN)
    s_Sb = at("Sb", [128, 8 * 64], BF16, offset=DYN + 2048)
    s_Sin = at("Sin", [128, 8 * 64], F32, offset=DYN + 3072)
    s_btk = at("btk", [128, NCHUNK * H], F32, offset=DYN + 5120)
    s_stat = at("stat_tk", [128, 16 * 8], F32, offset=DYN + 6144)
    s_statb = at("statb_tk", [128, 32], BF16, offset=DYN + 6656)
    s_grow = at("g_rows", [16, TPC], BF16, offset=A5 + 11808)
    s_sgt = at("sgt", [128, 1], F32, offset=A5 + 15904)
    s_biasf = at("biasf", [128, FC], F32, offset=A5 + 15936)
    s_rbnd = at("rbnd", [128, FC], BF16, offset=A5 + 15968)
    s_cfl = at("cfl", [128, 4], F32, offset=A5 + 16000)

    pb = [nc.alloc_psum_tensor(f"pb{i}", [128, 512], F32) for i in range(7)]
    pT = nc.alloc_psum_tensor("pTt", [128, 1024], BF16)

    ONEC = s_onesb[:, 128:129]

    def blk1(c):
        return s_onesb[:, c * 16:(c + 1) * 16]

    def xb(c, t0, n):
        return s_xbff[:, c * T + t0: c * T + t0 + n]

    def st(buf, c, t0, n):
        return buf[:, c * TPC + t0: c * TPC + t0 + n]

    def rksl(c, col0, n):
        return s_rk[:, c * (TPC + 1) + col0: c * (TPC + 1) + col0 + n]

    with nc.Block() as blk:
        def p0_sync(sync):
            for (dst, src) in [(s_ident[:], d_cpack[:, 0:128]),
                               (s_masksl[:], d_cpack[:, 128:256]),
                               (s_masksui[:], d_cpack[:, 256:384]),
                               (s_swap[:], d_cpack[:, 384:512]),
                               (s_onesb[:], d_conesb), (s_onesrow[:], d_onesrow),
                               (s_fre[:], d_freT), (s_fim[:], d_fimT),
                               (s_fimn[:], d_fimnT), (s_sgt[:], d_sgate),
                               (s_biasf[:], d_bias), (s_cfl[:], d_cfl)]:
                p.dma(sync, dst, src, "dma_c")
        blk.sync(p0_sync)

        def p0_gp(gp):
            for c in range(FC):
                p.dma(gp, xb(c, 0, T), d_x[c * 128:(c + 1) * 128, :], "dma_x")
            for c in range(FC):
                p.dma(gp, s_kfm[:, c * D:(c + 1) * D],
                      d_wre[c * 128:(c + 1) * 128, :], "dma_w1")
            for c in range(FC):
                p.dma(gp, s_kfm[:, 8192 + c * D: 8192 + (c + 1) * D],
                      d_wim[c * 128:(c + 1) * 128, :], "dma_w1")
            for c in range(FC):
                p.dma(gp, s_vtk0[:, c * D:(c + 1) * D],
                      d_gre[c * 128:(c + 1) * 128, :], "dma_w1")
            for c in range(FC):
                p.dma(gp, s_vtk0[:, 8192 + c * D: 8192 + (c + 1) * D],
                      d_gimn[c * 128:(c + 1) * 128, :], "dma_w1")
            for c in range(FC):
                p.dma(gp, s_wslabA[:, c * D:(c + 1) * D],
                      d_wkT[c * 128:(c + 1) * 128, :], "dma_wk0")
            for c in range(FC):
                p.dma(gp, s_wslabB[:, c * D:(c + 1) * D],
                      d_wqT[c * 128:(c + 1) * 128, :], "dma_wq")
        blk.gpsimd(p0_gp)

        def wre_t(dc, kc):
            return s_kfm[:, dc * D + kc * 128: dc * D + (kc + 1) * 128]

        def wim_t(dc, kc):
            return s_kfm[:, 8192 + dc * D + kc * 128: 8192 + dc * D + (kc + 1) * 128]

        sc = Sched(p)
        W_RDY = [("dma_w1", p.n("dma_w1")), ("dma_c", p.n("dma_c"))]
        evs = []
        for kc in range(FC):
            def mk_a(eng, kc=kc, im=False):
                bank = pb[1] if im else pb[0]
                for dc in range(FC):
                    eng.matmul(bank[:, 0:128],
                               (wim_t if im else wre_t)(dc, kc),
                               s_fre[:, dc * 128:(dc + 1) * 128],
                               start=(dc == 0), stop=False)
                for dc in range(FC):
                    i = eng.matmul(bank[:, 0:128],
                                   (wre_t if im else wim_t)(dc, kc),
                                   (s_fim if im else s_fimn)[:, dc * 128:(dc + 1) * 128],
                                   start=False, stop=(dc == FC - 1))
                return i
            i_re = sc.add("tensor", lambda e, kc=kc: mk_a(e, kc, False),
                          deps=([evs[-1][0]] if evs else []),
                          sem_waits=(W_RDY if kc == 0 else ()))
            e_re = sc.add("vector", lambda e, kc=kc: e.tensor_copy(
                s_areT[:, kc * 128:(kc + 1) * 128], pb[0][:, 0:128]), deps=[i_re])
            i_im = sc.add("tensor", lambda e, kc=kc: mk_a(e, kc, True),
                          deps=([evs[-1][1]] if evs else []))
            e_im = sc.add("vector", lambda e, kc=kc: e.tensor_copy(
                s_aimT[:, kc * 128:(kc + 1) * 128], pb[1][:, 0:128]), deps=[i_im])
            evs.append((e_re, e_im))
        mevs = []
        for half in range(2):
            def mk_m(eng, half=half):
                for kc in range(FC):
                    eng.matmul(pb[2][:, 0:512], s_areT[:, kc * 128:(kc + 1) * 128],
                               s_vtk0[:, kc * D + half * 512: kc * D + half * 512 + 512],
                               start=(kc == 0), stop=False)
                for kc in range(FC):
                    i = eng.matmul(pb[2][:, 0:512], s_aimT[:, kc * 128:(kc + 1) * 128],
                                   s_vtk0[:, 8192 + kc * D + half * 512: 8192 + kc * D + half * 512 + 512],
                                   start=False, stop=(kc == FC - 1))
                return i
            im = sc.add("tensor", mk_m,
                        deps=([evs[-1][0], evs[-1][1]] if half == 0 else [mevs[0]]))
            ev = sc.add("vector", lambda e, half=half: e.tensor_copy(
                s_mmine[:, half * 512: half * 512 + 512], pb[2][:, 0:512]), deps=[im])
            mevs.append(ev)

        def ship_m(eng):
            i = eng.dma_start(out=cc_m_in[:, :], in_=s_mmine[:])
            p.inc(i, "dma_cc", 16)
            return i
        dship = sc.add("gpsimd", ship_m, deps=mevs)

        def do_ccm(eng):
            eng.wait_ge(p.sem("dma_cc"), p.n("dma_cc"))
            i = eng.collective_compute(
                "AllGather", ALU.bypass,
                replica_groups=[[0, 1, 2, 3, 4, 5, 6, 7]],
                ins=[cc_m_in.ap().opt()], outs=[cc_m_out.ap().opt()])
            p.inc(i, "ccm_flag", 1)
            return i
        sc.add("gpsimd", do_ccm, deps=[dship])
        sc.emit(blk)
        CCM_DONE = [("ccm_flag", 1)]

        prev_tail = {}
        for half in range(2):
            sc = Sched(p)
            kv_evs = []
            if half == 1:
                def ld_wk1(eng):
                    for w in prev_tail["attend"] + prev_tail["pb45"]:
                        eng.wait_ge(p.sem(w[0]), w[1])
                    for c in range(FC):
                        i = eng.dma_start(out=s_wslabA[:, c * D:(c + 1) * D],
                                          in_=d_wkT[c * 128:(c + 1) * 128, :])
                        p.inc(i, "dma_wk1", 16)
                    return i
                sc.add("gpsimd", ld_wk1)
            for dc in range(FC):
                for t4 in range(4):
                    j = dc * 4 + t4

                    def mk_k(eng, dc=dc, t4=t4, j=j, half=half):
                        bank = pb[4 + j % 2]
                        for c in range(FC):
                            i = eng.matmul(
                                bank[:, 0:512],
                                s_wslabA[:, c * D + dc * 128: c * D + (dc + 1) * 128],
                                xb(c, half * TPC + t4 * 512, 512),
                                start=(c == 0), stop=(c == FC - 1))
                        return i

                    def ev_k(eng, dc=dc, t4=t4, j=j):
                        return eng.tensor_copy(
                            s_kfm[:, dc * TPC + t4 * 512: dc * TPC + t4 * 512 + 512],
                            pb[4 + j % 2][:, 0:512])
                    w = list(prev_tail.get("pb45", []))
                    if j == 0:
                        w += [("dma_x", 16 * FC), (f"dma_wk{half}", 16 * FC)]
                        if half == 1:
                            w += prev_tail["attend"]
                    im = sc.add("tensor", mk_k,
                                deps=([kv_evs[-2]] if len(kv_evs) >= 2 else []),
                                sem_waits=(w if j <= 1 else ()))
                    kv_evs.append(sc.add("vector", ev_k, deps=[im]))
            k_done = list(kv_evs[-2:])
            def ld_wv(eng, half=half):
                for c in range(FC):
                    i = eng.dma_start(out=s_wslabA[:, c * D:(c + 1) * D],
                                      in_=d_wvT[c * 128:(c + 1) * 128, :])
                    p.inc(i, f"dma_wv{half}", 16)
                return i
            sc.add("gpsimd", ld_wv, deps=k_done)
            for tt in range(NKT):
                for vh in range(2):
                    j = tt * 2 + vh

                    def mk_v(eng, tt=tt, vh=vh, j=j, half=half):
                        bank = pb[4 + j % 2]
                        for c in range(FC):
                            i = eng.matmul(
                                bank[:, 0:512],
                                xb(c, half * TPC + tt * 128, 128),
                                s_wslabA[:, c * D + vh * 512: c * D + vh * 512 + 512],
                                start=(c == 0), stop=(c == FC - 1))
                        return i

                    def ev_v(eng, tt=tt, vh=vh, j=j):
                        return eng.tensor_copy(
                            s_vtk0[:, tt * D + vh * 512: tt * D + vh * 512 + 512],
                            pb[4 + j % 2][:, 0:512])
                    w = [(f"dma_wv{half}", 16 * FC)] if j == 0 else ()
                    im = sc.add("tensor", mk_v, deps=[kv_evs[-2]], sem_waits=w)
                    kv_evs.append(sc.add("vector", ev_v, deps=[im]))
            vk_done = list(kv_evs[-2:])

            prev = None
            for qi in range(NQ):
                qevs = []
                for g in range(4):
                    def mk_q(eng, qi=qi, g=g):
                        bank = pb[4 + g % 2]
                        for k in range(2):
                            dc = g * 2 + k
                            for c in range(FC):
                                i = eng.matmul(
                                    bank[:, k * 256: k * 256 + 256],
                                    s_wslabB[:, c * D + dc * 128: c * D + (dc + 1) * 128],
                                    xb(c, qi * QT, QT),
                                    start=(c == 0), stop=(c == FC - 1))
                        return i

                    def ev_q(eng, qi=qi, g=g):
                        return eng.tensor_copy(
                            s_qtile[:, g * 512:(g + 1) * 512], pb[4 + g % 2][:, 0:512])
                    w = [("dma_wq", 16 * FC)] if (half == 0 and qi == 0 and g == 0) else ()
                    if qi == 0:
                        deps = vk_done if g < 2 else [qevs[g - 2]]
                    else:
                        deps = [prev["exp"][14 + g % 2]] if g < 2 else [qevs[g - 2]]
                    im = sc.add("tensor", mk_q, deps=deps, sem_waits=w)
                    qevs.append(sc.add("vector", ev_q, deps=[im]))
                exp_steps, log_steps = [], []
                for kt in range(NKT):
                    def mk_log(eng, qi=qi, kt=kt):
                        bank = pb[4 + kt % 2]
                        for c in range(FC):
                            i = eng.matmul(
                                bank[:, 0:256],
                                s_kfm[:, c * TPC + kt * 128: c * TPC + (kt + 1) * 128],
                                s_qtile[:, c * QT:(c + 1) * QT],
                                start=(c == 0), stop=(c == FC - 1))
                        return i
                    deps = list(qevs[2:]) if kt < 2 else [exp_steps[kt - 2]]
                    ilog = sc.add("tensor", mk_log, deps=deps)

                    def mk_exp(eng, qi=qi, kt=kt):
                        return eng.activation(
                            s_ptile[:, kt * QT: kt * QT + QT],
                            pb[4 + kt % 2][:, 0:256], AF.Exp,
                            bias=s_cfl[:, 0:1], scale=1.0)
                    deps = [ilog]
                    if kt == 0 and prev:
                        deps += [prev["pv"][-1]]
                    iexp = sc.add("scalar", mk_exp, deps=deps,
                                  sem_waits=([("dma_c", p.n("dma_c"))]
                                             if (half == 0 and qi == 0 and kt == 0) else ()))
                    exp_steps.append(iexp)
                pv_steps = []

                def mk_ssum(eng, qi=qi):
                    for kt in range(NKT):
                        i = eng.matmul(pb[6][0:1, 0:256], ONEC,
                                       s_ptile[:, kt * QT: kt * QT + QT],
                                       start=(kt == 0), stop=(kt == NKT - 1))
                    return i
                pv_steps.append(sc.add("tensor", mk_ssum,
                                       deps=[exp_steps[-1]] +
                                       ([prev["sevac"][4]] if prev else [])))
                for vc in range(FC):
                    def mk_pv(eng, qi=qi, vc=vc):
                        for kt in range(NKT):
                            i = eng.matmul(
                                pb[vc // 2][:, (vc % 2) * 256: (vc % 2) * 256 + 256],
                                s_vtk0[:, kt * D + vc * 128: kt * D + (vc + 1) * 128],
                                s_ptile[:, kt * QT: kt * QT + QT],
                                start=(kt == 0), stop=(kt == NKT - 1))
                        return i
                    deps = [exp_steps[-1]] + ([prev["sevac"][vc // 2]] if prev else [])
                    pv_steps.append(sc.add("tensor", mk_pv, deps=deps))
                sevac = []
                for g in range(4):
                    def ev_acc(eng, qi=qi, g=g, half=half):
                        for k in range(2):
                            vc = g * 2 + k
                            dst = s_acc[:, vc * TPC + qi * QT: vc * TPC + qi * QT + QT]
                            src = pb[g][:, k * 256: k * 256 + 256]
                            if half == 0:
                                i = eng.tensor_copy(dst, src)
                            else:
                                i = eng.tensor_tensor(dst, dst, src, ALU.add)
                        return i
                    sevac.append(sc.add("vector", ev_acc, deps=[pv_steps[-1]]))

                def ev_s(eng, qi=qi, half=half):
                    dst = s_srow[0:1, qi * QT: qi * QT + QT]
                    if half == 0:
                        return eng.tensor_copy(dst, pb[6][0:1, 0:256])
                    return eng.tensor_tensor(dst, dst, pb[6][0:1, 0:256], ALU.add)
                sevac.append(sc.add("vector", ev_s, deps=[pv_steps[-1]]))
                prev = dict(exp=exp_steps, pv=pv_steps, sevac=sevac)
            sc.emit(blk)
            prev_tail = dict(pb45=p.fence(("scalar",)), attend=p.fence(("vector",)))

        ATT_DONE = p.fence(("vector", "tensor", "scalar"))

        def dbg_tap(sync):
            for w in ATT_DONE:
                sync.wait_ge(p.sem(w[0]), w[1])
            if "acc" in taps:
                p.dma(sync, taps["acc"], s_acc[:], "dma_tap")
            if "srow" in taps:
                p.dma(sync, taps["srow"], s_srow[:], "dma_tap")
            if "kfm" in taps:
                p.dma(sync, taps["kfm"], s_kfm[:], "dma_tap")
            if "vtk0" in taps:
                p.dma(sync, taps["vtk0"], s_vtk0[:], "dma_tap")
            if "xbff_a" in taps:
                p.dma(sync, taps["xbff_a"], s_xbff[:, 0:16384], "dma_tap")
            if "xbff_b" in taps:
                p.dma(sync, taps["xbff_b"], s_xbff[:, 16384:32768], "dma_tap")
            if "qt" in taps:
                p.dma(sync, taps["qt"][:, 0:FC * QT], s_qtile[:], "dma_tap")
        if any(t in taps for t in ("acc", "srow", "kfm", "vtk0", "xbff_a", "xbff_b", "qt")):
            blk.sync(dbg_tap)
        TAPW = [("dma_tap", p.n("dma_tap"))] if p.n("dma_tap") else []

        def ld_x0(sync):
            for w in ATT_DONE:
                sync.wait_ge(p.sem(w[0]), w[1])
            for c in range(FC):
                p.dma(sync, st(s_xres, c, 0, TPC), d_x[c * 128:(c + 1) * 128, 0:TPC],
                      "dma_x0")
        blk.sync(ld_x0)

        sc = Sched(p)
        r0 = sc.add("vector", lambda e: e.reciprocal(s_srow[:], s_srow[:]),
                    sem_waits=ATT_DONE + TAPW)
        r1 = sc.add("vector", lambda e: e.tensor_scalar_mul(s_srow[:], s_srow[:], NF),
                    deps=[r0])
        r2 = sc.add("scalar", lambda e: e.copy(s_rsb[:], s_srow[:]), deps=[r1])
        bexp = []
        for t4 in range(4):
            def mk_b(eng, t4=t4):
                return eng.matmul(pb[4 + t4 % 2][:, 0:512],
                                  s_onesrow[0:1, 0:128],
                                  s_rsb[0:1, t4 * 512:(t4 + 1) * 512],
                                  start=True, stop=True)
            im = sc.add("tensor", mk_b, deps=[r2] + ([bexp[t4 - 2]] if t4 >= 2 else []))
            bexp.append(sc.add("vector", lambda e, t4=t4: e.tensor_copy(
                s_recipSbc[:, t4 * 512:(t4 + 1) * 512], pb[4 + t4 % 2][:, 0:512]),
                deps=[im]))
        zsteps = []
        for c in range(FC):
            a1 = sc.add("vector", lambda e, c=c: e.tensor_tensor(
                st(s_acc, c, 0, TPC), st(s_acc, c, 0, TPC), s_recipSbc[:], ALU.mult),
                deps=[bexp[-1]] if c == 0 else [],
                sem_waits=(TAPW if c == 0 else ()))
            a2 = sc.add("vector", lambda e, c=c: e.tensor_tensor(
                st(s_xres, c, 0, TPC), st(s_xres, c, 0, TPC), st(s_acc, c, 0, TPC),
                ALU.add), deps=[a1],
                sem_waits=[("dma_x0", 16 * FC)] if c == 0 else ())
            zsteps.append(a2)
        sc.emit(blk)

        Z_TAPW = []
        if "z" in taps or "rbcd" in taps:
            zf = p.fence(("vector",))

            def dbg2(sync):
                for w in zf:
                    sync.wait_ge(p.sem(w[0]), w[1])
                if "z" in taps:
                    for c in range(FC):
                        p.dma(sync, taps["z"][c * 128:(c + 1) * 128, :],
                              st(s_xres, c, 0, TPC), "dma_tap")
                if "rbcd" in taps:
                    p.dma(sync, taps["rbcd"][0:128, :], s_recipSbc[:], "dma_tap")
            blk.sync(dbg2)
            Z_TAPW = [("dma_tap", p.n("dma_tap"))]

        lndbg = {k: taps[k] for k in ("lnstat", "lnrow", "rbc2", "mbc2") if k in taps}
        emit_ln(p, blk, locals(), dep_fence=p.fence(("vector",)) + Z_TAPW,
                dbg=lndbg or None)
        if want_taps and "x1" in taps:
            emit_tap(p, blk, locals(), taps["x1"], s_xres, F32)
        if stop_after == "x1":
            emit_final(p, blk, locals(), s_xres, stop_early=True)
            return p, dict(locals())

        emit_ffn(p, blk, locals())
        if want_taps and "x2" in taps:
            emit_tap(p, blk, locals(), taps["x2"], s_xres, F32)
        if stop_after == "x2":
            emit_final(p, blk, locals(), s_xres, stop_early=True)
            return p, dict(locals())

        sub = {"l1prod": "prod", "l1solve": "solve", "l1pass2": "pass2"}.get(stop_after, "all")
        emit_l1(p, blk, locals(), taps if "l1o" in taps else None, substage=sub)
        if stop_after in ("l1prod", "l1solve", "l1pass2"):
            emit_final(p, blk, locals(), s_xres, stop_early=True)
            return p, dict(locals())
        if stop_after == "l1o":
            emit_final(p, blk, locals(), s_xres, stop_early=True)
            return p, dict(locals())
        if want_taps and "x3" in taps:
            emit_tap(p, blk, locals(), taps["x3"], s_xres, F32)
        if stop_after == "x3":
            emit_final(p, blk, locals(), s_xres, stop_early=True)
            return p, dict(locals())

        emit_moe(p, blk, locals())
        if want_taps and "x4" in taps:
            emit_tap(p, blk, locals(), taps["x4"], s_xres, F32)

        emit_final(p, blk, locals(), s_xres, stop_early=False)
    return p, dict(locals())


class NS:
    def __init__(self, d):
        self.__dict__.update(d)


def emit_ln(p, blk, env, dep_fence, dbg=None):
    g = NS(env)
    sc = Sched(p)
    ST = g.s_stat
    z0 = sc.add("vector", lambda e: e.memset(ST[:, 0:32], 0.0), sem_waits=dep_fence)
    tprev = None
    eprev = z0
    for c in range(FC):
        s1 = sc.add("scalar", lambda e, c=c: e.copy(
            g.s_zbf[:], g.st(g.s_xres, c, 0, TPC)),
            deps=([] if tprev is None else [tprev]),
            sem_waits=(dep_fence if c == 0 else ()))
        s2 = sc.add("scalar", lambda e, c=c: e.square(
            g.s_zsq[:], g.st(g.s_xres, c, 0, TPC)))

        def mk_sum(eng, c=c):
            for tt in range(16):
                eng.matmul(g.pb[4][:, tt:tt + 1],
                           g.s_zbf[:, tt * 128:(tt + 1) * 128], g.ONEC,
                           start=True, stop=True)
            for tt in range(16):
                i = eng.matmul(g.pb[4][:, 16 + tt:17 + tt],
                               g.s_zsq[:, tt * 128:(tt + 1) * 128], g.ONEC,
                               start=True, stop=True)
            return i
        tprev = sc.add("tensor", mk_sum, deps=[s1, s2, eprev],
                       sem_waits=(dep_fence if c == 0 else ()))
        eprev = sc.add("vector", lambda e, c=c: e.tensor_tensor(
            ST[:, 0:32], ST[:, 0:32], g.pb[4][:, 0:32], ALU.add), deps=[tprev])
    v = eprev
    v = sc.add("vector", lambda e: e.tensor_scalar_mul(ST[:, 32:48], ST[:, 0:16], 1.0 / D), deps=[v])
    v = sc.add("vector", lambda e: e.tensor_scalar_mul(ST[:, 48:64], ST[:, 16:32], 1.0 / D), deps=[v])
    v = sc.add("vector", lambda e: e.tensor_mul(ST[:, 64:80], ST[:, 32:48], ST[:, 32:48]), deps=[v])
    v = sc.add("vector", lambda e: e.tensor_sub(ST[:, 48:64], ST[:, 48:64], ST[:, 64:80]), deps=[v])
    v = sc.add("vector", lambda e: e.tensor_scalar_add(ST[:, 48:64], ST[:, 48:64], LNEPS), deps=[v])
    sq = sc.add("scalar", lambda e: e.sqrt(ST[:, 64:80], ST[:, 48:64]), deps=[v])
    v = sc.add("vector", lambda e: e.reciprocal(ST[:, 80:96], ST[:, 64:80]), deps=[sq])
    vb = sc.add("vector", lambda e: e.tensor_copy(g.s_statb[:, 0:16], ST[:, 80:96]), deps=[v])
    vb = sc.add("vector", lambda e: e.tensor_copy(g.s_statb[:, 16:32], ST[:, 32:48]), deps=[vb])
    vr = vb
    for grp in range(4):
        def mk_tr(eng, grp=grp):
            for k in range(8):
                i = eng.transpose(g.pT[0:1, k * 128:(k + 1) * 128],
                                  g.s_statb[:, grp * 8 + k: grp * 8 + k + 1],
                                  g.s_ident[:])
            return i
        ttg = sc.add("tensor", mk_tr, deps=[vr])

        def mk_rows(eng, grp=grp):
            for k in range(8):
                col = grp * 8 + k
                dst = (g.s_rrow[0:1, col * 128:(col + 1) * 128] if col < 16 else
                       g.s_rrow[0:1, TPC + (col - 16) * 128: TPC + (col - 15) * 128])
                i = eng.tensor_copy(dst, g.pT[0:1, k * 128:(k + 1) * 128])
            return i
        vr = sc.add("vector", mk_rows, deps=[ttg])
    bsteps = []
    for row, dst in ((0, g.s_rbc), (1, g.s_mbc)):
        for t4 in range(4):
            j = row * 4 + t4

            def mk_b(eng, row=row, t4=t4, j=j):
                return eng.matmul(g.pb[4 + j % 2][:, 0:512],
                                  g.s_onesrow[0:1, 0:128],
                                  g.s_rrow[0:1, row * TPC + t4 * 512: row * TPC + (t4 + 1) * 512],
                                  start=True, stop=True)
            deps = [vr] if j < 2 else [bsteps[j - 2]]
            im = sc.add("tensor", mk_b, deps=deps)
            bsteps.append(sc.add("vector", lambda e, dst=dst, t4=t4, j=j: e.tensor_copy(
                dst[:, t4 * 512:(t4 + 1) * 512], g.pb[4 + j % 2][:, 0:512]), deps=[im]))
    dbg_w = []
    if dbg:
        def dbg_ln(sync):
            for st_ in [bsteps[-1], bsteps[3]]:
                pass
            sync.wait_ge(p.sem("s_vector"), p.n("s_vector"))
            if "lnstat" in dbg:
                p.dma(sync, dbg["lnstat"], g.s_stat[:], "dma_tap")
            if "lnrow" in dbg:
                p.dma(sync, dbg["lnrow"], g.s_rrow[:], "dma_tap")
            if "rbc2" in dbg:
                p.dma(sync, dbg["rbc2"], g.s_rbc[:], "dma_tap")
            if "mbc2" in dbg:
                p.dma(sync, dbg["mbc2"], g.s_mbc[:], "dma_tap")
        dbg_w = [True]
    fin = []
    aprev = None
    for c in range(FC):
        a1 = sc.add("vector", lambda e, c=c: e.tensor_sub(
            g.st(g.s_xres, c, 0, TPC), g.st(g.s_xres, c, 0, TPC), g.s_mbc[:]),
            deps=([bsteps[7]] if c == 0 else [aprev]))
        a2 = sc.add("vector", lambda e, c=c: e.tensor_mul(
            g.st(g.s_xres, c, 0, TPC), g.st(g.s_xres, c, 0, TPC), g.s_rbc[:]),
            deps=[a1] + ([bsteps[3]] if c == 0 else []))
        aprev = a2
        fin.append(sc.add("scalar", lambda e, c=c: e.copy(
            g.st(g.s_xbf, c, 0, TPC), g.st(g.s_xres, c, 0, TPC)), deps=[a2]))
    sc.emit(blk)
    if dbg:
        blk.sync(dbg_ln)


def emit_tap(p, blk, env, tap_ap, src, dt):
    g = NS(env)
    f = p.fence()

    def tap(sync):
        for w in f:
            sync.wait_ge(p.sem(w[0]), w[1])
        for c in range(FC):
            p.dma(sync, tap_ap[c * 128:(c + 1) * 128, :], g.st(src, c, 0, TPC),
                  "dma_tap")
    blk.sync(tap)


def emit_final(p, blk, env, src, stop_early):
    g = NS(env)
    f = p.fence()

    def fin(sync):
        for w in f:
            sync.wait_ge(p.sem(w[0]), w[1])
        for c in range(FC):
            p.dma(sync, g.d_out[c * 128:(c + 1) * 128, :], g.st(src, c, 0, TPC),
                  "dma_out")
        sync.wait_ge(p.sem("dma_out"), p.n("dma_out"))
        if p.n("dma_tap"):
            sync.wait_ge(p.sem("dma_tap"), p.n("dma_tap"))
        if p.n("dma_dbg"):
            sync.wait_ge(p.sem("dma_dbg"), p.n("dma_dbg"))
    blk.sync(fin)


def emit_ffn(p, blk, env):
    g = NS(env)

    def ld_m(sync):
        for w in env["CCM_DONE"]:
            sync.wait_ge(p.sem(w[0]), w[1])
        for ic in range(FC):
            p.dma(sync, g.s_Mall[:, ic * D:(ic + 1) * D],
                  g.cc_m_out[ic * 128:(ic + 1) * 128, :], "dma_m")
    blk.sync(ld_m)

    sc = Sched(p)
    F_X1 = p.fence(("scalar",))
    evs = []
    for oc in range(FC):
        for t4 in range(4):
            j = oc * 4 + t4

            def mk(eng, oc=oc, t4=t4, j=j):
                bank = g.pb[j % 4]
                for ic in range(FC):
                    i = eng.matmul(
                        bank[:, 0:512],
                        g.s_Mall[:, ic * D + oc * 128: ic * D + (oc + 1) * 128],
                        g.st(g.s_xbf, ic, t4 * 512, 512),
                        start=(ic == 0), stop=(ic == FC - 1))
                return i

            def ev(eng, oc=oc, t4=t4, j=j):
                return eng.scalar_tensor_tensor(
                    g.st(g.s_xres, oc, t4 * 512, 512),
                    g.pb[j % 4][:, 0:512], g.s_biasf[:, oc:oc + 1],
                    g.st(g.s_xres, oc, t4 * 512, 512), ALU.add, ALU.add)
            w = [("dma_m", p.n("dma_m"))] + F_X1 if j == 0 else ()
            im = sc.add("tensor", mk, deps=([evs[j - 4]] if j >= 4 else []),
                        sem_waits=w)
            evs.append(sc.add("vector", ev, deps=[im]))
    sc.emit(blk)
    emit_ln(p, blk, env, dep_fence=p.fence(("vector",)))


def emit_l1(p, blk, env, tapd, substage="all"):
    g = NS(env)
    nc = p.nc

    def ld1(gp):
        F = p.fence(("scalar",))
        for w in F:
            gp.wait_ge(p.sem(w[0]), w[1])
        for c in range(FC):
            p.dma(gp, g.s_wv1t[:, c * D:(c + 1) * D],
                  g.d_wv1T[c * 128:(c + 1) * 128, :], "dma_wv1")
        for c in range(FC):
            p.dma(gp, g.s_wgbt[:, c * 32:(c + 1) * 32],
                  g.d_wgbT[c * 128:(c + 1) * 128, :], "dma_wgb")
    blk.gpsimd(ld1)

    def ld2(sync):
        p.dma(sync, g.s_cdyn[:, 0:FC * 128], g.d_cind, "dma_cd")
    blk.sync(ld2)

    sc = Sched(p)
    FLN2 = p.fence(("scalar", "vector"))
    sig = []
    for role in range(2):
        for t4 in range(4):
            j = role * 4 + t4

            def mk(eng, role=role, t4=t4, j=j):
                bank = g.pb[4 + j % 2]
                for c in range(FC):
                    i = eng.matmul(
                        bank[0:16, 0:512],
                        g.s_wgbt[:, c * 32 + role * 16: c * 32 + role * 16 + 16],
                        g.st(g.s_xbf, c, t4 * 512, 512),
                        start=(c == 0), stop=(c == FC - 1))
                return i

            def ev(eng, role=role, t4=t4, j=j):
                dst = g.s_grow if role == 0 else g.s_brow
                return eng.activation(dst[0:16, t4 * 512:(t4 + 1) * 512],
                                      g.pb[4 + j % 2][0:16, 0:512], AF.Sigmoid)
            w = ([("dma_wgb", p.n("dma_wgb"))] + FLN2) if j == 0 else ()
            im = sc.add("tensor", mk, deps=([sig[j - 2]] if j >= 2 else []),
                        sem_waits=w)
            sig.append(sc.add("scalar", ev, deps=[im]))
    btk = []
    for tt in range(16):
        def mk_t(eng, tt=tt):
            return eng.transpose(g.pT[0:128, (tt % 4) * 16:(tt % 4) * 16 + 16],
                                 g.s_brow[0:16, tt * 128:(tt + 1) * 128],
                                 g.s_ident[0:16, 0:16])
        deps = [sig[4 + min(tt // 4, 3)]] + ([btk[tt - 1]] if tt >= 1 else [])
        im = sc.add("tensor", mk_t, deps=deps)
        btk.append(sc.add("vector", lambda e, tt=tt: e.tensor_copy(
            g.s_btk[:, tt * 16:(tt + 1) * 16],
            g.pT[0:128, (tt % 4) * 16:(tt % 4) * 16 + 16]), deps=[im]))
    vevs = []
    for tt in range(NCHUNK):
        for vh in range(2):
            j = tt * 2 + vh

            def mk_v(eng, tt=tt, vh=vh, j=j):
                bank = g.pb[4 + j % 2]
                for c in range(FC):
                    i = eng.matmul(
                        bank[:, 0:512],
                        g.st(g.s_xbf, c, tt * 128, 128),
                        g.s_wv1t[:, c * D + vh * 512: c * D + vh * 512 + 512],
                        start=(c == 0), stop=(c == FC - 1))
                return i

            def ev_v(eng, tt=tt, vh=vh, j=j):
                return eng.tensor_copy(
                    g.s_vtk1[:, tt * D + vh * 512: tt * D + vh * 512 + 512],
                    g.pb[4 + j % 2][:, 0:512])
            w = [("dma_wv1", p.n("dma_wv1"))] if j == 0 else ()
            deps = [btk[-1], btk[-2]] if j < 2 else [vevs[j - 2]]
            im = sc.add("tensor", mk_v, deps=deps, sem_waits=w)
            vevs.append(sc.add("vector", ev_v, deps=[im]))
    nsteps = []
    for t4 in range(4):
        sq_steps = []
        for c in range(FC):
            s1 = sc.add("scalar", lambda e, c=c, t4=t4: e.square(
                g.s_x2sq[:, 0:512], g.st(g.s_xbf, c, t4 * 512, 512)),
                deps=([nsteps[-1]] if (c == 0 and nsteps) else
                      ([vevs[-1]] if (c == 0 and t4 == 0) else [])) + ([sq_steps[-1]] if sq_steps else []))

            def mk_n(eng, c=c, t4=t4):
                return eng.matmul(g.pb[6][0:16, 0:512], g.blk1(c),
                                  g.s_x2sq[:, 0:512],
                                  start=(c == 0), stop=(c == FC - 1))
            t1 = sc.add("tensor", mk_n, deps=[s1])
            sq_steps.append(t1)
        nsteps.append(sc.add("scalar", lambda e, t4=t4: e.sqrt(
            g.s_nrm[0:16, t4 * 512:(t4 + 1) * 512], g.pb[6][0:16, 0:512]),
            deps=[sq_steps[-1]]))
    v1 = sc.add("vector", lambda e: e.tensor_scalar_max(
        g.s_nrm[0:16, :], g.s_nrm[0:16, :], 1e-12), deps=[nsteps[-1]])
    v2 = sc.add("vector", lambda e: e.reciprocal(g.s_nrm[0:16, :], g.s_nrm[0:16, :]),
                deps=[v1])
    v3 = sc.add("vector", lambda e: e.tensor_copy(g.s_nrmb[0:16, :], g.s_nrm[0:16, :]),
                deps=[v2])
    rkevs = []
    bnd = sc.add("vector", lambda e: e.memset(
        g.s_rk[:].rearrange("p (c w) -> p c w", w=TPC + 1)[:, :, 0], 0.0))
    for c in range(FC):
        for t4 in range(4):
            j = c * 4 + t4

            def mk_e(eng, c=c, t4=t4, j=j):
                return eng.matmul(
                    g.pb[4 + j % 2][:, 0:512],
                    g.s_cdyn[0:16, c * 128:(c + 1) * 128],
                    g.s_nrmb[0:16, t4 * 512:(t4 + 1) * 512],
                    start=True, stop=True)

            def ev_e(eng, c=c, t4=t4, j=j):
                return eng.tensor_tensor(
                    g.rksl(c, 1 + t4 * 512, 512),
                    g.st(g.s_xbf, c, t4 * 512, 512),
                    g.pb[4 + j % 2][:, 0:512], ALU.mult)
            w = [("dma_cd", p.n("dma_cd"))] if j == 0 else ()
            deps = [v3] if j < 2 else [rkevs[j - 2]]
            im = sc.add("tensor", mk_e, deps=deps, sem_waits=w)
            rkevs.append(sc.add("vector", ev_e, deps=[im, bnd]))
    def shipb(eng):
        src = g.s_rk[:].rearrange("p (c w) -> p c w", w=TPC + 1)[:, :, TPC]
        with p.nc.allow_non_contiguous_dma(reason="8 strided boundary cols"):
            i = eng.dma_start(out=g.cc_rb_in[:, :], in_=src)
        p.inc(i, "dma_cc", 16)
        return i
    sb = sc.add("gpsimd", shipb, deps=rkevs[-2:])

    def ccb(eng):
        eng.wait_ge(p.sem("dma_cc"), p.n("dma_cc"))
        i = eng.collective_compute(
            "AllGather", ALU.bypass,
            replica_groups=[[0, 1], [2, 3], [4, 5], [6, 7]],
            ins=[g.cc_rb_in.ap().opt()], outs=[g.cc_rb_out.ap().opt()])
        p.inc(i, "ccb_flag", 1)
        return i
    cb = sc.add("gpsimd", ccb, deps=[sb])

    def ldb(eng):
        eng.wait_ge(p.sem("ccb_flag"), 1)
        i = eng.dma_start(out=g.s_rbnd[:], in_=g.cc_rb_out[0:128, :])
        p.inc(i, "rbnd_flag", 16)
        return i
    lb = sc.add("gpsimd", ldb, deps=[cb])
    gb = sc.add("vector", lambda e: e.tensor_scalar_mul(
        g.s_rbnd[:], g.s_rbnd[:], g.s_sgt[:, 0:1]),
        deps=[lb], sem_waits=[("rbnd_flag", 16)])
    BNDRY = sc.add("vector", lambda e: e.tensor_copy(
        g.s_rk[:].rearrange("p (c w) -> p c w", w=TPC + 1)[:, :, 0], g.s_rbnd[:]),
        deps=[gb])
    dumpb = None
    if os.environ.get("DBG_SER"):
        def dumps(gp):
            p.dma(gp, g.d_dbg_rk.ap(), g.s_rk[:], "dma_dbg")
            p.dma(gp, g.d_dbg_btk.ap(), g.s_btk[:], "dma_dbg")
            p.dma(gp, g.d_dbg_vtk.ap(), g.s_vtk1[:], "dma_dbg")
            return p.dma(gp, g.d_dbg_nrm.ap(), g.s_nrm[0:16, :], "dma_dbg")
        dumpb = sc.add("gpsimd", dumps,
                       deps=[BNDRY, rkevs[-1], btk[-1], vevs[-1], v3])
    def ld3(gp):
        for k in range(4):
            p.dma(gp, g.s_woSW[0:64, k * 4096:(k + 1) * 4096],
                  g.d_woSW[0:64, k * 4096:(k + 1) * 4096], "dma_wo")
    w3 = sc.add("gpsimd", ld3,
                deps=[rkevs[-1], v3] + ([dumpb] if dumpb is not None else []),
                sem_waits=([("dma_dbg", 64)] if dumpb is not None else []))
    sc.emit(blk)

    if substage == "prod":
        return
    emit_solves(p, blk, env)
    if substage == "solve":
        return
    emit_pass2(p, blk, env, tapd)
    if substage == "pass2":
        return
    emit_ln(p, blk, env, dep_fence=p.fence(("vector", "tensor")))


def emit_solves(p, blk, env):
    g = NS(env)
    sc = Sched(p)
    PROD = p.fence(("vector", "scalar"))
    z0 = sc.add("vector", lambda e: e.memset(g.s_Spair[:], 0.0), sem_waits=PROD)
    zb = sc.add("vector", lambda e: e.memset(g.s_Sb[:], 0.0), deps=[z0])
    LADDER = [None] * 7

    dma_chunks = []
    p1_steps = {}
    hist = []
    for ci in range(NCHUNK):
        head_final = []
        for h in range(H):
            P = (ci * H + h) % 2
            ro = 64 * (h % 2)
            hc = h // 2
            rkK = g.s_rk[ro:ro + 64,
                         hc * (TPC + 1) + ci * 128: hc * (TPC + 1) + ci * 128 + 128]
            rkR = g.s_rk[ro:ro + 64,
                         hc * (TPC + 1) + 1 + ci * 128: hc * (TPC + 1) + 1 + ci * 128 + 128]
            bcol = g.s_btk[:, ci * 16 + h: ci * 16 + h + 1]
            idn = g.s_ident if ro == 0 else g.s_swap
            FGO = P * 14 * 128
            fgs = lambda k, FGO=FGO: g.s_fg[:, FGO + k * 128: FGO + (k + 1) * 128]
            Xs = lambda k, P=P: g.s_X[:, (P * 2 + k) * 128: (P * 2 + k + 1) * 128]
            ktks = g.s_ktk[:, P * 64:(P + 1) * 64]
            yts = g.s_yt[:, P * 128:(P + 1) * 128]
            yxs = g.s_yxt[:, P * 128:(P + 1) * 128]
            pTs = lambda k, P=P: g.pT[:, (P * 4 + k) * 128: (P * 4 + k) * 128 + 128]
            h1 = hist[-1] if hist else {}
            h2 = hist[-2] if len(hist) >= 2 else {}
            d1 = lambda k: [h1[k]] if k in h1 else []
            d2 = lambda k: [h2[k]] if k in h2 else []

            t_gen = sc.add("tensor", lambda e, rkK=rkK, ro=ro: e.matmul(
                g.pb[0][:, 0:128], rkK, rkK, start=True, stop=True,
                tile_position=(ro, 0)),
                deps=d1("v_f0"),
                sem_waits=([("s_vector", p.n("s_vector"))] if (ci == 0 and h == 0) else ()))
            v_f0 = sc.add("vector", lambda e, bcol=bcol, fgs=fgs: e.scalar_tensor_tensor(
                fgs(0), g.pb[0][:, 0:128], bcol, g.s_masksl[:], ALU.mult, ALU.mult),
                deps=[t_gen] + d2("t_app6"))
            t_g0 = sc.add("tensor", lambda e, fgs=fgs, pTs=pTs: e.transpose(
                pTs(0), fgs(0), g.s_ident[:]), deps=[v_f0] + d1("v_pt"))
            v_g0 = sc.add("vector", lambda e, fgs=fgs, pTs=pTs: e.tensor_copy(
                fgs(1), pTs(0)), deps=[t_g0])
            ev_prev = v_g0
            rung_evs = []
            for r in range(1, 7):
                def mk_r(eng, r=r, fgs=fgs):
                    if r == 6:
                        return eng.matmul(
                            g.pb[1][:, ((r - 1) % 2) * 256: ((r - 1) % 2) * 256 + 128],
                            fgs(2 * r - 2), fgs(2 * r - 1), start=True, stop=True)
                    eng.matmul(g.pb[1][:, ((r - 1) % 2) * 256: ((r - 1) % 2) * 256 + 128],
                               fgs(2 * r - 1), fgs(2 * r - 2), start=True, stop=True)
                    return eng.matmul(
                        g.pb[1][:, ((r - 1) % 2) * 256 + 128: ((r - 1) % 2) * 256 + 256],
                        fgs(2 * r - 2), fgs(2 * r - 1), start=True, stop=True)

                def ev_r(eng, r=r, fgs=fgs, FGO=FGO):
                    if r <= 5:
                        return eng.tensor_copy(
                            g.s_fg[:, FGO + 2 * r * 128: FGO + (2 * r + 2) * 128],
                            g.pb[1][:, ((r - 1) % 2) * 256: ((r - 1) % 2) * 256 + 256])
                    return eng.tensor_copy(
                        fgs(13),
                        g.pb[1][:, ((r - 1) % 2) * 256: ((r - 1) % 2) * 256 + 128])
                tm = sc.add("tensor", mk_r,
                            deps=[ev_prev] + ([rung_evs[r - 3]] if r >= 3 else [])
                                 + (d1("rung6") if r == 1 else []))
                ev_prev = sc.add("vector", ev_r, deps=[tm])
                rung_evs.append(ev_prev)
            t_kt = sc.add("tensor", lambda e, rkK=rkK, pTs=pTs, idn=idn, ro=ro: e.transpose(
                pTs(1)[:, 0:64], rkK, idn[ro:ro + 64, 0:64]), deps=[v_g0])
            v_kt = sc.add("vector", lambda e, ktks=ktks, pTs=pTs: e.tensor_copy(
                ktks, pTs(1)[:, 0:64]), deps=[t_kt] + d2("t_pt"))
            v_x0a = sc.add("vector", lambda e, Xs=Xs, ktks=ktks, bcol=bcol: e.tensor_scalar_mul(
                Xs(0)[:, 0:64], ktks, bcol), deps=[v_kt] + d2("t_kx"))
            v_x0b = sc.add("vector", lambda e, Xs=Xs, ci=ci, h=h, bcol=bcol: e.tensor_scalar_mul(
                Xs(0)[:, 64:128], g.s_vtk1[:, ci * D + h * 64: ci * D + (h + 1) * 64], bcol),
                deps=[v_x0a])
            xcur = 0
            ev_app = v_x0b
            app_evs = []
            app_tms = []
            for j in range(7):
                def mk_a(eng, j=j, xcur=xcur, fgs=fgs, Xs=Xs):
                    return eng.matmul(g.pb[2][:, (j % 2) * 128: (j % 2) * 128 + 128],
                                      fgs(2 * j + 1), Xs(xcur), start=True, stop=True)
                gdep = v_g0 if j == 0 else rung_evs[j - 1]
                tm = sc.add("tensor", mk_a,
                            deps=[ev_app, gdep] + (d1("app_ev6") if j == 0 else []))
                odd = (h % 2 == 1)

                def ev_a(eng, j=j, xcur=xcur, Xs=Xs):
                    src = g.pb[2][:, (j % 2) * 128: (j % 2) * 128 + 128]
                    op = ALU.subtract if j == 0 else ALU.add
                    return eng.tensor_tensor(Xs(1 - xcur), Xs(xcur), src, op)
                ev_app = sc.add("vector", ev_a, deps=[tm])
                app_evs.append(ev_app)
                app_tms.append(tm)
                xcur = 1 - xcur
            XF = xcur
            t_yt = sc.add("tensor", lambda e, rkK=rkK, rkR=rkR, ro=ro: e.matmul(
                g.pb[3][:, 0:128], rkK, rkR, start=True, stop=True,
                tile_position=(ro, 0)), deps=[v_f0] + d1("v_yx"))
            v_yt = sc.add("vector", lambda e, yts=yts: e.tensor_tensor(
                yts, g.pb[3][:, 0:128], g.s_masksui[:], ALU.mult),
                deps=[t_yt] + d2("t_yx"))
            t_yx = sc.add("tensor", lambda e, yts=yts, Xs=Xs, XF=XF: e.matmul(
                g.pb[3][:, 128:256], yts, Xs(XF), start=True, stop=True),
                deps=[v_yt, ev_app])
            v_yx = sc.add("vector", lambda e, yxs=yxs: e.tensor_copy(
                yxs, g.pb[3][:, 128:256]), deps=[t_yx] + d2("t_yxt2"))
            t_yxt = sc.add("tensor", lambda e, yxs=yxs, pTs=pTs, ro=ro: e.transpose(
                pTs(2)[ro:ro + 64, 0:128], yxs[:, 0:64], g.s_ident[:],
                tile_position=(0, ro)), deps=[v_yx])
            t_yxt2 = sc.add("tensor", lambda e, yxs=yxs, pTs=pTs, ro=ro: e.transpose(
                pTs(3)[ro:ro + 64, 0:128], yxs[:, 64:128], g.s_ident[:],
                tile_position=(0, ro)), deps=[v_yx])
            odd = (h % 2 == 1)

            def ev_yxt(eng, pTs=pTs, h=h, ro=ro, hc=hc):
                eng.tensor_scalar_mul(
                    g.s_stgW[ro:ro + 64, hc * 128:(hc + 1) * 128],
                    pTs(2)[ro:ro + 64, 0:128], -1.0)
                return eng.tensor_copy(
                    g.s_stgO[ro:ro + 64, hc * 128:(hc + 1) * 128],
                    pTs(3)[ro:ro + 64, 0:128])
            v_yxt = sc.add("vector", ev_yxt, deps=[t_yxt, t_yxt2],
                           sem_waits=([("dma_ser", 64 * ci)] if (h == 0 and ci > 0) else []))
            t_kx = sc.add("tensor", lambda e, ktks=ktks, Xs=Xs, XF=XF, ro=ro: e.matmul(
                g.pb[4][ro:ro + 64, 0:128], ktks, Xs(XF), start=True, stop=True,
                tile_position=(0, ro)), deps=[ev_app, v_kt] + d1("v_kx"))
            wlo, qlo = 0, 64

            def ev_kx2(eng, ro=ro, wlo=wlo, qlo=qlo, hc=hc, ktks=ktks):
                eng.tensor_copy(ktks[ro:ro + 64, 0:64],
                                g.pb[4][ro:ro + 64, wlo:wlo + 64])
                return eng.tensor_copy(
                    g.s_stgQ[ro:ro + 64, hc * 64:(hc + 1) * 64],
                    g.pb[4][ro:ro + 64, qlo:qlo + 64])
            stg_w = [("dma_ser", 64 * ci)] if (h == 0 and ci > 0) else []
            v_kx = sc.add("vector", ev_kx2, deps=[t_kx], sem_waits=stg_w)
            t_pt = sc.add("tensor", lambda e, ktks=ktks, pTs=pTs, ro=ro: e.transpose(
                pTs(3)[ro:ro + 64, 0:64],
                ktks[ro:ro + 64, 0:64],
                g.s_ident[ro:ro + 64, ro:ro + 64] if ro else g.s_ident[0:64, 0:64],
                tile_position=(ro, ro)), deps=[v_kx])
            v_pt = sc.add("vector", lambda e, pTs=pTs, ro=ro, hc=hc: e.tensor_scalar_mul(
                g.s_stgP[ro:ro + 64, hc * 64:(hc + 1) * 64], pTs(3)[ro:ro + 64, 0:64], -1.0),
                deps=[t_pt])
            head_final.append((v_yxt, v_pt, v_kx))
            hist.append(dict(
                v_f0=v_f0, rung6=rung_evs[-1], app_ev6=app_evs[-1],
                t_app6=app_tms[-1], v_yx=v_yx, t_yx=t_yx, t_yxt2=t_yxt2,
                v_kx=v_kx, t_kx=t_kx, v_pt=v_pt, t_pt=t_pt))
        chain = []
        for h in range(H):
            ro = 64 * (h % 2)
            hc = h // 2

            def mk_s(eng, ro=ro, hc=hc):
                return eng.matmul(g.pb[5][ro:ro + 64, hc * 64:(hc + 1) * 64],
                                  g.s_stgP[ro:ro + 64, hc * 64:(hc + 1) * 64],
                                  g.s_Sb[ro:ro + 64, hc * 64:(hc + 1) * 64],
                                  start=True, stop=True, tile_position=(ro, ro))
            deps = [head_final[h][1]] + ([z0, zb] if ci == 0 else []) + ([chain[-1][1]] if chain else [])
            tm = sc.add("tensor", mk_s, deps=deps)

            ss_ = g.s_Spair[ro:ro + 64, hc * 64:(hc + 1) * 64]
            e1 = sc.add("vector", lambda e, ss=ss_, ro=ro, hc=hc: e.tensor_tensor(
                ss, ss, g.pb[5][ro:ro + 64, hc * 64:(hc + 1) * 64], ALU.add), deps=[tm])
            e2 = sc.add("vector", lambda e, ss=ss_, ro=ro, hc=hc: e.tensor_tensor(
                ss, ss, g.s_stgQ[ro:ro + 64, hc * 64:(hc + 1) * 64], ALU.add), deps=[e1])
            ev = sc.add("vector", lambda e, ss=ss_, ro=ro, hc=hc: e.tensor_copy(
                g.s_Sb[ro:ro + 64, hc * 64:(hc + 1) * 64], ss), deps=[e2])
            chain.append((tm, ev))
        def ship(eng, ci=ci):
            p.dma(eng, g.d_serP[ci], g.s_stgP[:], "dma_ser")
            p.dma(eng, g.d_serQ[ci], g.s_stgQ[:], "dma_ser")
            p.dma(eng, g.d_serW[ci], g.s_stgW[:], "dma_ser")
            i = p.dma(eng, g.d_serO[ci], g.s_stgO[:], "dma_ser")
            return i
        dd = sc.add("gpsimd", ship,
                    deps=[head_final[-1][0], head_final[-1][1], chain[-1][1]])
        dma_chunks.append(dd)
    sc.emit(blk)


def emit_pass2(p, blk, env, tapd):
    import os
    P2CUT = int(os.environ.get('P2CUT', '16'))
    P2SKIP = os.environ.get('P2SKIP', '')
    g = NS(env)
    sc = Sched(p)
    SOLVED = p.fence(("vector", "tensor"))

    def ship_s(eng):
        for w in SOLVED:
            eng.wait_ge(p.sem(w[0]), w[1])
        i = eng.dma_start(out=g.cc_s_in[:, :], in_=g.s_Spair[:])
        p.inc(i, "dma_cc", 16)
        return i
    s1 = sc.add("gpsimd", ship_s, sem_waits=[("dma_ser", 64 * NCHUNK)])

    def cc_s(eng):
        eng.wait_ge(p.sem("dma_cc"), p.n("dma_cc"))
        i = eng.collective_compute(
            "AllGather", ALU.bypass,
            replica_groups=[[0, 1], [2, 3], [4, 5], [6, 7]],
            ins=[g.cc_s_in.ap().opt()], outs=[g.cc_s_out.ap().opt()])
        p.inc(i, "ccs_flag", 1)
        return i
    s2 = sc.add("gpsimd", cc_s, deps=[s1])

    def ld_s(eng):
        eng.wait_ge(p.sem("ccs_flag"), 1)
        i = eng.dma_start(out=g.s_Sin[:], in_=g.cc_s_out[0:128, :])
        p.inc(i, "sin_flag", 16)
        return i
    s3 = sc.add("gpsimd", ld_s, deps=[s2])
    v0 = sc.add("vector", lambda e: e.tensor_scalar_mul(
        g.s_Spair[:], g.s_Sin[:], g.s_sgt[:, 0:1]),
        deps=[s3], sem_waits=[("sin_flag", 16)])
    vb = sc.add("vector", lambda e: e.tensor_copy(g.s_Sb[:], g.s_Spair[:]), deps=[v0])

    TF = p.fence(("tensor",))

    def ld_selh(eng):
        for w in TF:
            eng.wait_ge(p.sem(w[0]), w[1])
        i = eng.dma_start(out=g.s_cdyn[:, 0:H * 64], in_=g.d_cselh)
        p.inc(i, "dma_cd2", 16)
        return i
    sc.add("sync", ld_selh)

    dma_in = []
    cons_done = {0: [vb], 1: [vb]}
    last_och_cons = None
    for ci in range(min(NCHUNK, P2CUT)):
        par = ci % 2

        def ld(eng, ci=ci, par=par):
            p.dma(eng, g.s_p2P[par][:], g.d_serP[ci], f"dma_p2{par}")
            p.dma(eng, g.s_p2Q[par][:], g.d_serQ[ci], f"dma_p2{par}")
            p.dma(eng, g.s_p2W[par][:], g.d_serW[ci], f"dma_p2{par}")
            i = p.dma(eng, g.s_p2O[par][:], g.d_serO[ci], f"dma_p2{par}")
            return i
        dd = sc.add("gpsimd", ld, deps=list(cons_done[par]),
                    sem_waits=([("dma_ser", 64 * NCHUNK)] if ci < 2 else ()))
        dma_in.append(dd)
        IN_RDY = [(f"dma_p2{par}", 64 * (ci // 2 + 1))]

        gexp_evs = []
        for hg in range(4):
            def mk_g(eng, ci=ci, hg=hg):
                for k in range(4):
                    h = hg * 4 + k
                    i = eng.matmul(g.pb[0][0:64, k * 128:(k + 1) * 128],
                                   g.s_cdyn[0:16, h * 64:(h + 1) * 64],
                                   g.s_grow[0:16, ci * 128:(ci + 1) * 128],
                                   start=True, stop=True)
                return i

            def ev_g(eng, hg=hg):
                return eng.tensor_copy(g.s_gexp[0:64, hg * 512:(hg + 1) * 512],
                                       g.pb[0][0:64, 0:512])
            w = [("dma_cd2", 16)] if (ci == 0 and hg == 0) else []
            deps = [gexp_evs[-1]] if gexp_evs else ([last_och_cons] if last_och_cons else [])
            im = sc.add("tensor", mk_g, deps=deps + ([vb] if ci == 0 else []), sem_waits=w)
            gexp_evs.append(sc.add("vector", ev_g, deps=[im]))

        oevs = []
        supd = []
        for h in range(H):
            ro = 64 * (h % 2)
            hc = h // 2
            rkR = g.s_rk[ro:ro + 64,
                         hc * (TPC + 1) + 1 + ci * 128: hc * (TPC + 1) + 1 + ci * 128 + 128]
            Ssl = g.s_Spair[ro:ro + 64, hc * 64:(hc + 1) * 64]
            Sbl = g.s_Sb[ro:ro + 64, hc * 64:(hc + 1) * 64]
            par2 = par
            rOloc = 64 - ro

            def mk_o(eng, h=h, ro=ro, hc=hc, rkR=rkR, Sbl=Sbl, par2=par2):
                slot = g.pb[1 + 2 * (h % 2)][0:64, ((h // 2) % 4) * 128:((h // 2) % 4) * 128 + 128]
                eng.matmul(slot, Sbl, rkR, start=True, stop=False,
                           tile_position=(ro, 0))
                eng.matmul(slot, Sbl,
                           g.s_p2W[par2][ro:ro + 64, hc * 128:(hc + 1) * 128],
                           start=False, stop=False, tile_position=(ro, 0))
                lhs_i = g.s_ident[ro:ro + 64, ro:ro + 64] if ro else g.s_ident[0:64, 0:64]
                return eng.matmul(slot, lhs_i,
                                  g.s_p2O[par2][ro:ro + 64, hc * 128:(hc + 1) * 128],
                                  start=False, stop=True, tile_position=(ro, 0))
            deps = [gexp_evs[h // 4]] + ([oevs[h - 2]] if h >= 2 else ([last_och_cons] if last_och_cons else []))
            deps += [supd[-1][0]] if supd else []
            if "o" in P2SKIP:
                tm = sc.add("tensor", lambda e: None, deps=deps,
                            sem_waits=(IN_RDY if h == 0 else []))
            else:
                tm = sc.add("tensor", mk_o, deps=deps,
                            sem_waits=(IN_RDY if h == 0 else []))

            def ev_o(eng, h=h):
                return eng.tensor_tensor(
                    g.s_och[0:64, h * 128:(h + 1) * 128],
                    g.pb[1 + 2 * (h % 2)][0:64, ((h // 2) % 4) * 128:((h // 2) % 4) * 128 + 128],
                    g.s_gexp[0:64, h * 128:(h + 1) * 128], ALU.mult)
            oevs.append(sc.add("vector", (lambda e: None) if "o" in P2SKIP else ev_o, deps=[tm]))

            def mk_su(eng, ro=ro, hc=hc, Sbl=Sbl, par2=par2):
                return eng.matmul(g.pb[5][ro:ro + 64, hc * 64:(hc + 1) * 64],
                                  g.s_p2P[par2][ro:ro + 64, hc * 64:(hc + 1) * 64],
                                  Sbl, start=True, stop=True, tile_position=(ro, ro))
            tsu = sc.add("tensor", (lambda e: None) if "s" in P2SKIP else mk_su,
                         deps=[tm] + ([supd[-1][1]] if supd else []))

            u1 = sc.add("vector", lambda e, Ssl=Ssl, ro=ro, hc=hc: e.tensor_tensor(
                Ssl, Ssl, g.pb[5][ro:ro + 64, hc * 64:(hc + 1) * 64], ALU.add), deps=[tsu])
            u2 = sc.add("vector", lambda e, Ssl=Ssl, ro=ro, hc=hc, par2=par2: e.tensor_tensor(
                Ssl, Ssl, g.s_p2Q[par2][ro:ro + 64, hc * 64:(hc + 1) * 64], ALU.add), deps=[u1])
            vsu = sc.add("vector", lambda e, Ssl=Ssl, Sbl=Sbl: e.tensor_copy(
                Sbl, Ssl), deps=[u2])
            supd.append((tsu, vsu))
        cons_done[par] = [supd[-1][1], oevs[-1]]

        op_evs = []
        for jc in range(FC):
            def mk_op(eng, ci=ci, jc=jc):
                for h in range(H):
                    i = eng.matmul(
                        g.pb[2][:, (jc % 4) * 128:(jc % 4) * 128 + 128],
                        g.s_woSW[0:64, h * D + jc * 128: h * D + (jc + 1) * 128],
                        g.s_och[0:64, h * 128:(h + 1) * 128],
                        start=(h == 0), stop=(h == H - 1))
                return i
            w = [("dma_wo", p.n("dma_wo"))] if (ci == 0 and jc == 0) else []
            deps = [oevs[-1]] + ([op_evs[jc - 1]] if jc >= 1 else [])
            im = sc.add("tensor", (lambda e: None) if "p" in P2SKIP else mk_op,
                        deps=deps, sem_waits=w)

            def ev_op(eng, ci=ci, jc=jc):
                dst = g.st(g.s_xres, jc, ci * 128, 128)
                return eng.tensor_tensor(
                    dst, dst, g.pb[2][:, (jc % 4) * 128:(jc % 4) * 128 + 128], ALU.add)
            op_evs.append(sc.add("vector", (lambda e: None) if "p" in P2SKIP else ev_op, deps=[im]))
        last_och_cons = op_evs[-1]
    sc.emit(blk)
    if tapd:
        pass


def emit_moe(p, blk, env):
    g = NS(env)
    nc = p.nc
    sc = Sched(p)
    FLN3 = p.fence(("scalar", "vector"))
    FTEN = p.fence(("tensor",))

    def ld_r(eng):
        for w in FLN3 + FTEN:
            eng.wait_ge(p.sem(w[0]), w[1])
        for c in range(FC):
            p.dma(eng, g.s_wgbt[:, c * 32: c * 32 + 8],
                  g.d_rwT[c * 128:(c + 1) * 128, :], "dma_rw")
        p.dma(eng, g.s_moew[0][:].rearrange("p (c d) -> p c d", d=D),
              g.d_w1[0].rearrange("(c p) d -> p c d", p=128), "dma_me0")
        i = p.dma(eng, g.s_moew[2][:].rearrange("p (c d) -> p c d", d=D),
                  g.d_w2[0].rearrange("(c p) d -> p c d", p=128), "dma_me0")
        return i
    sc.add("gpsimd", ld_r)

    def ld_inde(eng):
        i = eng.dma_start(out=g.s_cdyn[0:8, 0:8 * 128], in_=g.d_cinde)
        p.inc(i, "dma_cd3", 16)
        return i
    sc.add("sync", ld_inde,
           sem_waits=FTEN + [("dma_cd2", p.n("dma_cd2"))])

    revs = []
    for t4 in range(4):
        def mk_r(eng, t4=t4):
            bank = g.pb[4 + t4 % 2]
            for c in range(FC):
                i = eng.matmul(bank[0:8, 0:512],
                               g.s_wgbt[:, c * 32: c * 32 + 8],
                               g.st(g.s_xbf, c, t4 * 512, 512),
                               start=(c == 0), stop=(c == FC - 1))
            return i
        w = [("dma_rw", 128)] + FLN3 if t4 == 0 else []
        im = sc.add("tensor", mk_r, deps=([revs[t4 - 2]] if t4 >= 2 else []),
                    sem_waits=w)
        revs.append(sc.add("vector", lambda e, t4=t4: e.tensor_copy(
            g.s_wgtb[0:8, t4 * 512:(t4 + 1) * 512], g.pb[4 + t4 % 2][0:8, 0:512]),
            deps=[im]))
    mz = sc.add("vector", lambda e: e.memset(g.s_mp[:, 8:24], 1e30))
    wtr = [mz]
    for tt in range(16):
        def mk_t(eng, tt=tt):
            return eng.transpose(g.pT[0:128, (tt % 4) * 8:(tt % 4) * 8 + 8],
                                 g.s_wgtb[0:8, tt * 128:(tt + 1) * 128],
                                 g.s_ident[0:8, 0:8])
        deps = [revs[-1]] if tt == 0 else [wtr[-1]]
        tr = sc.add("tensor", mk_t, deps=deps)
        v = sc.add("vector", lambda e, tt=tt: e.tensor_copy(
            g.s_rtk[:, 0:8], g.pT[0:128, (tt % 4) * 8:(tt % 4) * 8 + 8]), deps=[tr])
        v = sc.add("vector", lambda e: e.reduce_max(
            g.s_mp[:, 8:9], g.s_rtk[:, 0:8], AX.X), deps=[v] + ([mz] if tt == 0 else []))
        v = sc.add("vector", lambda e: e.tensor_copy(
            g.s_mp[:, 0:1], g.s_mp[:, 8:9]), deps=[v])
        v = sc.add("vector", lambda e: e.match_replace(
            out=g.s_rtk[:, 8:16], in_to_replace=g.s_mp[:, 8:16],
            in_values=g.s_rtk[:, 0:8], imm_value=-1e30), deps=[v])
        v = sc.add("vector", lambda e: e.reduce_max(
            g.s_mp[:, 16:17], g.s_rtk[:, 8:16], AX.X), deps=[v])
        v = sc.add("vector", lambda e: e.tensor_copy(
            g.s_mp[:, 2:3], g.s_mp[:, 16:17]), deps=[v])
        v = sc.add("vector", lambda e: e.match_replace(
            out=g.s_rtk[:, 16:24], in_to_replace=g.s_mp[:, 16:24],
            in_values=g.s_rtk[:, 8:16], imm_value=-1e30), deps=[v])
        v = sc.add("vector", lambda e: e.tensor_sub(
            g.s_mp[:, 4:5], g.s_mp[:, 0:1], g.s_mp[:, 2:3]), deps=[v])
        v = sc.add("scalar", lambda e: e.activation(
            g.s_mp[:, 5:6], g.s_mp[:, 4:5], AF.Sigmoid), deps=[v])
        v = sc.add("vector", lambda e: e.tensor_scalar(
            g.s_mp[:, 6:7], g.s_mp[:, 5:6], -1.0, 1.0, ALU.mult, ALU.add), deps=[v])
        v = sc.add("vector", lambda e: e.tensor_tensor(
            g.s_rtk[:, 0:8], g.s_rtk[:, 0:8], g.s_rtk[:, 8:16], ALU.not_equal), deps=[v])
        v = sc.add("vector", lambda e: e.tensor_tensor(
            g.s_rtk[:, 8:16], g.s_rtk[:, 8:16], g.s_rtk[:, 16:24], ALU.not_equal), deps=[v])
        v = sc.add("vector", lambda e: e.tensor_scalar_mul(
            g.s_rtk[:, 0:8], g.s_rtk[:, 0:8], g.s_mp[:, 5:6]), deps=[v])
        v = sc.add("vector", lambda e: e.tensor_scalar_mul(
            g.s_rtk[:, 8:16], g.s_rtk[:, 8:16], g.s_mp[:, 6:7]), deps=[v])
        v = sc.add("vector", lambda e: e.tensor_add(
            g.s_rtk[:, 16:24], g.s_rtk[:, 0:8], g.s_rtk[:, 8:16]), deps=[v])
        v = sc.add("vector", lambda e: e.tensor_copy(
            g.s_wgtk[:, 0:8], g.s_rtk[:, 16:24]), deps=[v])
        tb = sc.add("tensor", lambda e, tt=tt: e.transpose(
            g.pT[0:8, 512 + (tt % 4) * 128: 512 + (tt % 4) * 128 + 128],
            g.s_wgtk[:, 0:8], g.s_ident[:]), deps=[v])
        v = sc.add("vector", lambda e, tt=tt: e.tensor_copy(
            g.s_wgtb[0:8, tt * 128:(tt + 1) * 128],
            g.pT[0:8, 512 + (tt % 4) * 128: 512 + (tt % 4) * 128 + 128]), deps=[tb])
        wtr.append(v)

    prev = dict(h=[], wexp=[], evac=[])
    for e_i in range(8):
        me = e_i % 2
        if e_i < 7:
            def pre(eng, e_i=e_i):
                nx = (e_i + 1) % 2
                p.dma(eng, g.s_moew[nx][:].rearrange("p (c d) -> p c d", d=D),
                      g.d_w1[e_i + 1].rearrange("(c p) d -> p c d", p=128),
                      f"dma_me{nx}")
                i = p.dma(eng, g.s_moew[2 + nx][:].rearrange("p (c d) -> p c d", d=D),
                          g.d_w2[e_i + 1].rearrange("(c p) d -> p c d", p=128),
                          f"dma_me{nx}")
                return i
            sc.add("gpsimd", pre,
                   deps=list(prev["evac"]) + list(prev["h"]) +
                        ([revs[-1]] if e_i == 0 else []),
                   sem_waits=(FTEN if e_i == 0 else []))
        W_RDY = [(f"dma_me{me}", 32 * (e_i // 2 + 1))]
        for t4 in range(4):
            def mk_w(eng, e_i=e_i, t4=t4):
                return eng.matmul(g.pb[4][:, 0:512],
                                  g.s_cdyn[0:8, e_i * 128:(e_i + 1) * 128],
                                  g.s_wgtb[0:8, t4 * 512:(t4 + 1) * 512],
                                  start=True, stop=True)
            wdeps = ([wtr[-1]] if (e_i == 0 and t4 == 0) else []) + prev["wexp"]
            wsw = [("dma_cd3", 16)] if (e_i == 0 and t4 == 0) else []
            imw = sc.add("tensor", mk_w, deps=wdeps, sem_waits=wsw)
            evw = sc.add("vector", lambda e: e.tensor_copy(
                g.s_wexp[:, 0:512], g.pb[4][:, 0:512]), deps=[imw])
            hsteps = []
            wave_last = []
            for wv in range(4):
                for k in range(2):
                    dc = wv * 2 + k

                    def mk_h(eng, dc=dc, k=k, t4=t4, me=me):
                        bank = g.pb[5 + k]
                        for c in range(FC):
                            i = eng.matmul(
                                bank[:, 0:512],
                                g.s_moew[me][:, c * D + dc * 128: c * D + (dc + 1) * 128],
                                g.st(g.s_xbf, c, t4 * 512, 512),
                                start=(c == 0), stop=(c == FC - 1))
                        return i
                    hdeps = []
                    if wv == 0 and k == 0:
                        hdeps = [evw] + prev["evac"]
                    elif len(wave_last) >= 2:
                        hdeps = [wave_last[-2]]
                    imh = sc.add("tensor", mk_h, deps=hdeps,
                                 sem_waits=(W_RDY if (t4 == 0 and dc == 0) else []))

                    def rl(eng, dc=dc, k=k):
                        return eng.activation(
                            g.s_htile[:, dc * 512:(dc + 1) * 512],
                            g.pb[5 + k][:, 0:512], AF.Relu)
                    sr = sc.add("scalar", rl, deps=[imh])

                    q1 = sc.add("vector", lambda e, dc=dc: e.tensor_tensor(
                        g.s_htile[:, dc * 512:(dc + 1) * 512],
                        g.s_htile[:, dc * 512:(dc + 1) * 512],
                        g.s_htile[:, dc * 512:(dc + 1) * 512], ALU.mult), deps=[sr])
                    vs = sc.add("vector", lambda e, dc=dc: e.tensor_tensor(
                        g.s_htile[:, dc * 512:(dc + 1) * 512],
                        g.s_htile[:, dc * 512:(dc + 1) * 512],
                        g.s_wexp[:, 0:512], ALU.mult), deps=[q1])
                    hsteps.append(vs)
                    wave_last.append(sr)
            w2evs = []
            for c in range(FC):
                def mk_o(eng, c=c, me=me):
                    for dc in range(FC):
                        i = eng.matmul(
                            g.pb[c % 4][:, 0:512],
                            g.s_moew[2 + me][:, dc * D + c * 128: dc * D + (c + 1) * 128],
                            g.s_htile[:, dc * 512:(dc + 1) * 512],
                            start=(dc == 0), stop=(dc == FC - 1))
                    return i
                deps = [hsteps[-1]] + ([w2evs[c - 4]] if c >= 4 else [])
                imo = sc.add("tensor", mk_o, deps=deps)

                def ev_o(eng, c=c, t4=t4):
                    dst = g.st(g.s_xres, c, t4 * 512, 512)
                    return eng.tensor_tensor(dst, dst, g.pb[c % 4][:, 0:512], ALU.add)
                w2evs.append(sc.add("vector", ev_o, deps=[imo]))
            prev = dict(h=[hsteps[-1]], wexp=[hsteps[-1]], evac=[w2evs[-1]])
    sc.emit(blk)
    emit_ln(p, blk, env, dep_fence=p.fence(("vector", "tensor")))


def _dft_consts():
    idx = np.arange(D, dtype=np.float64)
    ang = 2.0 * np.pi * np.outer(idx, idx) / D
    fre = np.cos(ang)
    fim = -np.sin(ang)
    gre = np.cos(ang) / D
    gim = np.sin(ang) / D
    return fre, fim, gre, gim


def _host_constants():
    fre, fim, gre, gim = _dft_consts()
    cs = {}
    ident = np.eye(128, dtype=np.float32)
    mask_sl = np.tril(np.ones((128, 128), np.float32), -1)
    mask_sui = np.triu(np.ones((128, 128), np.float32), 0)
    swap = np.zeros((128, 128), np.float32)
    swap[0:64, 64:128] = np.eye(64)
    swap[64:128, 0:64] = np.eye(64)
    cs["c_pack"] = np.concatenate([ident, mask_sl, mask_sui, swap], axis=1).astype(BFNP)
    onesb = np.zeros((128, 132), np.float32)
    for c in range(FC):
        onesb[0:64, c * 16 + 2 * c] = 1.0
        onesb[64:128, c * 16 + 2 * c + 1] = 1.0
    onesb[:, 128] = 1.0
    cs["c_onesb"] = onesb.astype(BFNP)
    cs["c_onesrow"] = np.ones((1, 128), np.float32).astype(BFNP)
    cfl = np.zeros((128, 4), np.float32); cfl[:, 0] = CSHIFT; cfl[:, 1] = 1e-24
    cs["c_fl"] = cfl
    cind = np.zeros((16, FC * 128), np.float32)
    for c in range(FC):
        cind[2 * c, c * 128: c * 128 + 64] = 1.0
        cind[2 * c + 1, c * 128 + 64: c * 128 + 128] = 1.0
    cs["c_ind"] = cind.astype(BFNP)
    cinde = np.zeros((8, 8 * 128), np.float32)
    for e in range(8):
        cinde[e, e * 128:(e + 1) * 128] = 1.0
    cs["c_inde"] = cinde.astype(BFNP)
    cselh = np.zeros((16, H * 64), np.float32)
    for h in range(H):
        cselh[h, h * 64:(h + 1) * 64] = 1.0
    cs["c_selh"] = cselh.astype(BFNP)
    cs["c_gre"] = gre.astype(BFNP)
    cs["c_gimn"] = (-gim).astype(BFNP)
    return cs, fre, fim


_BUILD_CACHE = {}


def kernel(L0_attn_x, L0_attn_Wq, L0_attn_Wk, L0_attn_Wv, L0_ffn_w_freq_real,
           L0_ffn_w_freq_imag, L0_ffn_bias, L1_attn_Wv, L1_attn_Wo, L1_attn_Wg,
           L1_attn_Wb, L1_ffn_router_w, L1_ffn_w1_all, L1_ffn_w2_all,
           _stop_after="all", _taps=()):
    key = (_stop_after, tuple(_taps))
    if key not in _BUILD_CACHE:
        _BUILD_CACHE.clear()
        _BUILD_CACHE[key] = build(stop_after=_stop_after, want_taps=_taps)
    p, env = _BUILD_CACHE[key]

    cs, fre, fim = _host_constants()
    ac = np.ascontiguousarray
    shared = dict(cs)
    shared["wqT"] = ac(L0_attn_Wq.T)
    shared["wkT"] = ac(L0_attn_Wk.T)
    shared["wvT"] = ac(L0_attn_Wv.T)
    shared["wre"] = ac(L0_ffn_w_freq_real)
    shared["wim"] = ac(L0_ffn_w_freq_imag)
    shared["bias_fm"] = ac(L0_ffn_bias.reshape(FC, 128).T)
    shared["wv1T"] = ac(L1_attn_Wv.T)
    woSW = ac(L1_attn_Wo.T.reshape(H, 64, D).transpose(1, 0, 2).reshape(64, H * D))
    shared["woSW"] = woSW
    shared["wgbT"] = ac(np.concatenate([L1_attn_Wg, L1_attn_Wb], axis=0).T)
    shared["rwT"] = ac(L1_ffn_router_w.T)
    shared["w1_all"] = ac(L1_ffn_w1_all)
    shared["w2_all"] = ac(L1_ffn_w2_all)

    in_maps = []
    for core in range(8):
        b, half = core // 2, core % 2
        m = dict(shared)
        xo = ac(L0_attn_x[b, half * TPC:(half + 1) * TPC, :].T)
        xr = ac(L0_attn_x[b, (1 - half) * TPC:(2 - half) * TPC, :].T)
        m["x_fm"] = np.concatenate([xo, xr], axis=1)
        m["s_gate"] = np.full((128, 1), float(half), np.float32)
        r0 = core * 128
        fr = fre[r0:r0 + 128, :]
        fi = fim[r0:r0 + 128, :]
        frT = ac(fr.T).reshape(FC, 128, 128).transpose(1, 0, 2).reshape(128, FC * 128)
        fiT = ac(fi.T).reshape(FC, 128, 128).transpose(1, 0, 2).reshape(128, FC * 128)
        m["c_freT"] = frT.astype(BFNP)
        m["c_fimT"] = fiT.astype(BFNP)
        m["c_fimnT"] = (-fiT).astype(BFNP)
        in_maps.append(m)

    res = run_bass_kernel_spmd(p.nc, in_maps, core_ids=list(range(8)))
    out = np.empty((B, T, D), np.float32)
    for core in range(8):
        b, half = core // 2, core % 2
        out[b, half * TPC:(half + 1) * TPC, :] = res.results[core]["out_fm"].T
    if _taps:
        tap_res = [{k: v for k, v in r.items() if k.startswith("tap_")}
                   for r in res.results]
        return out, tap_res
    return out


# version 03 (final)
# speedup vs baseline: 1.0339x; 1.0232x over previous
import os
import numpy as np
import ml_dtypes
import concourse.bass as bass
import concourse.mybir as mybir
from concourse.bass_utils import run_bass_kernel_spmd

F32 = mybir.dt.float32
BF16 = mybir.dt.bfloat16
AF = mybir.ActivationFunctionType
ALU = mybir.AluOpType
AX = mybir.AxisListType
BFNP = ml_dtypes.bfloat16

B, T, D = 4, 4096, 1024
H, DH = 16, 64
TPC = 2048
FC = 8
CH = 128
NCHUNK = TPC // CH
QT = 256
NQ = TPC // QT
NKT = TPC // 128
NF = 1.0 / 32.0
CSHIFT = -64.0
LNEPS = 1e-5

SBASE = 16512
A1 = SBASE
A2 = SBASE + 65568
A3 = SBASE + 131104
A4 = SBASE + 163872
A5 = SBASE + 196640
SOLV = A4

ENGS = ("tensor", "vector", "scalar", "gpsimd", "sync")


class Prog:
    def __init__(self):
        self.nc = bass.Bass("TRN2", target_bir_lowering=False, debug=False)
        self.sems = {}
        self.cnt = {}

    def sem(self, name):
        if name not in self.sems:
            self.sems[name] = self.nc.alloc_semaphore(name)
            self.cnt[name] = 0
        return self.sems[name]

    def inc(self, instr, name, k=1):
        instr.then_inc(self.sem(name), k)
        self.cnt[name] += k
        return self.cnt[name]

    def dma(self, eng, out, in_, name):
        i = eng.dma_start(out=out, in_=in_)
        self.inc(i, name, 16)
        return i

    def n(self, name):
        self.sem(name)
        return self.cnt[name]

    def fence(self, engs=ENGS):
        return [(f"s_{e}", self.n(f"s_{e}")) for e in engs]


class Sched:

    def __init__(self, p):
        self.p = p
        self.steps = []

    def add(self, eng, fn, deps=(), sem_waits=()):
        self.steps.append(dict(eng=eng, fn=fn, deps=list(deps),
                               sem_waits=list(sem_waits), seq=None))
        return len(self.steps) - 1

    def emit(self, blk):
        p = self.p
        counts = {e: 0 for e in ENGS}
        for s in self.steps:
            counts[s["eng"]] += 1
            s["seq"] = counts[s["eng"]]
        base = {e: p.n(f"s_{e}") for e in ENGS}
        for e in ENGS:
            mysteps = [s for s in self.steps if s["eng"] == e]
            if not mysteps:
                continue

            def body(eng, mysteps=mysteps, e=e):
                for s in mysteps:
                    for d in s["deps"]:
                        ds = self.steps[d]
                        eng.wait_ge(p.sem(f"s_{ds['eng']}"),
                                    base[ds["eng"]] + ds["seq"])
                    for (sn, v) in s["sem_waits"]:
                        eng.wait_ge(p.sem(sn), v)
                    instr = s["fn"](eng)
                    nm = type(instr.ins).__name__ if instr is not None else ""
                    if instr is None or "DMA" in nm or "Dma" in nm or "Collective" in nm:
                        instr = eng.nop()
                    p.inc(instr, f"s_{e}", 1)
            getattr(blk, e)(body)


def build(stop_after="all", want_taps=()):
    p = Prog()
    nc = p.nc
    STAGES = ["x1", "x2", "l1prod", "l1solve", "l1pass2", "l1o", "x3", "x4", "all"]
    lim = STAGES.index(stop_after) if stop_after in STAGES else len(STAGES) - 1

    def reach(stg):
        return STAGES.index(stg) <= lim

    dt_i = lambda nm, sh, dt=F32: nc.dram_tensor(nm, sh, dt, kind="ExternalInput").ap()
    d_x = dt_i("x_fm", [D, T])
    d_wqT = dt_i("wqT", [D, D])
    d_wkT = dt_i("wkT", [D, D])
    d_wvT = dt_i("wvT", [D, D])
    d_wre = dt_i("wre", [D, D])
    d_wim = dt_i("wim", [D, D])
    d_bias = dt_i("bias_fm", [128, FC])
    d_wv1T = dt_i("wv1T", [D, D])
    d_woSW = dt_i("woSW", [64, H * D])
    d_wgbT = dt_i("wgbT", [D, 32])
    d_rwT = dt_i("rwT", [D, 8])
    d_w1 = dt_i("w1_all", [8, D, D])
    d_w2 = dt_i("w2_all", [8, D, D])
    d_sgate = dt_i("s_gate", [128, 1])
    d_freT = dt_i("c_freT", [128, FC * 128], BF16)
    d_fimT = dt_i("c_fimT", [128, FC * 128], BF16)
    d_fimnT = dt_i("c_fimnT", [128, FC * 128], BF16)
    d_gre = dt_i("c_gre", [D, D], BF16)
    d_gimn = dt_i("c_gimn", [D, D], BF16)
    d_cpack = dt_i("c_pack", [128, 512], BF16)
    d_cind = dt_i("c_ind", [16, FC * 128], BF16)
    d_cinde = dt_i("c_inde", [8, 8 * 128], BF16)
    d_cselh = dt_i("c_selh", [16, H * 64], BF16)
    d_conesb = dt_i("c_onesb", [128, 132], BF16)
    d_onesrow = dt_i("c_onesrow", [1, 128], BF16)
    d_cfl = dt_i("c_fl", [128, 4])

    d_out = nc.dram_tensor("out_fm", [D, TPC], F32, kind="ExternalOutput").ap()
    taps = {}
    for tn in want_taps:
        if tn == "srow":
            taps[tn] = nc.dram_tensor("tap_srow", [1, TPC], F32,
                                      kind="ExternalOutput").ap()
        elif tn in ("acc", "kfm", "vtk0", "xbff_a", "xbff_b", "qt"):
            taps[tn] = nc.dram_tensor("tap_" + tn, [128, 16384], BF16,
                                      kind="ExternalOutput").ap()
        elif tn == "rbcd":
            taps[tn] = nc.dram_tensor("tap_rbcd", [128, TPC], F32,
                                      kind="ExternalOutput").ap()
        elif tn in ("rbc2", "mbc2"):
            taps[tn] = nc.dram_tensor("tap_" + tn, [128, TPC], F32,
                                      kind="ExternalOutput").ap()
        elif tn == "lnstat":
            taps[tn] = nc.dram_tensor("tap_lnstat", [128, 128], F32,
                                      kind="ExternalOutput").ap()
        elif tn == "lnrow":
            taps[tn] = nc.dram_tensor("tap_lnrow", [1, 2 * TPC], BF16,
                                      kind="ExternalOutput").ap()
        else:
            taps[tn] = nc.dram_tensor("tap_" + tn, [D, TPC],
                                      BF16 if tn in ("att", "l1o") else F32,
                                      kind="ExternalOutput").ap()

    cc_m_in = nc.dram_tensor("cc_m_in", [128, D], BF16)
    cc_m_out = nc.dram_tensor("cc_m_out", [8 * 128, D], BF16)
    cc_rb_in = nc.dram_tensor("cc_rb_in", [128, FC], BF16)
    cc_rb_out = nc.dram_tensor("cc_rb_out", [256, FC], BF16)
    cc_s_in = nc.dram_tensor("cc_s_in", [128, 8 * 64], F32)
    cc_s_out = nc.dram_tensor("cc_s_out", [256, 8 * 64], F32)
    _serk = {"kind": "ExternalOutput"} if os.environ.get("DBG_SER") else {}
    if os.environ.get("DBG_SER"):
        d_dbg_rk = nc.dram_tensor("dbg_rk", [128, FC * (TPC + 1)], BF16, kind="ExternalOutput")
        d_dbg_btk = nc.dram_tensor("dbg_btk", [128, NCHUNK * H], F32, kind="ExternalOutput")
        d_dbg_vtk = nc.dram_tensor("dbg_vtk", [128, NCHUNK * D], BF16, kind="ExternalOutput")
        d_dbg_nrm = nc.dram_tensor("dbg_nrm", [16, TPC], F32, kind="ExternalOutput")
    d_serP = nc.dram_tensor("ser_pt", [NCHUNK, 128, 8 * 64], BF16, **_serk)
    d_serQ = nc.dram_tensor("ser_q", [NCHUNK, 128, 8 * 64], F32, **_serk)
    d_serW = nc.dram_tensor("ser_yw", [NCHUNK, 128, 8 * 128], BF16, **_serk)
    d_serO = nc.dram_tensor("ser_yo", [NCHUNK, 128, 8 * 128], BF16, **_serk)

    at = nc.alloc_sbuf_tensor_at
    s_xbff = at("xbf_full", [128, FC * T], BF16, offset=A1)
    s_Mall = at("M_all", [128, FC * D], BF16, offset=A1)
    s_rk = at("rk", [128, FC * (TPC + 1)], BF16, offset=A1)
    s_wv1t = at("wv1t", [128, FC * D], BF16, offset=A1 + 32800)
    s_wgbt = at("wgbt", [128, FC * 32], BF16, offset=A1 + 49216)
    s_x2sq = at("x2sq", [128, 512], BF16, offset=A1 + 49792)
    s_nrm = at("nrm", [16, TPC], F32, offset=A1 + 50816)
    s_nrmb = at("nrm_bf", [16, TPC], BF16, offset=A1 + 59008)
    s_brow = at("brow", [16, TPC], BF16, offset=A1 + 50816)
    s_woSW = at("woSW", [64, H * D], BF16, offset=A1 + 32800)
    s_moew = [at(f"moew{i}", [128, FC * D], BF16, offset=A1 + i * 16384)
              for i in range(4)]
    s_kfm = at("kfm", [128, FC * TPC], BF16, offset=A2)
    s_vtk0 = at("vtk0", [128, NKT * D], BF16, offset=A2 + 32768)
    s_xres = at("xres", [128, FC * TPC], F32, offset=A2)
    s_fre = at("fre", [128, FC * 128], BF16, offset=A3)
    s_fim = at("fim", [128, FC * 128], BF16, offset=A3 + 2048)
    s_fimn = at("fimn", [128, FC * 128], BF16, offset=A3 + 4096)
    s_areT = at("areT", [128, FC * 128], BF16, offset=A3 + 6144)
    s_aimT = at("aimT", [128, FC * 128], BF16, offset=A3 + 8192)
    s_mmine = at("m_mine", [128, D], BF16, offset=A3 + 10240)
    s_acc = at("acc_att", [128, FC * TPC], BF16, offset=A3)
    s_rbc = at("rbc", [128, TPC], F32, offset=A3)
    s_mbc = at("mbc", [128, TPC], F32, offset=A3 + 8192)
    s_zbf = at("zbf", [128, TPC], BF16, offset=A3 + 16384)
    s_zsq = at("zsq", [128, TPC], BF16, offset=A3 + 20480)
    s_rrow = at("rrow", [1, 2 * TPC], BF16, offset=A3 + 24576)
    s_vtk1 = at("vtk1", [128, NCHUNK * D], BF16, offset=A3)
    s_htile = at("htile", [128, FC * 512], BF16, offset=A3)
    s_wexp = at("wexp", [128, 512], BF16, offset=A3 + 8192)
    s_wgtb = at("wgt_rows", [8, TPC], BF16, offset=A3 + 9216)
    s_rtk = at("rtk", [128, 3 * 8], F32, offset=A3 + 13312)
    s_mp = at("mp", [128, 24], F32, offset=A3 + 13408)
    s_wgtk = at("wgtk", [128, 16], BF16, offset=A3 + 13504)
    s_wslabA = at("wslabA", [128, FC * D], BF16, offset=A4)
    s_wslabB = at("wslabB", [128, FC * D], BF16, offset=A4 + 16384)
    s_qtile = at("qtile", [128, FC * QT], BF16, offset=A4)
    s_rsb = at("recipS_bf", [1, TPC], BF16, offset=A4 + 12288)
    s_recipSbc = at("recipSbc", [128, TPC], F32, offset=A4 + 16384)
    s_ptile = at("ptile", [128, 16 * QT], BF16, offset=A4 + 4096)
    s_xbf = at("xbf", [128, FC * TPC], BF16, offset=A4)
    s_fg = at("fg", [128, 2 * 14 * 128], BF16, offset=SOLV)
    s_X = at("Xbuf", [128, 2 * 2 * 128], BF16, offset=SOLV + 7168)
    s_ktk = at("ktk", [128, 2 * 64], BF16, offset=SOLV + 8192)
    s_kx = at("kx", [64, 2 * 128], F32, offset=SOLV + 8448)
    s_yxt = at("yxt", [128, 2 * 128], BF16, offset=SOLV + 9472)
    s_yt = at("yt", [128, 2 * 128], BF16, offset=SOLV + 9984)
    s_stgP = at("stgP", [128, 8 * 64], BF16, offset=SOLV + 10496)
    s_stgQ = at("stgQ", [128, 8 * 64], F32, offset=SOLV + 11520)
    s_stgW = at("stgW", [128, 8 * 128], BF16, offset=SOLV + 13568)
    s_stgO = at("stgO", [128, 8 * 128], BF16, offset=SOLV + 15616)
    s_och = at("ochunk", [64, H * 128], BF16, offset=SOLV + 17664)
    s_gexp = at("gexp", [64, H * 128], BF16, offset=SOLV + 21760)
    s_p2P = [at(f"p2P{i}", [128, 8 * 64], BF16, offset=SOLV + i * 1024)
             for i in range(2)]
    s_p2Q = [at(f"p2Q{i}", [128, 8 * 64], F32, offset=SOLV + 2048 + i * 2048)
             for i in range(2)]
    s_p2W = [at(f"p2W{i}", [128, 8 * 128], BF16, offset=SOLV + 6144 + i * 2048)
             for i in range(2)]
    s_p2O = [at(f"p2O{i}", [128, 8 * 128], BF16, offset=SOLV + 10240 + i * 2048)
             for i in range(2)]
    s_ident = at("ident", [128, 128], BF16, offset=A5)
    s_masksl = at("mask_sl", [128, 128], BF16, offset=A5 + 256)
    s_masksui = at("mask_sui", [128, 128], BF16, offset=A5 + 512)
    s_swap = at("swapm", [128, 128], BF16, offset=A5 + 768)
    s_onesb = at("onesb", [128, 132], BF16, offset=A5 + 1024)
    s_onesrow = at("onesrow", [1, 128], BF16, offset=A5 + 1312)
    s_cdyn = at("cdyn", [16, 1024], BF16, offset=A5 + 1568)
    DYN = A5 + 3616
    s_srow = at("srow", [1, TPC], F32, offset=DYN)
    s_Spair = at("Spair", [128, 8 * 64], F32, offset=DYN)
    s_Sb = at("Sb", [128, 8 * 64], BF16, offset=DYN + 2048)
    s_Sin = at("Sin", [128, 8 * 64], F32, offset=DYN + 3072)
    s_btk = at("btk", [128, NCHUNK * H], F32, offset=DYN + 5120)
    s_stat = at("stat_tk", [128, 16 * 8], F32, offset=DYN + 6144)
    s_statb = at("statb_tk", [128, 32], BF16, offset=DYN + 6656)
    s_grow = at("g_rows", [16, TPC], BF16, offset=A5 + 11808)
    s_sgt = at("sgt", [128, 1], F32, offset=A5 + 15904)
    s_biasf = at("biasf", [128, FC], F32, offset=A5 + 15936)
    s_rbnd = at("rbnd", [128, FC], BF16, offset=A5 + 15968)
    s_cfl = at("cfl", [128, 4], F32, offset=A5 + 16000)

    pb = [nc.alloc_psum_tensor(f"pb{i}", [128, 512], F32) for i in range(7)]
    pT = nc.alloc_psum_tensor("pTt", [128, 1024], BF16)

    ONEC = s_onesb[:, 128:129]

    def blk1(c):
        return s_onesb[:, c * 16:(c + 1) * 16]

    def xb(c, t0, n):
        return s_xbff[:, c * T + t0: c * T + t0 + n]

    def st(buf, c, t0, n):
        return buf[:, c * TPC + t0: c * TPC + t0 + n]

    def rksl(c, col0, n):
        return s_rk[:, c * (TPC + 1) + col0: c * (TPC + 1) + col0 + n]

    with nc.Block() as blk:
        def p0_sync(sync):
            for (dst, src) in [(s_ident[:], d_cpack[:, 0:128]),
                               (s_masksl[:], d_cpack[:, 128:256]),
                               (s_masksui[:], d_cpack[:, 256:384]),
                               (s_swap[:], d_cpack[:, 384:512]),
                               (s_onesb[:], d_conesb), (s_onesrow[:], d_onesrow),
                               (s_fre[:], d_freT), (s_fim[:], d_fimT),
                               (s_fimn[:], d_fimnT), (s_sgt[:], d_sgate),
                               (s_biasf[:], d_bias), (s_cfl[:], d_cfl)]:
                p.dma(sync, dst, src, "dma_c")
        blk.sync(p0_sync)

        def p0_gp(gp):
            for c in range(FC):
                p.dma(gp, xb(c, 0, T), d_x[c * 128:(c + 1) * 128, :], "dma_x")
            for c in range(FC):
                p.dma(gp, s_kfm[:, c * D:(c + 1) * D],
                      d_wre[c * 128:(c + 1) * 128, :], "dma_w1")
            for c in range(FC):
                p.dma(gp, s_kfm[:, 8192 + c * D: 8192 + (c + 1) * D],
                      d_wim[c * 128:(c + 1) * 128, :], "dma_w1")
            for c in range(FC):
                p.dma(gp, s_vtk0[:, c * D:(c + 1) * D],
                      d_gre[c * 128:(c + 1) * 128, :], "dma_w1")
            for c in range(FC):
                p.dma(gp, s_vtk0[:, 8192 + c * D: 8192 + (c + 1) * D],
                      d_gimn[c * 128:(c + 1) * 128, :], "dma_w1")
            for c in range(FC):
                p.dma(gp, s_wslabA[:, c * D:(c + 1) * D],
                      d_wkT[c * 128:(c + 1) * 128, :], "dma_wk0")
            for c in range(FC):
                p.dma(gp, s_wslabB[:, c * D:(c + 1) * D],
                      d_wqT[c * 128:(c + 1) * 128, :], "dma_wq")
        blk.gpsimd(p0_gp)

        def wre_t(dc, kc):
            return s_kfm[:, dc * D + kc * 128: dc * D + (kc + 1) * 128]

        def wim_t(dc, kc):
            return s_kfm[:, 8192 + dc * D + kc * 128: 8192 + dc * D + (kc + 1) * 128]

        sc = Sched(p)
        W_RDY = [("dma_w1", p.n("dma_w1")), ("dma_c", p.n("dma_c"))]
        evs = []
        for kc in range(FC):
            def mk_a(eng, kc=kc, im=False):
                bank = pb[1] if im else pb[0]
                for dc in range(FC):
                    eng.matmul(bank[:, 0:128],
                               (wim_t if im else wre_t)(dc, kc),
                               s_fre[:, dc * 128:(dc + 1) * 128],
                               start=(dc == 0), stop=False)
                for dc in range(FC):
                    i = eng.matmul(bank[:, 0:128],
                                   (wre_t if im else wim_t)(dc, kc),
                                   (s_fim if im else s_fimn)[:, dc * 128:(dc + 1) * 128],
                                   start=False, stop=(dc == FC - 1))
                return i
            i_re = sc.add("tensor", lambda e, kc=kc: mk_a(e, kc, False),
                          deps=([evs[-1][0]] if evs else []),
                          sem_waits=(W_RDY if kc == 0 else ()))
            e_re = sc.add("vector", lambda e, kc=kc: e.tensor_copy(
                s_areT[:, kc * 128:(kc + 1) * 128], pb[0][:, 0:128]), deps=[i_re])
            i_im = sc.add("tensor", lambda e, kc=kc: mk_a(e, kc, True),
                          deps=([evs[-1][1]] if evs else []))
            e_im = sc.add("vector", lambda e, kc=kc: e.tensor_copy(
                s_aimT[:, kc * 128:(kc + 1) * 128], pb[1][:, 0:128]), deps=[i_im])
            evs.append((e_re, e_im))
        mevs = []
        for half in range(2):
            def mk_m(eng, half=half):
                for kc in range(FC):
                    eng.matmul(pb[2][:, 0:512], s_areT[:, kc * 128:(kc + 1) * 128],
                               s_vtk0[:, kc * D + half * 512: kc * D + half * 512 + 512],
                               start=(kc == 0), stop=False)
                for kc in range(FC):
                    i = eng.matmul(pb[2][:, 0:512], s_aimT[:, kc * 128:(kc + 1) * 128],
                                   s_vtk0[:, 8192 + kc * D + half * 512: 8192 + kc * D + half * 512 + 512],
                                   start=False, stop=(kc == FC - 1))
                return i
            im = sc.add("tensor", mk_m,
                        deps=([evs[-1][0], evs[-1][1]] if half == 0 else [mevs[0]]))
            ev = sc.add("vector", lambda e, half=half: e.tensor_copy(
                s_mmine[:, half * 512: half * 512 + 512], pb[2][:, 0:512]), deps=[im])
            mevs.append(ev)

        def ship_m(eng):
            i = eng.dma_start(out=cc_m_in[:, :], in_=s_mmine[:])
            p.inc(i, "dma_cc", 16)
            return i
        dship = sc.add("gpsimd", ship_m, deps=mevs)

        def do_ccm(eng):
            eng.wait_ge(p.sem("dma_cc"), p.n("dma_cc"))
            i = eng.collective_compute(
                "AllGather", ALU.bypass,
                replica_groups=[[0, 1, 2, 3, 4, 5, 6, 7]],
                ins=[cc_m_in.ap().opt()], outs=[cc_m_out.ap().opt()])
            p.inc(i, "ccm_flag", 1)
            return i
        sc.add("gpsimd", do_ccm, deps=[dship])
        sc.emit(blk)
        CCM_DONE = [("ccm_flag", 1)]

        prev_tail = {}
        for half in range(2):
            sc = Sched(p)
            kv_evs = []
            if half == 1:
                def ld_wk1(eng):
                    for w in prev_tail["attend"] + prev_tail["pb45"]:
                        eng.wait_ge(p.sem(w[0]), w[1])
                    for c in range(FC):
                        i = eng.dma_start(out=s_wslabA[:, c * D:(c + 1) * D],
                                          in_=d_wkT[c * 128:(c + 1) * 128, :])
                        p.inc(i, "dma_wk1", 16)
                    return i
                sc.add("gpsimd", ld_wk1)
            for dc in range(FC):
                for t4 in range(4):
                    j = dc * 4 + t4

                    def mk_k(eng, dc=dc, t4=t4, j=j, half=half):
                        bank = pb[4 + j % 2]
                        for c in range(FC):
                            i = eng.matmul(
                                bank[:, 0:512],
                                s_wslabA[:, c * D + dc * 128: c * D + (dc + 1) * 128],
                                xb(c, half * TPC + t4 * 512, 512),
                                start=(c == 0), stop=(c == FC - 1))
                        return i

                    def ev_k(eng, dc=dc, t4=t4, j=j):
                        return eng.tensor_copy(
                            s_kfm[:, dc * TPC + t4 * 512: dc * TPC + t4 * 512 + 512],
                            pb[4 + j % 2][:, 0:512])
                    w = list(prev_tail.get("pb45", []))
                    if j == 0:
                        w += [("dma_x", 16 * FC), (f"dma_wk{half}", 16 * FC)]
                        if half == 1:
                            w += prev_tail["attend"]
                    im = sc.add("tensor", mk_k,
                                deps=([kv_evs[-2]] if len(kv_evs) >= 2 else []),
                                sem_waits=(w if j <= 1 else ()))
                    kv_evs.append(sc.add("vector", ev_k, deps=[im]))
            k_done = list(kv_evs[-2:])
            def ld_wv(eng, half=half):
                for c in range(FC):
                    i = eng.dma_start(out=s_wslabA[:, c * D:(c + 1) * D],
                                      in_=d_wvT[c * 128:(c + 1) * 128, :])
                    p.inc(i, f"dma_wv{half}", 16)
                return i
            sc.add("gpsimd", ld_wv, deps=k_done)
            for tt in range(NKT):
                for vh in range(2):
                    j = tt * 2 + vh

                    def mk_v(eng, tt=tt, vh=vh, j=j, half=half):
                        bank = pb[4 + j % 2]
                        for c in range(FC):
                            i = eng.matmul(
                                bank[:, 0:512],
                                xb(c, half * TPC + tt * 128, 128),
                                s_wslabA[:, c * D + vh * 512: c * D + vh * 512 + 512],
                                start=(c == 0), stop=(c == FC - 1))
                        return i

                    def ev_v(eng, tt=tt, vh=vh, j=j):
                        return eng.tensor_copy(
                            s_vtk0[:, tt * D + vh * 512: tt * D + vh * 512 + 512],
                            pb[4 + j % 2][:, 0:512])
                    w = [(f"dma_wv{half}", 16 * FC)] if j == 0 else ()
                    im = sc.add("tensor", mk_v, deps=[kv_evs[-2]], sem_waits=w)
                    kv_evs.append(sc.add("vector", ev_v, deps=[im]))
            vk_done = list(kv_evs[-2:])

            prev = None
            for qi in range(NQ):
                qevs = []
                for g in range(4):
                    def mk_q(eng, qi=qi, g=g):
                        bank = pb[4 + g % 2]
                        for k in range(2):
                            dc = g * 2 + k
                            for c in range(FC):
                                i = eng.matmul(
                                    bank[:, k * 256: k * 256 + 256],
                                    s_wslabB[:, c * D + dc * 128: c * D + (dc + 1) * 128],
                                    xb(c, qi * QT, QT),
                                    start=(c == 0), stop=(c == FC - 1))
                        return i

                    def ev_q(eng, qi=qi, g=g):
                        return eng.tensor_copy(
                            s_qtile[:, g * 512:(g + 1) * 512], pb[4 + g % 2][:, 0:512])
                    w = [("dma_wq", 16 * FC)] if (half == 0 and qi == 0 and g == 0) else ()
                    if qi == 0:
                        deps = vk_done if g < 2 else [qevs[g - 2]]
                    else:
                        deps = [prev["exp"][14 + g % 2]] if g < 2 else [qevs[g - 2]]
                    im = sc.add("tensor", mk_q, deps=deps, sem_waits=w)
                    qevs.append(sc.add("vector", ev_q, deps=[im]))
                exp_steps, log_steps = [], []
                for kt in range(NKT):
                    def mk_log(eng, qi=qi, kt=kt):
                        bank = pb[4 + kt % 2]
                        for c in range(FC):
                            i = eng.matmul(
                                bank[:, 0:256],
                                s_kfm[:, c * TPC + kt * 128: c * TPC + (kt + 1) * 128],
                                s_qtile[:, c * QT:(c + 1) * QT],
                                start=(c == 0), stop=(c == FC - 1))
                        return i
                    deps = list(qevs[2:]) if kt < 2 else [exp_steps[kt - 2]]
                    ilog = sc.add("tensor", mk_log, deps=deps)

                    def mk_exp(eng, qi=qi, kt=kt):
                        return eng.activation(
                            s_ptile[:, kt * QT: kt * QT + QT],
                            pb[4 + kt % 2][:, 0:256], AF.Exp,
                            bias=s_cfl[:, 0:1], scale=1.0)
                    deps = [ilog]
                    if kt == 0 and prev:
                        deps += [prev["pv"][-1]]
                    iexp = sc.add("scalar", mk_exp, deps=deps,
                                  sem_waits=([("dma_c", p.n("dma_c"))]
                                             if (half == 0 and qi == 0 and kt == 0) else ()))
                    exp_steps.append(iexp)
                pv_steps = []

                def mk_ssum(eng, qi=qi):
                    for kt in range(NKT):
                        i = eng.matmul(pb[6][0:1, 0:256], ONEC,
                                       s_ptile[:, kt * QT: kt * QT + QT],
                                       start=(kt == 0), stop=(kt == NKT - 1))
                    return i
                pv_steps.append(sc.add("tensor", mk_ssum,
                                       deps=[exp_steps[-1]] +
                                       ([prev["sevac"][4]] if prev else [])))
                for vc in range(FC):
                    def mk_pv(eng, qi=qi, vc=vc):
                        for kt in range(NKT):
                            i = eng.matmul(
                                pb[vc // 2][:, (vc % 2) * 256: (vc % 2) * 256 + 256],
                                s_vtk0[:, kt * D + vc * 128: kt * D + (vc + 1) * 128],
                                s_ptile[:, kt * QT: kt * QT + QT],
                                start=(kt == 0), stop=(kt == NKT - 1))
                        return i
                    deps = [exp_steps[-1]] + ([prev["sevac"][vc // 2]] if prev else [])
                    pv_steps.append(sc.add("tensor", mk_pv, deps=deps))
                sevac = []
                for g in range(4):
                    def ev_acc(eng, qi=qi, g=g, half=half):
                        for k in range(2):
                            vc = g * 2 + k
                            dst = s_acc[:, vc * TPC + qi * QT: vc * TPC + qi * QT + QT]
                            src = pb[g][:, k * 256: k * 256 + 256]
                            if half == 0:
                                i = eng.tensor_copy(dst, src)
                            else:
                                i = eng.tensor_tensor(dst, dst, src, ALU.add)
                        return i
                    sevac.append(sc.add("vector", ev_acc, deps=[pv_steps[-1]]))

                def ev_s(eng, qi=qi, half=half):
                    dst = s_srow[0:1, qi * QT: qi * QT + QT]
                    if half == 0:
                        return eng.tensor_copy(dst, pb[6][0:1, 0:256])
                    return eng.tensor_tensor(dst, dst, pb[6][0:1, 0:256], ALU.add)
                sevac.append(sc.add("vector", ev_s, deps=[pv_steps[-1]]))
                prev = dict(exp=exp_steps, pv=pv_steps, sevac=sevac)
            sc.emit(blk)
            prev_tail = dict(pb45=p.fence(("scalar",)), attend=p.fence(("vector",)))

        ATT_DONE = p.fence(("vector", "tensor", "scalar"))

        def dbg_tap(sync):
            for w in ATT_DONE:
                sync.wait_ge(p.sem(w[0]), w[1])
            if "acc" in taps:
                p.dma(sync, taps["acc"], s_acc[:], "dma_tap")
            if "srow" in taps:
                p.dma(sync, taps["srow"], s_srow[:], "dma_tap")
            if "kfm" in taps:
                p.dma(sync, taps["kfm"], s_kfm[:], "dma_tap")
            if "vtk0" in taps:
                p.dma(sync, taps["vtk0"], s_vtk0[:], "dma_tap")
            if "xbff_a" in taps:
                p.dma(sync, taps["xbff_a"], s_xbff[:, 0:16384], "dma_tap")
            if "xbff_b" in taps:
                p.dma(sync, taps["xbff_b"], s_xbff[:, 16384:32768], "dma_tap")
            if "qt" in taps:
                p.dma(sync, taps["qt"][:, 0:FC * QT], s_qtile[:], "dma_tap")
        if any(t in taps for t in ("acc", "srow", "kfm", "vtk0", "xbff_a", "xbff_b", "qt")):
            blk.sync(dbg_tap)
        TAPW = [("dma_tap", p.n("dma_tap"))] if p.n("dma_tap") else []

        def ld_x0(sync):
            for w in ATT_DONE:
                sync.wait_ge(p.sem(w[0]), w[1])
            for c in range(FC):
                p.dma(sync, st(s_xres, c, 0, TPC), d_x[c * 128:(c + 1) * 128, 0:TPC],
                      "dma_x0")
        blk.sync(ld_x0)

        sc = Sched(p)
        r0 = sc.add("vector", lambda e: e.reciprocal(s_srow[:], s_srow[:]),
                    sem_waits=ATT_DONE + TAPW)
        r1 = sc.add("vector", lambda e: e.tensor_scalar_mul(s_srow[:], s_srow[:], NF),
                    deps=[r0])
        r2 = sc.add("scalar", lambda e: e.copy(s_rsb[:], s_srow[:]), deps=[r1])
        bexp = []
        for t4 in range(4):
            def mk_b(eng, t4=t4):
                return eng.matmul(pb[4 + t4 % 2][:, 0:512],
                                  s_onesrow[0:1, 0:128],
                                  s_rsb[0:1, t4 * 512:(t4 + 1) * 512],
                                  start=True, stop=True)
            im = sc.add("tensor", mk_b, deps=[r2] + ([bexp[t4 - 2]] if t4 >= 2 else []))
            bexp.append(sc.add("vector", lambda e, t4=t4: e.tensor_copy(
                s_recipSbc[:, t4 * 512:(t4 + 1) * 512], pb[4 + t4 % 2][:, 0:512]),
                deps=[im]))
        zsteps = []
        for c in range(FC):
            a1 = sc.add("vector", lambda e, c=c: e.tensor_tensor(
                st(s_acc, c, 0, TPC), st(s_acc, c, 0, TPC), s_recipSbc[:], ALU.mult),
                deps=[bexp[-1]] if c == 0 else [],
                sem_waits=(TAPW if c == 0 else ()))
            a2 = sc.add("vector", lambda e, c=c: e.tensor_tensor(
                st(s_xres, c, 0, TPC), st(s_xres, c, 0, TPC), st(s_acc, c, 0, TPC),
                ALU.add), deps=[a1],
                sem_waits=[("dma_x0", 16 * FC)] if c == 0 else ())
            zsteps.append(a2)
        sc.emit(blk)

        Z_TAPW = []
        if "z" in taps or "rbcd" in taps:
            zf = p.fence(("vector",))

            def dbg2(sync):
                for w in zf:
                    sync.wait_ge(p.sem(w[0]), w[1])
                if "z" in taps:
                    for c in range(FC):
                        p.dma(sync, taps["z"][c * 128:(c + 1) * 128, :],
                              st(s_xres, c, 0, TPC), "dma_tap")
                if "rbcd" in taps:
                    p.dma(sync, taps["rbcd"][0:128, :], s_recipSbc[:], "dma_tap")
            blk.sync(dbg2)
            Z_TAPW = [("dma_tap", p.n("dma_tap"))]

        lndbg = {k: taps[k] for k in ("lnstat", "lnrow", "rbc2", "mbc2") if k in taps}
        emit_ln(p, blk, locals(), dep_fence=p.fence(("vector",)) + Z_TAPW,
                dbg=lndbg or None)
        if want_taps and "x1" in taps:
            emit_tap(p, blk, locals(), taps["x1"], s_xres, F32)
        if stop_after == "x1":
            emit_final(p, blk, locals(), s_xres, stop_early=True)
            return p, dict(locals())

        emit_ffn(p, blk, locals())
        if want_taps and "x2" in taps:
            emit_tap(p, blk, locals(), taps["x2"], s_xres, F32)
        if stop_after == "x2":
            emit_final(p, blk, locals(), s_xres, stop_early=True)
            return p, dict(locals())

        sub = {"l1prod": "prod", "l1solve": "solve", "l1pass2": "pass2"}.get(stop_after, "all")
        emit_l1(p, blk, locals(), taps if "l1o" in taps else None, substage=sub)
        if stop_after in ("l1prod", "l1solve", "l1pass2"):
            emit_final(p, blk, locals(), s_xres, stop_early=True)
            return p, dict(locals())
        if stop_after == "l1o":
            emit_final(p, blk, locals(), s_xres, stop_early=True)
            return p, dict(locals())
        if want_taps and "x3" in taps:
            emit_tap(p, blk, locals(), taps["x3"], s_xres, F32)
        if stop_after == "x3":
            emit_final(p, blk, locals(), s_xres, stop_early=True)
            return p, dict(locals())

        emit_moe(p, blk, locals())
        if want_taps and "x4" in taps:
            emit_tap(p, blk, locals(), taps["x4"], s_xres, F32)

        emit_final(p, blk, locals(), s_xres, stop_early=False)
    return p, dict(locals())


class NS:
    def __init__(self, d):
        self.__dict__.update(d)


def emit_ln(p, blk, env, dep_fence, dbg=None):
    g = NS(env)
    sc = Sched(p)
    ST = g.s_stat
    z0 = sc.add("vector", lambda e: e.memset(ST[:, 0:32], 0.0), sem_waits=dep_fence)
    tprev = None
    eprev = z0
    for c in range(FC):
        s1 = sc.add("scalar", lambda e, c=c: e.copy(
            g.s_zbf[:], g.st(g.s_xres, c, 0, TPC)),
            deps=([] if tprev is None else [tprev]),
            sem_waits=(dep_fence if c == 0 else ()))
        s2 = sc.add("scalar", lambda e, c=c: e.square(
            g.s_zsq[:], g.st(g.s_xres, c, 0, TPC)))

        def mk_sum(eng, c=c):
            for tt in range(16):
                eng.matmul(g.pb[4][:, tt:tt + 1],
                           g.s_zbf[:, tt * 128:(tt + 1) * 128], g.ONEC,
                           start=True, stop=True)
            for tt in range(16):
                i = eng.matmul(g.pb[4][:, 16 + tt:17 + tt],
                               g.s_zsq[:, tt * 128:(tt + 1) * 128], g.ONEC,
                               start=True, stop=True)
            return i
        tprev = sc.add("tensor", mk_sum, deps=[s1, s2, eprev],
                       sem_waits=(dep_fence if c == 0 else ()))
        eprev = sc.add("vector", lambda e, c=c: e.tensor_tensor(
            ST[:, 0:32], ST[:, 0:32], g.pb[4][:, 0:32], ALU.add), deps=[tprev])
    v = eprev
    v = sc.add("vector", lambda e: e.tensor_scalar_mul(ST[:, 32:48], ST[:, 0:16], 1.0 / D), deps=[v])
    v = sc.add("vector", lambda e: e.tensor_scalar_mul(ST[:, 48:64], ST[:, 16:32], 1.0 / D), deps=[v])
    v = sc.add("vector", lambda e: e.tensor_mul(ST[:, 64:80], ST[:, 32:48], ST[:, 32:48]), deps=[v])
    v = sc.add("vector", lambda e: e.tensor_sub(ST[:, 48:64], ST[:, 48:64], ST[:, 64:80]), deps=[v])
    v = sc.add("vector", lambda e: e.tensor_scalar_add(ST[:, 48:64], ST[:, 48:64], LNEPS), deps=[v])
    sq = sc.add("scalar", lambda e: e.sqrt(ST[:, 64:80], ST[:, 48:64]), deps=[v])
    v = sc.add("vector", lambda e: e.reciprocal(ST[:, 80:96], ST[:, 64:80]), deps=[sq])
    vb = sc.add("vector", lambda e: e.tensor_copy(g.s_statb[:, 0:16], ST[:, 80:96]), deps=[v])
    vb = sc.add("vector", lambda e: e.tensor_copy(g.s_statb[:, 16:32], ST[:, 32:48]), deps=[vb])
    vr = vb
    for grp in range(4):
        def mk_tr(eng, grp=grp):
            for k in range(8):
                i = eng.transpose(g.pT[0:1, k * 128:(k + 1) * 128],
                                  g.s_statb[:, grp * 8 + k: grp * 8 + k + 1],
                                  g.s_ident[:])
            return i
        ttg = sc.add("tensor", mk_tr, deps=[vr])

        def mk_rows(eng, grp=grp):
            for k in range(8):
                col = grp * 8 + k
                dst = (g.s_rrow[0:1, col * 128:(col + 1) * 128] if col < 16 else
                       g.s_rrow[0:1, TPC + (col - 16) * 128: TPC + (col - 15) * 128])
                i = eng.tensor_copy(dst, g.pT[0:1, k * 128:(k + 1) * 128])
            return i
        vr = sc.add("vector", mk_rows, deps=[ttg])
    bsteps = []
    for row, dst in ((0, g.s_rbc), (1, g.s_mbc)):
        for t4 in range(4):
            j = row * 4 + t4

            def mk_b(eng, row=row, t4=t4, j=j):
                return eng.matmul(g.pb[4 + j % 2][:, 0:512],
                                  g.s_onesrow[0:1, 0:128],
                                  g.s_rrow[0:1, row * TPC + t4 * 512: row * TPC + (t4 + 1) * 512],
                                  start=True, stop=True)
            deps = [vr] if j < 2 else [bsteps[j - 2]]
            im = sc.add("tensor", mk_b, deps=deps)
            bsteps.append(sc.add("vector", lambda e, dst=dst, t4=t4, j=j: e.tensor_copy(
                dst[:, t4 * 512:(t4 + 1) * 512], g.pb[4 + j % 2][:, 0:512]), deps=[im]))
    dbg_w = []
    if dbg:
        def dbg_ln(sync):
            for st_ in [bsteps[-1], bsteps[3]]:
                pass
            sync.wait_ge(p.sem("s_vector"), p.n("s_vector"))
            if "lnstat" in dbg:
                p.dma(sync, dbg["lnstat"], g.s_stat[:], "dma_tap")
            if "lnrow" in dbg:
                p.dma(sync, dbg["lnrow"], g.s_rrow[:], "dma_tap")
            if "rbc2" in dbg:
                p.dma(sync, dbg["rbc2"], g.s_rbc[:], "dma_tap")
            if "mbc2" in dbg:
                p.dma(sync, dbg["mbc2"], g.s_mbc[:], "dma_tap")
        dbg_w = [True]
    fin = []
    aprev = None
    for c in range(FC):
        a1 = sc.add("vector", lambda e, c=c: e.tensor_sub(
            g.st(g.s_xres, c, 0, TPC), g.st(g.s_xres, c, 0, TPC), g.s_mbc[:]),
            deps=([bsteps[7]] if c == 0 else [aprev]))
        a2 = sc.add("vector", lambda e, c=c: e.tensor_mul(
            g.st(g.s_xres, c, 0, TPC), g.st(g.s_xres, c, 0, TPC), g.s_rbc[:]),
            deps=[a1] + ([bsteps[3]] if c == 0 else []))
        aprev = a2
        fin.append(sc.add("scalar", lambda e, c=c: e.copy(
            g.st(g.s_xbf, c, 0, TPC), g.st(g.s_xres, c, 0, TPC)), deps=[a2]))
    sc.emit(blk)
    if dbg:
        blk.sync(dbg_ln)


def emit_tap(p, blk, env, tap_ap, src, dt):
    g = NS(env)
    f = p.fence()

    def tap(sync):
        for w in f:
            sync.wait_ge(p.sem(w[0]), w[1])
        for c in range(FC):
            p.dma(sync, tap_ap[c * 128:(c + 1) * 128, :], g.st(src, c, 0, TPC),
                  "dma_tap")
    blk.sync(tap)


def emit_final(p, blk, env, src, stop_early):
    g = NS(env)
    f = p.fence()

    def fin(sync):
        for w in f:
            sync.wait_ge(p.sem(w[0]), w[1])
        for c in range(FC):
            p.dma(sync, g.d_out[c * 128:(c + 1) * 128, :], g.st(src, c, 0, TPC),
                  "dma_out")
        sync.wait_ge(p.sem("dma_out"), p.n("dma_out"))
        if p.n("dma_tap"):
            sync.wait_ge(p.sem("dma_tap"), p.n("dma_tap"))
        if p.n("dma_dbg"):
            sync.wait_ge(p.sem("dma_dbg"), p.n("dma_dbg"))
    blk.sync(fin)


def emit_ffn(p, blk, env):
    g = NS(env)

    def ld_m(sync):
        for w in env["CCM_DONE"]:
            sync.wait_ge(p.sem(w[0]), w[1])
        for ic in range(FC):
            p.dma(sync, g.s_Mall[:, ic * D:(ic + 1) * D],
                  g.cc_m_out[ic * 128:(ic + 1) * 128, :], "dma_m")
    blk.sync(ld_m)

    sc = Sched(p)
    F_X1 = p.fence(("scalar",))
    evs = []
    for oc in range(FC):
        for t4 in range(4):
            j = oc * 4 + t4

            def mk(eng, oc=oc, t4=t4, j=j):
                bank = g.pb[j % 4]
                for ic in range(FC):
                    i = eng.matmul(
                        bank[:, 0:512],
                        g.s_Mall[:, ic * D + oc * 128: ic * D + (oc + 1) * 128],
                        g.st(g.s_xbf, ic, t4 * 512, 512),
                        start=(ic == 0), stop=(ic == FC - 1))
                return i

            def ev(eng, oc=oc, t4=t4, j=j):
                return eng.scalar_tensor_tensor(
                    g.st(g.s_xres, oc, t4 * 512, 512),
                    g.pb[j % 4][:, 0:512], g.s_biasf[:, oc:oc + 1],
                    g.st(g.s_xres, oc, t4 * 512, 512), ALU.add, ALU.add)
            w = [("dma_m", p.n("dma_m"))] + F_X1 if j == 0 else ()
            im = sc.add("tensor", mk, deps=([evs[j - 4]] if j >= 4 else []),
                        sem_waits=w)
            evs.append(sc.add("vector", ev, deps=[im]))
    sc.emit(blk)
    emit_ln(p, blk, env, dep_fence=p.fence(("vector",)))


def emit_l1(p, blk, env, tapd, substage="all"):
    g = NS(env)
    nc = p.nc

    def ld1(gp):
        F = p.fence(("scalar",))
        for w in F:
            gp.wait_ge(p.sem(w[0]), w[1])
        for c in range(FC):
            p.dma(gp, g.s_wv1t[:, c * D:(c + 1) * D],
                  g.d_wv1T[c * 128:(c + 1) * 128, :], "dma_wv1")
        for c in range(FC):
            p.dma(gp, g.s_wgbt[:, c * 32:(c + 1) * 32],
                  g.d_wgbT[c * 128:(c + 1) * 128, :], "dma_wgb")
    blk.gpsimd(ld1)

    def ld2(sync):
        p.dma(sync, g.s_cdyn[:, 0:FC * 128], g.d_cind, "dma_cd")
    blk.sync(ld2)

    sc = Sched(p)
    FLN2 = p.fence(("scalar", "vector"))
    sig = []
    for role in range(2):
        for t4 in range(4):
            j = role * 4 + t4

            def mk(eng, role=role, t4=t4, j=j):
                bank = g.pb[4 + j % 2]
                for c in range(FC):
                    i = eng.matmul(
                        bank[0:16, 0:512],
                        g.s_wgbt[:, c * 32 + role * 16: c * 32 + role * 16 + 16],
                        g.st(g.s_xbf, c, t4 * 512, 512),
                        start=(c == 0), stop=(c == FC - 1))
                return i

            def ev(eng, role=role, t4=t4, j=j):
                dst = g.s_grow if role == 0 else g.s_brow
                return eng.activation(dst[0:16, t4 * 512:(t4 + 1) * 512],
                                      g.pb[4 + j % 2][0:16, 0:512], AF.Sigmoid)
            w = ([("dma_wgb", p.n("dma_wgb"))] + FLN2) if j == 0 else ()
            im = sc.add("tensor", mk, deps=([sig[j - 2]] if j >= 2 else []),
                        sem_waits=w)
            sig.append(sc.add("scalar", ev, deps=[im]))
    btk = []
    for tt in range(16):
        def mk_t(eng, tt=tt):
            return eng.transpose(g.pT[0:128, (tt % 4) * 16:(tt % 4) * 16 + 16],
                                 g.s_brow[0:16, tt * 128:(tt + 1) * 128],
                                 g.s_ident[0:16, 0:16])
        deps = [sig[4 + min(tt // 4, 3)]] + ([btk[tt - 1]] if tt >= 1 else [])
        im = sc.add("tensor", mk_t, deps=deps)
        btk.append(sc.add("vector", lambda e, tt=tt: e.tensor_copy(
            g.s_btk[:, tt * 16:(tt + 1) * 16],
            g.pT[0:128, (tt % 4) * 16:(tt % 4) * 16 + 16]), deps=[im]))
    vevs = []
    for tt in range(NCHUNK):
        for vh in range(2):
            j = tt * 2 + vh

            def mk_v(eng, tt=tt, vh=vh, j=j):
                bank = g.pb[4 + j % 2]
                for c in range(FC):
                    i = eng.matmul(
                        bank[:, 0:512],
                        g.st(g.s_xbf, c, tt * 128, 128),
                        g.s_wv1t[:, c * D + vh * 512: c * D + vh * 512 + 512],
                        start=(c == 0), stop=(c == FC - 1))
                return i

            def ev_v(eng, tt=tt, vh=vh, j=j):
                return eng.tensor_copy(
                    g.s_vtk1[:, tt * D + vh * 512: tt * D + vh * 512 + 512],
                    g.pb[4 + j % 2][:, 0:512])
            w = [("dma_wv1", p.n("dma_wv1"))] if j == 0 else ()
            deps = [btk[-1], btk[-2]] if j < 2 else [vevs[j - 2]]
            im = sc.add("tensor", mk_v, deps=deps, sem_waits=w)
            vevs.append(sc.add("vector", ev_v, deps=[im]))
    nsteps = []
    for t4 in range(4):
        sq_steps = []
        for c in range(FC):
            s1 = sc.add("scalar", lambda e, c=c, t4=t4: e.square(
                g.s_x2sq[:, 0:512], g.st(g.s_xbf, c, t4 * 512, 512)),
                deps=([nsteps[-1]] if (c == 0 and nsteps) else
                      ([vevs[-1]] if (c == 0 and t4 == 0) else [])) + ([sq_steps[-1]] if sq_steps else []))

            def mk_n(eng, c=c, t4=t4):
                return eng.matmul(g.pb[6][0:16, 0:512], g.blk1(c),
                                  g.s_x2sq[:, 0:512],
                                  start=(c == 0), stop=(c == FC - 1))
            t1 = sc.add("tensor", mk_n, deps=[s1])
            sq_steps.append(t1)
        nsteps.append(sc.add("scalar", lambda e, t4=t4: e.sqrt(
            g.s_nrm[0:16, t4 * 512:(t4 + 1) * 512], g.pb[6][0:16, 0:512]),
            deps=[sq_steps[-1]]))
    v1 = sc.add("vector", lambda e: e.tensor_scalar_max(
        g.s_nrm[0:16, :], g.s_nrm[0:16, :], 1e-12), deps=[nsteps[-1]])
    v2 = sc.add("vector", lambda e: e.reciprocal(g.s_nrm[0:16, :], g.s_nrm[0:16, :]),
                deps=[v1])
    v3 = sc.add("vector", lambda e: e.tensor_copy(g.s_nrmb[0:16, :], g.s_nrm[0:16, :]),
                deps=[v2])
    rkevs = []
    bnd = sc.add("vector", lambda e: e.memset(
        g.s_rk[:].rearrange("p (c w) -> p c w", w=TPC + 1)[:, :, 0], 0.0))
    for c in range(FC):
        for t4 in range(4):
            j = c * 4 + t4

            def mk_e(eng, c=c, t4=t4, j=j):
                return eng.matmul(
                    g.pb[4 + j % 2][:, 0:512],
                    g.s_cdyn[0:16, c * 128:(c + 1) * 128],
                    g.s_nrmb[0:16, t4 * 512:(t4 + 1) * 512],
                    start=True, stop=True)

            def ev_e(eng, c=c, t4=t4, j=j):
                return eng.tensor_tensor(
                    g.rksl(c, 1 + t4 * 512, 512),
                    g.st(g.s_xbf, c, t4 * 512, 512),
                    g.pb[4 + j % 2][:, 0:512], ALU.mult)
            w = [("dma_cd", p.n("dma_cd"))] if j == 0 else ()
            deps = [v3] if j < 2 else [rkevs[j - 2]]
            im = sc.add("tensor", mk_e, deps=deps, sem_waits=w)
            rkevs.append(sc.add("vector", ev_e, deps=[im, bnd]))
    def shipb(eng):
        src = g.s_rk[:].rearrange("p (c w) -> p c w", w=TPC + 1)[:, :, TPC]
        with p.nc.allow_non_contiguous_dma(reason="8 strided boundary cols"):
            i = eng.dma_start(out=g.cc_rb_in[:, :], in_=src)
        p.inc(i, "dma_cc", 16)
        return i
    sb = sc.add("gpsimd", shipb, deps=rkevs[-2:])

    def ccb(eng):
        eng.wait_ge(p.sem("dma_cc"), p.n("dma_cc"))
        i = eng.collective_compute(
            "AllGather", ALU.bypass,
            replica_groups=[[0, 1], [2, 3], [4, 5], [6, 7]],
            ins=[g.cc_rb_in.ap().opt()], outs=[g.cc_rb_out.ap().opt()])
        p.inc(i, "ccb_flag", 1)
        return i
    cb = sc.add("gpsimd", ccb, deps=[sb])

    def ldb(eng):
        eng.wait_ge(p.sem("ccb_flag"), 1)
        i = eng.dma_start(out=g.s_rbnd[:], in_=g.cc_rb_out[0:128, :])
        p.inc(i, "rbnd_flag", 16)
        return i
    lb = sc.add("gpsimd", ldb, deps=[cb])
    gb = sc.add("vector", lambda e: e.tensor_scalar_mul(
        g.s_rbnd[:], g.s_rbnd[:], g.s_sgt[:, 0:1]),
        deps=[lb], sem_waits=[("rbnd_flag", 16)])
    BNDRY = sc.add("vector", lambda e: e.tensor_copy(
        g.s_rk[:].rearrange("p (c w) -> p c w", w=TPC + 1)[:, :, 0], g.s_rbnd[:]),
        deps=[gb])
    dumpb = None
    if os.environ.get("DBG_SER"):
        def dumps(gp):
            p.dma(gp, g.d_dbg_rk.ap(), g.s_rk[:], "dma_dbg")
            p.dma(gp, g.d_dbg_btk.ap(), g.s_btk[:], "dma_dbg")
            p.dma(gp, g.d_dbg_vtk.ap(), g.s_vtk1[:], "dma_dbg")
            return p.dma(gp, g.d_dbg_nrm.ap(), g.s_nrm[0:16, :], "dma_dbg")
        dumpb = sc.add("gpsimd", dumps,
                       deps=[BNDRY, rkevs[-1], btk[-1], vevs[-1], v3])
    def ld3(gp):
        for k in range(4):
            p.dma(gp, g.s_woSW[0:64, k * 4096:(k + 1) * 4096],
                  g.d_woSW[0:64, k * 4096:(k + 1) * 4096], "dma_wo")
    w3 = sc.add("gpsimd", ld3,
                deps=[rkevs[-1], v3] + ([dumpb] if dumpb is not None else []),
                sem_waits=([("dma_dbg", 64)] if dumpb is not None else []))
    sc.emit(blk)

    if substage == "prod":
        return
    emit_solves(p, blk, env)
    if substage == "solve":
        return
    emit_pass2(p, blk, env, tapd)
    if substage == "pass2":
        return
    emit_ln(p, blk, env, dep_fence=p.fence(("vector", "tensor")))


def emit_solves(p, blk, env):
    g = NS(env)
    sc = Sched(p)
    PROD = p.fence(("vector", "scalar"))
    z0 = sc.add("vector", lambda e: e.memset(g.s_Spair[:], 0.0), sem_waits=PROD)
    zb = sc.add("vector", lambda e: e.memset(g.s_Sb[:], 0.0), deps=[z0])
    LADDER = [None] * 7

    dma_chunks = []
    p1_steps = {}
    hist = []
    for ci in range(NCHUNK):
        head_final = []
        for h in range(H):
            P = (ci * H + h) % 2
            ro = 64 * (h % 2)
            hc = h // 2
            rkK = g.s_rk[ro:ro + 64,
                         hc * (TPC + 1) + ci * 128: hc * (TPC + 1) + ci * 128 + 128]
            rkR = g.s_rk[ro:ro + 64,
                         hc * (TPC + 1) + 1 + ci * 128: hc * (TPC + 1) + 1 + ci * 128 + 128]
            bcol = g.s_btk[:, ci * 16 + h: ci * 16 + h + 1]
            idn = g.s_ident if ro == 0 else g.s_swap
            FGO = P * 14 * 128
            fgs = lambda k, FGO=FGO: g.s_fg[:, FGO + k * 128: FGO + (k + 1) * 128]
            Xs = lambda k, P=P: g.s_X[:, (P * 2 + k) * 128: (P * 2 + k + 1) * 128]
            ktks = g.s_ktk[:, P * 64:(P + 1) * 64]
            yts = g.s_yt[:, P * 128:(P + 1) * 128]
            yxs = g.s_yxt[:, P * 128:(P + 1) * 128]
            pTs = lambda k, P=P: g.pT[:, (P * 4 + k) * 128: (P * 4 + k) * 128 + 128]
            h1 = hist[-1] if hist else {}
            h2 = hist[-2] if len(hist) >= 2 else {}
            d1 = lambda k: [h1[k]] if k in h1 else []
            d2 = lambda k: [h2[k]] if k in h2 else []

            t_gen = sc.add("tensor", lambda e, rkK=rkK, ro=ro: e.matmul(
                g.pb[0][:, 0:128], rkK, rkK, start=True, stop=True,
                tile_position=(ro, 0)),
                deps=d1("v_f0"),
                sem_waits=([("s_vector", p.n("s_vector"))] if (ci == 0 and h == 0) else ()))
            v_f0 = sc.add("vector", lambda e, bcol=bcol, fgs=fgs: e.scalar_tensor_tensor(
                fgs(0), g.pb[0][:, 0:128], bcol, g.s_masksl[:], ALU.mult, ALU.mult),
                deps=[t_gen] + d2("t_app6"))
            t_g0 = sc.add("tensor", lambda e, fgs=fgs, pTs=pTs: e.transpose(
                pTs(0), fgs(0), g.s_ident[:]), deps=[v_f0] + d1("v_pt"))
            v_g0 = sc.add("vector", lambda e, fgs=fgs, pTs=pTs: e.tensor_copy(
                fgs(1), pTs(0)), deps=[t_g0])
            ev_prev = v_g0
            rung_evs = []
            for r in range(1, 7):
                def mk_r(eng, r=r, fgs=fgs):
                    if r == 6:
                        return eng.matmul(
                            g.pb[1][:, ((r - 1) % 2) * 256: ((r - 1) % 2) * 256 + 128],
                            fgs(2 * r - 2), fgs(2 * r - 1), start=True, stop=True)
                    eng.matmul(g.pb[1][:, ((r - 1) % 2) * 256: ((r - 1) % 2) * 256 + 128],
                               fgs(2 * r - 1), fgs(2 * r - 2), start=True, stop=True)
                    return eng.matmul(
                        g.pb[1][:, ((r - 1) % 2) * 256 + 128: ((r - 1) % 2) * 256 + 256],
                        fgs(2 * r - 2), fgs(2 * r - 1), start=True, stop=True)

                def ev_r(eng, r=r, fgs=fgs, FGO=FGO):
                    if r <= 5:
                        return eng.tensor_copy(
                            g.s_fg[:, FGO + 2 * r * 128: FGO + (2 * r + 2) * 128],
                            g.pb[1][:, ((r - 1) % 2) * 256: ((r - 1) % 2) * 256 + 256])
                    return eng.tensor_copy(
                        fgs(13),
                        g.pb[1][:, ((r - 1) % 2) * 256: ((r - 1) % 2) * 256 + 128])
                tm = sc.add("tensor", mk_r,
                            deps=[ev_prev] + ([rung_evs[r - 3]] if r >= 3 else [])
                                 + (d1("rung6") if r == 1 else []))
                ev_prev = sc.add("vector", ev_r, deps=[tm])
                rung_evs.append(ev_prev)
            t_kt = sc.add("tensor", lambda e, rkK=rkK, pTs=pTs, idn=idn, ro=ro: e.transpose(
                pTs(1)[:, 0:64], rkK, idn[ro:ro + 64, 0:64]), deps=[v_g0])
            v_kt = sc.add("vector", lambda e, ktks=ktks, pTs=pTs: e.tensor_copy(
                ktks, pTs(1)[:, 0:64]), deps=[t_kt] + d2("t_pt"))
            v_x0a = sc.add("vector", lambda e, Xs=Xs, ktks=ktks, bcol=bcol: e.tensor_scalar_mul(
                Xs(0)[:, 0:64], ktks, bcol), deps=[v_kt] + d2("t_kx"))
            v_x0b = sc.add("vector", lambda e, Xs=Xs, ci=ci, h=h, bcol=bcol: e.tensor_scalar_mul(
                Xs(0)[:, 64:128], g.s_vtk1[:, ci * D + h * 64: ci * D + (h + 1) * 64], bcol),
                deps=[v_x0a])
            xcur = 0
            ev_app = v_x0b
            app_evs = []
            app_tms = []
            for j in range(7):
                def mk_a(eng, j=j, xcur=xcur, fgs=fgs, Xs=Xs):
                    return eng.matmul(g.pb[2][:, (j % 2) * 128: (j % 2) * 128 + 128],
                                      fgs(2 * j + 1), Xs(xcur), start=True, stop=True)
                gdep = v_g0 if j == 0 else rung_evs[j - 1]
                tm = sc.add("tensor", mk_a,
                            deps=[ev_app, gdep] + (d1("app_ev6") if j == 0 else []))
                odd = (h % 2 == 1)

                def ev_a(eng, j=j, xcur=xcur, Xs=Xs):
                    src = g.pb[2][:, (j % 2) * 128: (j % 2) * 128 + 128]
                    op = ALU.subtract if j == 0 else ALU.add
                    return eng.tensor_tensor(Xs(1 - xcur), Xs(xcur), src, op)
                ev_app = sc.add("vector", ev_a, deps=[tm])
                app_evs.append(ev_app)
                app_tms.append(tm)
                xcur = 1 - xcur
            XF = xcur
            t_yt = sc.add("tensor", lambda e, rkK=rkK, rkR=rkR, ro=ro: e.matmul(
                g.pb[3][:, 0:128], rkK, rkR, start=True, stop=True,
                tile_position=(ro, 0)), deps=[v_f0] + d1("v_yx"))
            v_yt = sc.add("vector", lambda e, yts=yts: e.tensor_tensor(
                yts, g.pb[3][:, 0:128], g.s_masksui[:], ALU.mult),
                deps=[t_yt] + d2("t_yx"))
            t_yx = sc.add("tensor", lambda e, yts=yts, Xs=Xs, XF=XF: e.matmul(
                g.pb[3][:, 128:256], yts, Xs(XF), start=True, stop=True),
                deps=[v_yt, ev_app])
            v_yx = sc.add("vector", lambda e, yxs=yxs: e.tensor_copy(
                yxs, g.pb[3][:, 128:256]), deps=[t_yx] + d2("t_yxt2"))
            t_yxt = sc.add("tensor", lambda e, yxs=yxs, pTs=pTs, ro=ro: e.transpose(
                pTs(2)[ro:ro + 64, 0:128], yxs[:, 0:64], g.s_ident[:],
                tile_position=(0, ro)), deps=[v_yx])
            t_yxt2 = sc.add("tensor", lambda e, yxs=yxs, pTs=pTs, ro=ro: e.transpose(
                pTs(3)[ro:ro + 64, 0:128], yxs[:, 64:128], g.s_ident[:],
                tile_position=(0, ro)), deps=[v_yx])
            odd = (h % 2 == 1)

            def ev_yxt(eng, pTs=pTs, h=h, ro=ro, hc=hc):
                eng.tensor_scalar_mul(
                    g.s_stgW[ro:ro + 64, hc * 128:(hc + 1) * 128],
                    pTs(2)[ro:ro + 64, 0:128], -1.0)
                return eng.tensor_copy(
                    g.s_stgO[ro:ro + 64, hc * 128:(hc + 1) * 128],
                    pTs(3)[ro:ro + 64, 0:128])
            v_yxt = sc.add("vector", ev_yxt, deps=[t_yxt, t_yxt2],
                           sem_waits=([("dma_ser", 64 * ci)] if (h == 0 and ci > 0) else []))
            t_kx = sc.add("tensor", lambda e, ktks=ktks, Xs=Xs, XF=XF, ro=ro: e.matmul(
                g.pb[4][ro:ro + 64, 0:128], ktks, Xs(XF), start=True, stop=True,
                tile_position=(0, ro)), deps=[ev_app, v_kt] + d1("v_kx"))
            wlo, qlo = 0, 64

            def ev_kx2(eng, ro=ro, wlo=wlo, qlo=qlo, hc=hc, ktks=ktks):
                eng.tensor_copy(ktks[ro:ro + 64, 0:64],
                                g.pb[4][ro:ro + 64, wlo:wlo + 64])
                return eng.tensor_copy(
                    g.s_stgQ[ro:ro + 64, hc * 64:(hc + 1) * 64],
                    g.pb[4][ro:ro + 64, qlo:qlo + 64])
            stg_w = [("dma_ser", 64 * ci)] if (h == 0 and ci > 0) else []
            v_kx = sc.add("vector", ev_kx2, deps=[t_kx], sem_waits=stg_w)
            t_pt = sc.add("tensor", lambda e, ktks=ktks, pTs=pTs, ro=ro: e.transpose(
                pTs(3)[ro:ro + 64, 0:64],
                ktks[ro:ro + 64, 0:64],
                g.s_ident[ro:ro + 64, ro:ro + 64] if ro else g.s_ident[0:64, 0:64],
                tile_position=(ro, ro)), deps=[v_kx])
            v_pt = sc.add("vector", lambda e, pTs=pTs, ro=ro, hc=hc: e.tensor_scalar_mul(
                g.s_stgP[ro:ro + 64, hc * 64:(hc + 1) * 64], pTs(3)[ro:ro + 64, 0:64], -1.0),
                deps=[t_pt])
            head_final.append((v_yxt, v_pt, v_kx))
            hist.append(dict(
                v_f0=v_f0, rung6=rung_evs[-1], app_ev6=app_evs[-1],
                t_app6=app_tms[-1], v_yx=v_yx, t_yx=t_yx, t_yxt2=t_yxt2,
                v_kx=v_kx, t_kx=t_kx, v_pt=v_pt, t_pt=t_pt))
        chain_tms = []
        for h in range(H):
            ro = 64 * (h % 2)
            hc = h // 2

            def mk_s(eng, ro=ro, hc=hc):
                return eng.matmul(g.pb[5][ro:ro + 64, hc * 64:(hc + 1) * 64],
                                  g.s_stgP[ro:ro + 64, hc * 64:(hc + 1) * 64],
                                  g.s_Sb[ro:ro + 64, hc * 64:(hc + 1) * 64],
                                  start=True, stop=True, tile_position=(ro, ro))
            deps = [head_final[h][1]] + ([z0, zb] if ci == 0 else []) \
                + ([prev_chain_ev] if (ci > 0 and h == 0) else [])
            chain_tms.append(sc.add("tensor", mk_s, deps=deps))
        e1 = sc.add("vector", lambda e: e.tensor_tensor(
            g.s_Spair[:], g.s_Spair[:], g.pb[5][:, 0:512], ALU.add),
            deps=[chain_tms[-1]])
        e2 = sc.add("vector", lambda e: e.tensor_tensor(
            g.s_Spair[:], g.s_Spair[:], g.s_stgQ[:], ALU.add), deps=[e1])
        ev = sc.add("vector", lambda e: e.tensor_copy(
            g.s_Sb[:], g.s_Spair[:]), deps=[e2])
        prev_chain_ev = ev
        chain = [(chain_tms[-1], ev)]
        def ship(eng, ci=ci):
            p.dma(eng, g.d_serP[ci], g.s_stgP[:], "dma_ser")
            p.dma(eng, g.d_serQ[ci], g.s_stgQ[:], "dma_ser")
            p.dma(eng, g.d_serW[ci], g.s_stgW[:], "dma_ser")
            i = p.dma(eng, g.d_serO[ci], g.s_stgO[:], "dma_ser")
            return i
        dd = sc.add("gpsimd", ship,
                    deps=[head_final[-1][0], head_final[-1][1], chain[-1][1]])
        dma_chunks.append(dd)
    sc.emit(blk)


def emit_pass2(p, blk, env, tapd):
    import os
    P2CUT = int(os.environ.get('P2CUT', '16'))
    P2SKIP = os.environ.get('P2SKIP', '')
    g = NS(env)
    sc = Sched(p)
    SOLVED = p.fence(("vector", "tensor"))

    def ship_s(eng):
        for w in SOLVED:
            eng.wait_ge(p.sem(w[0]), w[1])
        i = eng.dma_start(out=g.cc_s_in[:, :], in_=g.s_Spair[:])
        p.inc(i, "dma_cc", 16)
        return i
    s1 = sc.add("gpsimd", ship_s, sem_waits=[("dma_ser", 64 * NCHUNK)])

    def cc_s(eng):
        eng.wait_ge(p.sem("dma_cc"), p.n("dma_cc"))
        i = eng.collective_compute(
            "AllGather", ALU.bypass,
            replica_groups=[[0, 1], [2, 3], [4, 5], [6, 7]],
            ins=[g.cc_s_in.ap().opt()], outs=[g.cc_s_out.ap().opt()])
        p.inc(i, "ccs_flag", 1)
        return i
    s2 = sc.add("gpsimd", cc_s, deps=[s1])

    def ld_s(eng):
        eng.wait_ge(p.sem("ccs_flag"), 1)
        i = eng.dma_start(out=g.s_Sin[:], in_=g.cc_s_out[0:128, :])
        p.inc(i, "sin_flag", 16)
        return i
    s3 = sc.add("gpsimd", ld_s, deps=[s2])
    v0 = sc.add("vector", lambda e: e.tensor_scalar_mul(
        g.s_Spair[:], g.s_Sin[:], g.s_sgt[:, 0:1]),
        deps=[s3], sem_waits=[("sin_flag", 16)])
    vb = sc.add("vector", lambda e: e.tensor_copy(g.s_Sb[:], g.s_Spair[:]), deps=[v0])

    TF = p.fence(("tensor",))

    def ld_selh(eng):
        for w in TF:
            eng.wait_ge(p.sem(w[0]), w[1])
        i = eng.dma_start(out=g.s_cdyn[:, 0:H * 64], in_=g.d_cselh)
        p.inc(i, "dma_cd2", 16)
        return i
    sc.add("sync", ld_selh)

    dma_in = []
    cons_done = {0: [vb], 1: [vb]}
    last_och_cons = None
    for ci in range(min(NCHUNK, P2CUT)):
        par = ci % 2

        def ld(eng, ci=ci, par=par):
            p.dma(eng, g.s_p2P[par][:], g.d_serP[ci], f"dma_p2{par}")
            p.dma(eng, g.s_p2Q[par][:], g.d_serQ[ci], f"dma_p2{par}")
            p.dma(eng, g.s_p2W[par][:], g.d_serW[ci], f"dma_p2{par}")
            i = p.dma(eng, g.s_p2O[par][:], g.d_serO[ci], f"dma_p2{par}")
            return i
        dd = sc.add("gpsimd", ld, deps=list(cons_done[par]),
                    sem_waits=([("dma_ser", 64 * NCHUNK)] if ci < 2 else ()))
        dma_in.append(dd)
        IN_RDY = [(f"dma_p2{par}", 64 * (ci // 2 + 1))]

        gexp_evs = []
        for hg in range(4):
            def mk_g(eng, ci=ci, hg=hg):
                for k in range(4):
                    h = hg * 4 + k
                    i = eng.matmul(g.pb[0][0:64, k * 128:(k + 1) * 128],
                                   g.s_cdyn[0:16, h * 64:(h + 1) * 64],
                                   g.s_grow[0:16, ci * 128:(ci + 1) * 128],
                                   start=True, stop=True)
                return i

            def ev_g(eng, hg=hg):
                return eng.tensor_copy(g.s_gexp[0:64, hg * 512:(hg + 1) * 512],
                                       g.pb[0][0:64, 0:512])
            w = [("dma_cd2", 16)] if (ci == 0 and hg == 0) else []
            deps = [gexp_evs[-1]] if gexp_evs else ([last_och_cons] if last_och_cons else [])
            im = sc.add("tensor", mk_g, deps=deps + ([vb] if ci == 0 else []), sem_waits=w)
            gexp_evs.append(sc.add("vector", ev_g, deps=[im]))

        oevs = []
        supd = []
        for h in range(H):
            ro = 64 * (h % 2)
            hc = h // 2
            rkR = g.s_rk[ro:ro + 64,
                         hc * (TPC + 1) + 1 + ci * 128: hc * (TPC + 1) + 1 + ci * 128 + 128]
            Ssl = g.s_Spair[ro:ro + 64, hc * 64:(hc + 1) * 64]
            Sbl = g.s_Sb[ro:ro + 64, hc * 64:(hc + 1) * 64]
            par2 = par
            rOloc = 64 - ro

            def mk_o(eng, h=h, ro=ro, hc=hc, rkR=rkR, Sbl=Sbl, par2=par2):
                slot = g.pb[1 + 2 * (h % 2)][0:64, ((h // 2) % 4) * 128:((h // 2) % 4) * 128 + 128]
                eng.matmul(slot, Sbl, rkR, start=True, stop=False,
                           tile_position=(ro, 0))
                eng.matmul(slot, Sbl,
                           g.s_p2W[par2][ro:ro + 64, hc * 128:(hc + 1) * 128],
                           start=False, stop=False, tile_position=(ro, 0))
                lhs_i = g.s_ident[ro:ro + 64, ro:ro + 64] if ro else g.s_ident[0:64, 0:64]
                return eng.matmul(slot, lhs_i,
                                  g.s_p2O[par2][ro:ro + 64, hc * 128:(hc + 1) * 128],
                                  start=False, stop=True, tile_position=(ro, 0))
            deps = [gexp_evs[h // 4]] + ([oevs[h - 2]] if h >= 2 else ([last_och_cons] if last_och_cons else []))
            deps += [supd[-1][0]] if supd else []
            if "o" in P2SKIP:
                tm = sc.add("tensor", lambda e: None, deps=deps,
                            sem_waits=(IN_RDY if h == 0 else []))
            else:
                tm = sc.add("tensor", mk_o, deps=deps,
                            sem_waits=(IN_RDY if h == 0 else []))

            def ev_o(eng, h=h):
                return eng.tensor_tensor(
                    g.s_och[0:64, h * 128:(h + 1) * 128],
                    g.pb[1 + 2 * (h % 2)][0:64, ((h // 2) % 4) * 128:((h // 2) % 4) * 128 + 128],
                    g.s_gexp[0:64, h * 128:(h + 1) * 128], ALU.mult)
            oevs.append(sc.add("vector", (lambda e: None) if "o" in P2SKIP else ev_o, deps=[tm]))

            def mk_su(eng, ro=ro, hc=hc, Sbl=Sbl, par2=par2):
                return eng.matmul(g.pb[5][ro:ro + 64, hc * 64:(hc + 1) * 64],
                                  g.s_p2P[par2][ro:ro + 64, hc * 64:(hc + 1) * 64],
                                  Sbl, start=True, stop=True, tile_position=(ro, ro))
            tsu = sc.add("tensor", (lambda e: None) if "s" in P2SKIP else mk_su,
                         deps=[tm] + ([supd[-1][1]] if supd else []))

            u1 = sc.add("vector", lambda e, Ssl=Ssl, ro=ro, hc=hc: e.tensor_tensor(
                Ssl, Ssl, g.pb[5][ro:ro + 64, hc * 64:(hc + 1) * 64], ALU.add), deps=[tsu])
            u2 = sc.add("vector", lambda e, Ssl=Ssl, ro=ro, hc=hc, par2=par2: e.tensor_tensor(
                Ssl, Ssl, g.s_p2Q[par2][ro:ro + 64, hc * 64:(hc + 1) * 64], ALU.add), deps=[u1])
            vsu = sc.add("vector", lambda e, Ssl=Ssl, Sbl=Sbl: e.tensor_copy(
                Sbl, Ssl), deps=[u2])
            supd.append((tsu, vsu))
        cons_done[par] = [supd[-1][1], oevs[-1]]

        op_evs = []
        for jc in range(FC):
            def mk_op(eng, ci=ci, jc=jc):
                for h in range(H):
                    i = eng.matmul(
                        g.pb[2][:, (jc % 4) * 128:(jc % 4) * 128 + 128],
                        g.s_woSW[0:64, h * D + jc * 128: h * D + (jc + 1) * 128],
                        g.s_och[0:64, h * 128:(h + 1) * 128],
                        start=(h == 0), stop=(h == H - 1))
                return i
            w = [("dma_wo", p.n("dma_wo"))] if (ci == 0 and jc == 0) else []
            deps = [oevs[-1]] + ([op_evs[jc - 1]] if jc >= 1 else [])
            im = sc.add("tensor", (lambda e: None) if "p" in P2SKIP else mk_op,
                        deps=deps, sem_waits=w)

            def ev_op(eng, ci=ci, jc=jc):
                dst = g.st(g.s_xres, jc, ci * 128, 128)
                return eng.tensor_tensor(
                    dst, dst, g.pb[2][:, (jc % 4) * 128:(jc % 4) * 128 + 128], ALU.add)
            op_evs.append(sc.add("vector", (lambda e: None) if "p" in P2SKIP else ev_op, deps=[im]))
        last_och_cons = op_evs[-1]
    sc.emit(blk)
    if tapd:
        pass


def emit_moe(p, blk, env):
    g = NS(env)
    nc = p.nc
    sc = Sched(p)
    FLN3 = p.fence(("scalar", "vector"))
    FTEN = p.fence(("tensor",))

    def ld_r(eng):
        for w in FLN3 + FTEN:
            eng.wait_ge(p.sem(w[0]), w[1])
        for c in range(FC):
            p.dma(eng, g.s_wgbt[:, c * 32: c * 32 + 8],
                  g.d_rwT[c * 128:(c + 1) * 128, :], "dma_rw")
        p.dma(eng, g.s_moew[0][:].rearrange("p (c d) -> p c d", d=D),
              g.d_w1[0].rearrange("(c p) d -> p c d", p=128), "dma_me0")
        i = p.dma(eng, g.s_moew[2][:].rearrange("p (c d) -> p c d", d=D),
                  g.d_w2[0].rearrange("(c p) d -> p c d", p=128), "dma_me0")
        return i
    sc.add("gpsimd", ld_r)

    def ld_inde(eng):
        i = eng.dma_start(out=g.s_cdyn[0:8, 0:8 * 128], in_=g.d_cinde)
        p.inc(i, "dma_cd3", 16)
        return i
    sc.add("sync", ld_inde,
           sem_waits=FTEN + [("dma_cd2", p.n("dma_cd2"))])

    revs = []
    for t4 in range(4):
        def mk_r(eng, t4=t4):
            bank = g.pb[4 + t4 % 2]
            for c in range(FC):
                i = eng.matmul(bank[0:8, 0:512],
                               g.s_wgbt[:, c * 32: c * 32 + 8],
                               g.st(g.s_xbf, c, t4 * 512, 512),
                               start=(c == 0), stop=(c == FC - 1))
            return i
        w = [("dma_rw", 128)] + FLN3 if t4 == 0 else []
        im = sc.add("tensor", mk_r, deps=([revs[t4 - 2]] if t4 >= 2 else []),
                    sem_waits=w)
        revs.append(sc.add("vector", lambda e, t4=t4: e.tensor_copy(
            g.s_wgtb[0:8, t4 * 512:(t4 + 1) * 512], g.pb[4 + t4 % 2][0:8, 0:512]),
            deps=[im]))
    mz = sc.add("vector", lambda e: e.memset(g.s_mp[:, 8:24], 1e30))
    wtr = [mz]
    for tt in range(16):
        def mk_t(eng, tt=tt):
            return eng.transpose(g.pT[0:128, (tt % 4) * 8:(tt % 4) * 8 + 8],
                                 g.s_wgtb[0:8, tt * 128:(tt + 1) * 128],
                                 g.s_ident[0:8, 0:8])
        deps = [revs[-1]] if tt == 0 else [wtr[-1]]
        tr = sc.add("tensor", mk_t, deps=deps)
        v = sc.add("vector", lambda e, tt=tt: e.tensor_copy(
            g.s_rtk[:, 0:8], g.pT[0:128, (tt % 4) * 8:(tt % 4) * 8 + 8]), deps=[tr])
        v = sc.add("vector", lambda e: e.reduce_max(
            g.s_mp[:, 8:9], g.s_rtk[:, 0:8], AX.X), deps=[v] + ([mz] if tt == 0 else []))
        v = sc.add("vector", lambda e: e.tensor_copy(
            g.s_mp[:, 0:1], g.s_mp[:, 8:9]), deps=[v])
        v = sc.add("vector", lambda e: e.match_replace(
            out=g.s_rtk[:, 8:16], in_to_replace=g.s_mp[:, 8:16],
            in_values=g.s_rtk[:, 0:8], imm_value=-1e30), deps=[v])
        v = sc.add("vector", lambda e: e.reduce_max(
            g.s_mp[:, 16:17], g.s_rtk[:, 8:16], AX.X), deps=[v])
        v = sc.add("vector", lambda e: e.tensor_copy(
            g.s_mp[:, 2:3], g.s_mp[:, 16:17]), deps=[v])
        v = sc.add("vector", lambda e: e.match_replace(
            out=g.s_rtk[:, 16:24], in_to_replace=g.s_mp[:, 16:24],
            in_values=g.s_rtk[:, 8:16], imm_value=-1e30), deps=[v])
        v = sc.add("vector", lambda e: e.tensor_sub(
            g.s_mp[:, 4:5], g.s_mp[:, 0:1], g.s_mp[:, 2:3]), deps=[v])
        v = sc.add("scalar", lambda e: e.activation(
            g.s_mp[:, 5:6], g.s_mp[:, 4:5], AF.Sigmoid), deps=[v])
        v = sc.add("vector", lambda e: e.tensor_scalar(
            g.s_mp[:, 6:7], g.s_mp[:, 5:6], -1.0, 1.0, ALU.mult, ALU.add), deps=[v])
        v = sc.add("vector", lambda e: e.tensor_tensor(
            g.s_rtk[:, 0:8], g.s_rtk[:, 0:8], g.s_rtk[:, 8:16], ALU.not_equal), deps=[v])
        v = sc.add("vector", lambda e: e.tensor_tensor(
            g.s_rtk[:, 8:16], g.s_rtk[:, 8:16], g.s_rtk[:, 16:24], ALU.not_equal), deps=[v])
        v = sc.add("vector", lambda e: e.tensor_scalar_mul(
            g.s_rtk[:, 0:8], g.s_rtk[:, 0:8], g.s_mp[:, 5:6]), deps=[v])
        v = sc.add("vector", lambda e: e.tensor_scalar_mul(
            g.s_rtk[:, 8:16], g.s_rtk[:, 8:16], g.s_mp[:, 6:7]), deps=[v])
        v = sc.add("vector", lambda e: e.tensor_add(
            g.s_rtk[:, 16:24], g.s_rtk[:, 0:8], g.s_rtk[:, 8:16]), deps=[v])
        v = sc.add("vector", lambda e: e.tensor_copy(
            g.s_wgtk[:, 0:8], g.s_rtk[:, 16:24]), deps=[v])
        tb = sc.add("tensor", lambda e, tt=tt: e.transpose(
            g.pT[0:8, 512 + (tt % 4) * 128: 512 + (tt % 4) * 128 + 128],
            g.s_wgtk[:, 0:8], g.s_ident[:]), deps=[v])
        v = sc.add("vector", lambda e, tt=tt: e.tensor_copy(
            g.s_wgtb[0:8, tt * 128:(tt + 1) * 128],
            g.pT[0:8, 512 + (tt % 4) * 128: 512 + (tt % 4) * 128 + 128]), deps=[tb])
        wtr.append(v)

    prev = dict(h=[], wexp=[], evac=[])
    for e_i in range(8):
        me = e_i % 2
        if e_i < 7:
            def pre(eng, e_i=e_i):
                nx = (e_i + 1) % 2
                p.dma(eng, g.s_moew[nx][:].rearrange("p (c d) -> p c d", d=D),
                      g.d_w1[e_i + 1].rearrange("(c p) d -> p c d", p=128),
                      f"dma_me{nx}")
                i = p.dma(eng, g.s_moew[2 + nx][:].rearrange("p (c d) -> p c d", d=D),
                          g.d_w2[e_i + 1].rearrange("(c p) d -> p c d", p=128),
                          f"dma_me{nx}")
                return i
            sc.add("gpsimd", pre,
                   deps=list(prev["evac"]) + list(prev["h"]) +
                        ([revs[-1]] if e_i == 0 else []),
                   sem_waits=(FTEN if e_i == 0 else []))
        W_RDY = [(f"dma_me{me}", 32 * (e_i // 2 + 1))]
        for t4 in range(4):
            def mk_w(eng, e_i=e_i, t4=t4):
                return eng.matmul(g.pb[4][:, 0:512],
                                  g.s_cdyn[0:8, e_i * 128:(e_i + 1) * 128],
                                  g.s_wgtb[0:8, t4 * 512:(t4 + 1) * 512],
                                  start=True, stop=True)
            wdeps = ([wtr[-1]] if (e_i == 0 and t4 == 0) else []) + prev["wexp"]
            wsw = [("dma_cd3", 16)] if (e_i == 0 and t4 == 0) else []
            imw = sc.add("tensor", mk_w, deps=wdeps, sem_waits=wsw)
            evw = sc.add("vector", lambda e: e.tensor_copy(
                g.s_wexp[:, 0:512], g.pb[4][:, 0:512]), deps=[imw])
            hsteps = []
            wave_last = []
            for wv in range(4):
                for k in range(2):
                    dc = wv * 2 + k

                    def mk_h(eng, dc=dc, k=k, t4=t4, me=me):
                        bank = g.pb[5 + k]
                        for c in range(FC):
                            i = eng.matmul(
                                bank[:, 0:512],
                                g.s_moew[me][:, c * D + dc * 128: c * D + (dc + 1) * 128],
                                g.st(g.s_xbf, c, t4 * 512, 512),
                                start=(c == 0), stop=(c == FC - 1))
                        return i
                    hdeps = []
                    if wv == 0 and k == 0:
                        hdeps = [evw] + prev["evac"]
                    elif len(wave_last) >= 2:
                        hdeps = [wave_last[-2]]
                    imh = sc.add("tensor", mk_h, deps=hdeps,
                                 sem_waits=(W_RDY if (t4 == 0 and dc == 0) else []))

                    def rl(eng, dc=dc, k=k):
                        return eng.activation(
                            g.s_htile[:, dc * 512:(dc + 1) * 512],
                            g.pb[5 + k][:, 0:512], AF.Relu)
                    sr = sc.add("scalar", rl, deps=[imh])

                    q1 = sc.add("vector", lambda e, dc=dc: e.tensor_tensor(
                        g.s_htile[:, dc * 512:(dc + 1) * 512],
                        g.s_htile[:, dc * 512:(dc + 1) * 512],
                        g.s_htile[:, dc * 512:(dc + 1) * 512], ALU.mult), deps=[sr])
                    vs = sc.add("vector", lambda e, dc=dc: e.tensor_tensor(
                        g.s_htile[:, dc * 512:(dc + 1) * 512],
                        g.s_htile[:, dc * 512:(dc + 1) * 512],
                        g.s_wexp[:, 0:512], ALU.mult), deps=[q1])
                    hsteps.append(vs)
                    wave_last.append(sr)
            w2evs = []
            for c in range(FC):
                def mk_o(eng, c=c, me=me):
                    for dc in range(FC):
                        i = eng.matmul(
                            g.pb[c % 4][:, 0:512],
                            g.s_moew[2 + me][:, dc * D + c * 128: dc * D + (c + 1) * 128],
                            g.s_htile[:, dc * 512:(dc + 1) * 512],
                            start=(dc == 0), stop=(dc == FC - 1))
                    return i
                deps = [hsteps[-1]] + ([w2evs[c - 4]] if c >= 4 else [])
                imo = sc.add("tensor", mk_o, deps=deps)

                def ev_o(eng, c=c, t4=t4):
                    dst = g.st(g.s_xres, c, t4 * 512, 512)
                    return eng.tensor_tensor(dst, dst, g.pb[c % 4][:, 0:512], ALU.add)
                w2evs.append(sc.add("vector", ev_o, deps=[imo]))
            prev = dict(h=[hsteps[-1]], wexp=[hsteps[-1]], evac=[w2evs[-1]])
    sc.emit(blk)
    emit_ln(p, blk, env, dep_fence=p.fence(("vector", "tensor")))


def _dft_consts():
    idx = np.arange(D, dtype=np.float64)
    ang = 2.0 * np.pi * np.outer(idx, idx) / D
    fre = np.cos(ang)
    fim = -np.sin(ang)
    gre = np.cos(ang) / D
    gim = np.sin(ang) / D
    return fre, fim, gre, gim


def _host_constants():
    fre, fim, gre, gim = _dft_consts()
    cs = {}
    ident = np.eye(128, dtype=np.float32)
    mask_sl = np.tril(np.ones((128, 128), np.float32), -1)
    mask_sui = np.triu(np.ones((128, 128), np.float32), 0)
    swap = np.zeros((128, 128), np.float32)
    swap[0:64, 64:128] = np.eye(64)
    swap[64:128, 0:64] = np.eye(64)
    cs["c_pack"] = np.concatenate([ident, mask_sl, mask_sui, swap], axis=1).astype(BFNP)
    onesb = np.zeros((128, 132), np.float32)
    for c in range(FC):
        onesb[0:64, c * 16 + 2 * c] = 1.0
        onesb[64:128, c * 16 + 2 * c + 1] = 1.0
    onesb[:, 128] = 1.0
    cs["c_onesb"] = onesb.astype(BFNP)
    cs["c_onesrow"] = np.ones((1, 128), np.float32).astype(BFNP)
    cfl = np.zeros((128, 4), np.float32); cfl[:, 0] = CSHIFT; cfl[:, 1] = 1e-24
    cs["c_fl"] = cfl
    cind = np.zeros((16, FC * 128), np.float32)
    for c in range(FC):
        cind[2 * c, c * 128: c * 128 + 64] = 1.0
        cind[2 * c + 1, c * 128 + 64: c * 128 + 128] = 1.0
    cs["c_ind"] = cind.astype(BFNP)
    cinde = np.zeros((8, 8 * 128), np.float32)
    for e in range(8):
        cinde[e, e * 128:(e + 1) * 128] = 1.0
    cs["c_inde"] = cinde.astype(BFNP)
    cselh = np.zeros((16, H * 64), np.float32)
    for h in range(H):
        cselh[h, h * 64:(h + 1) * 64] = 1.0
    cs["c_selh"] = cselh.astype(BFNP)
    cs["c_gre"] = gre.astype(BFNP)
    cs["c_gimn"] = (-gim).astype(BFNP)
    return cs, fre, fim


_BUILD_CACHE = {}


def kernel(L0_attn_x, L0_attn_Wq, L0_attn_Wk, L0_attn_Wv, L0_ffn_w_freq_real,
           L0_ffn_w_freq_imag, L0_ffn_bias, L1_attn_Wv, L1_attn_Wo, L1_attn_Wg,
           L1_attn_Wb, L1_ffn_router_w, L1_ffn_w1_all, L1_ffn_w2_all,
           _stop_after="all", _taps=()):
    key = (_stop_after, tuple(_taps))
    if key not in _BUILD_CACHE:
        _BUILD_CACHE.clear()
        _BUILD_CACHE[key] = build(stop_after=_stop_after, want_taps=_taps)
    p, env = _BUILD_CACHE[key]

    cs, fre, fim = _host_constants()
    ac = np.ascontiguousarray
    shared = dict(cs)
    shared["wqT"] = ac(L0_attn_Wq.T)
    shared["wkT"] = ac(L0_attn_Wk.T)
    shared["wvT"] = ac(L0_attn_Wv.T)
    shared["wre"] = ac(L0_ffn_w_freq_real)
    shared["wim"] = ac(L0_ffn_w_freq_imag)
    shared["bias_fm"] = ac(L0_ffn_bias.reshape(FC, 128).T)
    shared["wv1T"] = ac(L1_attn_Wv.T)
    woSW = ac(L1_attn_Wo.T.reshape(H, 64, D).transpose(1, 0, 2).reshape(64, H * D))
    shared["woSW"] = woSW
    shared["wgbT"] = ac(np.concatenate([L1_attn_Wg, L1_attn_Wb], axis=0).T)
    shared["rwT"] = ac(L1_ffn_router_w.T)
    shared["w1_all"] = ac(L1_ffn_w1_all)
    shared["w2_all"] = ac(L1_ffn_w2_all)

    in_maps = []
    for core in range(8):
        b, half = core // 2, core % 2
        m = dict(shared)
        xo = ac(L0_attn_x[b, half * TPC:(half + 1) * TPC, :].T)
        xr = ac(L0_attn_x[b, (1 - half) * TPC:(2 - half) * TPC, :].T)
        m["x_fm"] = np.concatenate([xo, xr], axis=1)
        m["s_gate"] = np.full((128, 1), float(half), np.float32)
        r0 = core * 128
        fr = fre[r0:r0 + 128, :]
        fi = fim[r0:r0 + 128, :]
        frT = ac(fr.T).reshape(FC, 128, 128).transpose(1, 0, 2).reshape(128, FC * 128)
        fiT = ac(fi.T).reshape(FC, 128, 128).transpose(1, 0, 2).reshape(128, FC * 128)
        m["c_freT"] = frT.astype(BFNP)
        m["c_fimT"] = fiT.astype(BFNP)
        m["c_fimnT"] = (-fiT).astype(BFNP)
        in_maps.append(m)

    res = run_bass_kernel_spmd(p.nc, in_maps, core_ids=list(range(8)))
    out = np.empty((B, T, D), np.float32)
    for core in range(8):
        b, half = core // 2, core % 2
        out[b, half * TPC:(half + 1) * TPC, :] = res.results[core]["out_fm"].T
    if _taps:
        tap_res = [{k: v for k, v in r.items() if k.startswith("tap_")}
                   for r in res.results]
        return out, tap_res
    return out
